# version 07
# baseline (speedup 1.0000x reference)
import os
import numpy as np
import ml_dtypes
import concourse.bass as bass
import concourse.mybir as mybir
from concourse.bass_utils import run_bass_kernel_spmd
from contextlib import ExitStack

F32 = mybir.dt.float32
BF16 = mybir.dt.bfloat16
AF = mybir.ActivationFunctionType
ALU = mybir.AluOpType
AX = mybir.AxisListType
NB = ml_dtypes.bfloat16

R = 8
T = 8192
NT = T // R
C = 128
NCH = NT // C
H, DK, DV, M, D = 16, 128, 128, 64, 2048
ND = D // 128
INTER = 5632
NI = INTER // 128
SCALE = DK ** -0.5
EPS = 1e-5
CLAMP = 32.0

ENGINES = ("pe", "act", "dve", "pool", "sp")
DMA_OWNER = {"sp": "dsp", "pool": "dpool", "act": "dact"}
DMA_LANES = ("a", "b", "c", "d")
ALL_SEMS = list(ENGINES) + [q + l for q in DMA_OWNER.values() for l in DMA_LANES]


class Sched:

    def __init__(self):
        self.ops = {e: [] for e in ENGINES}
        self.count = {s: 0 for s in ALL_SEMS}
        self.writer = {}
        self.readers = {}
        self.clock = {e: {} for e in ENGINES}
        self.snap = {}
        self.nwaits = 0

    def op(self, eng, fn, reads=(), writes=(), dma=False, lane="a"):
        sem = (DMA_OWNER[eng] + lane) if dma else eng
        inc = 16 if dma else 1
        need = {}
        for r_ in reads:
            w = self.writer.get(r_)
            if w:
                need[w[0]] = max(need.get(w[0], 0), w[1])
        for w_ in writes:
            lw = self.writer.get(w_)
            if lw:
                need[lw[0]] = max(need.get(lw[0], 0), lw[1])
            for rs, rv in self.readers.get(w_, ()):
                need[rs] = max(need.get(rs, 0), rv)
        clk = self.clock[eng]
        waits = []
        for s, v in sorted(need.items()):
            if eng == "pe" and s == "pe":
                continue
            if clk.get(s, 0) >= v:
                continue
            waits.append((s, v))
        for s, v in waits:
            if clk.get(s, 0) < v:
                clk[s] = v
            psnap = self.snap.get((s, v))
            if psnap:
                for s2, v2 in psnap.items():
                    if clk.get(s2, 0) < v2:
                        clk[s2] = v2
        self.nwaits += len(waits)
        self.count[sem] += inc
        cnt = self.count[sem]
        snap_clk = dict(clk)
        snap_clk[sem] = max(snap_clk.get(sem, 0), cnt)
        self.snap[(sem, cnt)] = snap_clk
        self.ops[eng].append((waits, fn, sem, inc))
        key = (sem, cnt)
        for w_ in writes:
            self.writer[w_] = key
            self.readers[w_] = []
        for r_ in reads:
            self.readers.setdefault(r_, []).append(key)
        return key

    def fence(self, eng, names, ctx):
        t = ctx["t"]
        if eng == "dve":
            fn = lambda e: e.tensor_copy(out=t["scr"][0:1, 0:1], in_=t["scr"][0:1, 1:2])
        else:
            fn = lambda e: e.activation(out=t["scr"][0:1, 0:1], in_=t["scr"][0:1, 1:2], func=AF.Copy)
        self.op(eng, fn, reads=(), writes=())
        self.op(eng, fn, reads=(), writes=tuple(names))

    def barrier(self):
        vals = {s: v for s, v in self.count.items() if v > 0}
        for e in ENGINES:
            clk = self.clock[e]
            waits = [(s, v) for s, v in sorted(vals.items()) if clk.get(s, 0) < v]
            for s, v in waits:
                clk[s] = v
            if waits:
                self.ops[e].append((waits, None, None, 0))
            self.nwaits += len(waits)

    def emit(self, block, sems):
        def runner(eng_obj, name):
            for waits, fn, sem, inc in self.ops[name]:
                for s, v in waits:
                    eng_obj.wait_ge(sems[s], v)
                if fn is not None:
                    fn(eng_obj).then_inc(sems[sem], inc)

        @block.tensor
        def _(e):
            runner(e, "pe")

        @block.scalar
        def _(e):
            runner(e, "act")

        @block.vector
        def _(e):
            runner(e, "dve")

        @block.gpsimd
        def _(e):
            runner(e, "pool")

        @block.sync
        def _(e):
            runner(e, "sp")


def bcast(ap, shape):
    return ap.broadcast_to(list(shape))


def mmN(sd, psum_ap, lhsT_ap_fn, rhs_ap, n_total, reads, wr, start, stop):
    o = 0
    while o < n_total:
        n = min(512, n_total - o)
        sd.op("pe",
              (lambda e, o=o, n=n: e.matmul(psum_ap[:, o:o + n], lhsT_ap_fn(), rhs_ap[:, o:o + n],
                                            start=start, stop=stop)),
              reads=reads, writes=(wr,))
        o += n



def declare_inputs(nc, launch2):
    ap = {}

    def di(name, shape, dt=F32):
        ap[name] = nc.dram_tensor(name, list(shape), dt, kind="ExternalInput").ap()

    di("xT", (D, NT))
    di("Ws1T", (D, 16))
    di("Ws2T", (16, H * M))
    di("bs2r", (1, H * M))
    di("cosr", (64, NT), BF16)
    di("sinr", (64, NT), BF16)
    di("triu_bf", (128, 128), BF16)
    di("maskS", (128, 128))
    di("mask1", (128, 128))
    di("ident", (128, 128), BF16)
    di("ones_bf", (128, 1), BF16)
    di("onesK1", (1, 128), BF16)
    if not launch2:
        for w in ("WqT", "WkT", "WvT", "WgT"):
            di(w, (D, D))
        di("gnw", (128, 1))
    else:
        di("WoT", (D, D))
        di("gate_wT", (D, INTER))
        di("up_wT", (D, INTER))
        di("down_wT", (INTER, D))
        di("gs_col", (128, NI))
        di("us_col", (128, NI))
        di("ds_col", (128, ND))
        di("q_in", (128, H * NT), BF16)
        di("k_in", (128, H * NT), BF16)
        di("v_in", (128, NCH * D), BF16)
        di("sg_in", (128, NCH * D), BF16)
        di("s1_in", (16, NT), BF16)
        di("st_all", (R * 128, 2048))
        di("c_all", (1, R * H * M))
        di("pmask_bc", (128, R))
        di("pmask_row", (1, R))
    return ap


def setup_consts(ctx):
    nc, sd, es, I = ctx["nc"], ctx["sd"], ctx["es"], ctx["in"]
    S = lambda n, sh, dt: es.enter_context(nc.sbuf_tensor(n, list(sh), dt))
    t = {}
    t["cos"] = S("c_cos", (64, NT), BF16)
    t["sin"] = S("c_sin", (64, NT), BF16)
    t["triu"] = S("c_triu", (128, 128), BF16)
    t["maskS"] = S("c_maskS", (128, 128), F32)
    t["mask1"] = S("c_mask1", (128, 128), F32)
    t["ident"] = S("c_ident", (128, 128), BF16)
    t["ones"] = S("c_ones", (128, 1), BF16)
    t["onesK1"] = S("c_onesK1", (1, 128), BF16)
    t["eps"] = S("c_eps", (128, 1), F32)
    t["bs2bc"] = S("c_bs2bc", (128, H * M), F32)
    t["ws2"] = S("c_ws2", (16, H * M), BF16)
    t["scr"] = S("c_scr", (1, 4), F32)
    t["stg32"] = S("c_stg32", (128, H * M), F32)
    ctx["t"] = t
    prev = ()
    for n, src in (("cos", "cosr"), ("sin", "sinr"), ("triu", "triu_bf"), ("maskS", "maskS"),
                   ("mask1", "mask1"), ("ident", "ident"), ("ones", "ones_bf"), ("onesK1", "onesK1")):
        sd.op("sp", (lambda e, n=n, src=src: e.dma_start(out=t[n][:], in_=I[src])),
              reads=prev, writes=("c_" + n,), dma=True, lane="d")
        prev = ("c_" + n,)
    sd.op("dve", lambda e: e.memset(t["eps"][:], EPS), writes=("c_eps",))
    sd.op("dve", lambda e: e.memset(t["scr"][:], 0.0), writes=("c_scr",))
    sd.op("sp", lambda e: e.dma_start(out=t["bs2bc"][:], in_=I["bs2r"].broadcast_to([128, H * M])),
          reads=prev, writes=("c_bs2bc",), dma=True, lane="d")
    sd.op("sp", lambda e: e.dma_start(out=t["stg32"][0:16, :], in_=I["Ws2T"]),
          reads=("c_bs2bc",), writes=("c_stg32",), dma=True, lane="d")
    sd.op("dve", lambda e: e.tensor_copy(out=t["ws2"][:], in_=t["stg32"][0:16, :]),
          reads=("c_stg32",), writes=("c_ws2",))


def emit_ln(ctx, x_dram_ap, hs, pref):
    nc, sd, t = ctx["nc"], ctx["sd"], ctx["t"]
    rowd = nc.dram_tensor(pref + "_rowd", [2, NT], F32, kind="Internal").ap()
    with ExitStack() as pse:
        es = pse
        xa = [es.enter_context(nc.sbuf_tensor(f"{pref}_x{i}", [128, NT], F32)) for i in range(2)]
        xb = [es.enter_context(nc.sbuf_tensor(f"{pref}_xb{i}", [128, NT], BF16)) for i in range(2)]
        sq = xb
        stm = es.enter_context(nc.sbuf_tensor(pref + "_stm", [1, NT], F32))
        stv = es.enter_context(nc.sbuf_tensor(pref + "_stv", [1, NT], F32))
        mb = es.enter_context(nc.sbuf_tensor(pref + "_mb", [128, NT], F32))
        m2 = mb
        rb = es.enter_context(nc.sbuf_tensor(pref + "_rb", [128, NT], F32))
        ps = [pse.enter_context(nc.psum_tensor(f"{pref}_p{i}", [1, 512], F32)) for i in range(4)]
        for fi in range(ND):
            b = fi % 2
            xr, xbr, sqr = f"{pref}x{b}", f"{pref}xb{b}", f"{pref}sq{b}"
            sd.op("sp", lambda e, fi=fi, b=b: e.dma_start(out=xa[b][:], in_=x_dram_ap[fi * 128:(fi + 1) * 128, :]),
                  reads=(xr,), writes=(xr,), dma=True)
            sd.op("dve", lambda e, b=b: e.tensor_copy(out=xb[b][:], in_=xa[b][:]),
                  reads=(xr,), writes=(xbr, sqr))
            for hf in range(2):
                sl = slice(hf * 512, (hf + 1) * 512)
                sd.op("pe", lambda e, fi=fi, b=b, hf=hf, sl=sl: e.matmul(
                    ps[hf][:, :], t["ones"][:, 0:1], xb[b][:, sl], start=(fi == 0), stop=(fi == ND - 1)),
                    reads=(xbr, "c_ones"), writes=(f"{pref}pm{hf}",))
            sd.op("act", lambda e, b=b: e.activation(out=sq[b][:], in_=xa[b][:], func=AF.Square),
                  reads=(xr,), writes=(sqr, xbr))
            for hf in range(2):
                sl = slice(hf * 512, (hf + 1) * 512)
                sd.op("pe", lambda e, fi=fi, b=b, hf=hf, sl=sl: e.matmul(
                    ps[2 + hf][:, :], t["ones"][:, 0:1], sq[b][:, sl], start=(fi == 0), stop=(fi == ND - 1)),
                    reads=(sqr, "c_ones"), writes=(f"{pref}pq{hf}",))
        for hf in range(2):
            sl = slice(hf * 512, (hf + 1) * 512)
            sd.op("dve", lambda e, hf=hf, sl=sl: e.tensor_scalar(
                out=stm[0:1, sl], in0=ps[hf][:, :], scalar1=1.0 / D, scalar2=None, op0=ALU.mult),
                reads=(f"{pref}pm{hf}",), writes=(f"{pref}mrow",))
            sd.op("dve", lambda e, hf=hf, sl=sl: e.tensor_scalar(
                out=stv[0:1, sl], in0=ps[2 + hf][:, :], scalar1=1.0 / D, scalar2=None, op0=ALU.mult),
                reads=(f"{pref}pq{hf}",), writes=(f"{pref}vrow",))
        sd.op("dve", lambda e: e.tensor_mul(out=m2[0:1, :], in0=stm[0:1, :], in1=stm[0:1, :]),
              reads=(f"{pref}mrow", f"{pref}mb"), writes=(f"{pref}m2",))
        sd.op("dve", lambda e: e.tensor_sub(out=stv[0:1, :], in0=stv[0:1, :], in1=m2[0:1, :]),
              reads=(f"{pref}vrow", f"{pref}m2"), writes=(f"{pref}vrow",))
        sd.fence("dve", (f"{pref}vrow", f"{pref}mrow"), ctx)
        sd.op("act", lambda e: e.activation(out=m2[0:1, :], in_=stv[0:1, :], func=AF.Ln, bias=t["eps"][0:1, 0:1]),
              reads=(f"{pref}vrow", "c_eps", f"{pref}m2"), writes=(f"{pref}m2",))
        sd.op("act", lambda e: e.activation(out=stv[0:1, :], in_=m2[0:1, :], func=AF.Exp, scale=-0.5),
              reads=(f"{pref}m2",), writes=(f"{pref}vrow",))
        sd.fence("act", (f"{pref}vrow",), ctx)
        sd.op("pool", lambda e: e.dma_start(out=rowd[0:1, :], in_=stm[:, :]),
              reads=(f"{pref}mrow",), writes=(f"{pref}rowd",), dma=True)
        sd.op("pool", lambda e: e.dma_start(out=rowd[1:2, :], in_=stv[:, :]),
              reads=(f"{pref}vrow",), writes=(f"{pref}rowd",), dma=True)
        sd.op("pool", lambda e: e.dma_start(out=mb[:], in_=rowd[0:1, :].broadcast_to([128, NT])),
              reads=(f"{pref}rowd", f"{pref}m2"), writes=(f"{pref}mb",), dma=True)
        sd.op("pool", lambda e: e.dma_start(out=rb[:], in_=rowd[1:2, :].broadcast_to([128, NT])),
              reads=(f"{pref}rowd",), writes=(f"{pref}rb",), dma=True, lane="b")
        if os.environ.get("L1_DBG_HS") == "1":
            st_dbg = nc.dram_tensor(pref + "_stdbg", [4, NT], F32, kind="ExternalOutput").ap()
            sd.op("pool", lambda e: e.dma_start(out=st_dbg[0:1, :], in_=stm[:, :]),
                  reads=(f"{pref}mrow",), writes=("stdbg",), dma=True)
            sd.op("pool", lambda e: e.dma_start(out=st_dbg[1:2, :], in_=stv[:, :]),
                  reads=(f"{pref}vrow",), writes=("stdbg",), dma=True)
            sd.op("pool", lambda e: e.dma_start(out=st_dbg[2:3, :], in_=mb[0:1, :]),
                  reads=(f"{pref}mb",), writes=("stdbg",), dma=True)
            sd.op("pool", lambda e: e.dma_start(out=st_dbg[3:4, :], in_=rb[0:1, :]),
                  reads=(f"{pref}rb",), writes=("stdbg",), dma=True)
        for fi in range(ND):
            b = fi % 2
            xr = f"{pref}x{b}"
            sd.op("sp", lambda e, fi=fi, b=b: e.dma_start(out=xa[b][:], in_=x_dram_ap[fi * 128:(fi + 1) * 128, :]),
                  reads=(xr,), writes=(xr,), dma=True, lane="ab"[b])
            sd.op("dve", lambda e, b=b: e.tensor_sub(out=xa[b][:], in0=xa[b][:], in1=mb[:]),
                  reads=(xr, f"{pref}mb"), writes=(xr,))
            sd.op("dve", lambda e, fi=fi, b=b: e.tensor_mul(out=hs[:, fi * NT:(fi + 1) * NT], in0=xa[b][:], in1=rb[:]),
                  reads=(xr, f"{pref}rb"), writes=(f"hs{fi}",))
            sd.fence("dve", (f"hs{fi}",), ctx)
        sd.barrier()




def pipe_stream(ctx, pref, w_dram, slices, consume, post=None):
    nc, sd = ctx["nc"], ctx["sd"]
    W = slices[0][2]
    with ExitStack() as pse:
        wsa = [pse.enter_context(nc.sbuf_tensor(f"{pref}_wa{i}", [128, W], F32)) for i in range(2)]
        wsb = [pse.enter_context(nc.sbuf_tensor(f"{pref}_wb{i}", [128, W], BF16)) for i in range(2)]
        n = len(slices)

        def _dma(i):
            b = i % 2
            ra = f"{pref}wa{b}"
            r0, c0, _ = slices[i]
            sd.op("sp", lambda e, r0=r0, c0=c0, b=b: e.dma_start(
                out=wsa[b][:], in_=w_dram[r0:r0 + 128, c0:c0 + W]),
                reads=(ra,), writes=(ra,), dma=True, lane="ab"[b])

        def _consume(j, gate):
            b = j % 2
            ra, rb = f"{pref}wa{b}", f"{pref}wb{b}"
            sd.op("dve", lambda e, b=b: e.tensor_copy(out=wsb[b][:], in_=wsa[b][:]),
                  reads=(ra,), writes=(rb,))
            sd.fence("dve", (rb,), ctx)
            consume(j, wsb[b], (rb, gate))
            if post:
                post(j)

        for i in range(n):
            _dma(i)
            if i > 0:
                _consume(i - 1, f"{pref}wa{i % 2}")
        bg = n % 2
        sd.op("sp", lambda e, bg=bg: e.dma_start(out=wsa[bg][0:1, 0:8], in_=w_dram[0:1, 0:8]),
              reads=(f"{pref}wa{bg}",), writes=(f"{pref}wa{bg}",), dma=True, lane="ab"[bg])
        _consume(n - 1, f"{pref}wa{bg}")
        sd.barrier()

def emit_weight_stream_mm(ctx, wname, w_dram, hs, dst_fn, n_fo, pref, grp=4, nbuf=None):
    nc, sd = ctx["nc"], ctx["sd"]
    W = grp * 128
    with ExitStack() as pse:
        pp = [pse.enter_context(nc.psum_tensor(f"{pref}_pp{i}", [128, NT], F32)) for i in range(grp)]
        slices = []
        for fog in range(n_fo // grp):
            for fi in range(ND):
                slices.append((fi * 128, fog * W, W))

        def consume(j, wb, reads):
            fi = j % ND
            for c4 in range(grp):
                for hf in range(2):
                    sd.op("pe", lambda e, fi=fi, wb=wb, c4=c4, hf=hf: e.matmul(
                        pp[c4][:, hf * 512:(hf + 1) * 512],
                        wb[:, c4 * 128:(c4 + 1) * 128],
                        hs[:, fi * NT + hf * 512: fi * NT + (hf + 1) * 512],
                        start=(fi == 0), stop=(fi == ND - 1)),
                        reads=reads + (f"hs{fi}",), writes=(f"{pref}pp{c4}",))

        def post(j):
            if j % ND == ND - 1:
                fog = j // ND
                for c4 in range(grp):
                    dst_fn(fog * grp + c4, pp[c4], f"{pref}pp{c4}")

        pipe_stream(ctx, pref, w_dram, slices, consume, post)


def emit_rope(ctx, q_sb, name, sc):
    nc, sd, t = ctx["nc"], ctx["sd"], ctx["t"]
    NGH = 2
    cosv = t["cos"][:, :].unsqueeze(1).broadcast_to([64, NGH, NT])
    sinv = t["sin"][:, :].unsqueeze(1).broadcast_to([64, NGH, NT])
    for g in range(H // NGH):
        heads = tuple(f"{name}{h}" for h in range(NGH * g, NGH * g + NGH))
        x1 = q_sb[0:64, g * NGH * NT:(g + 1) * NGH * NT].rearrange("p (h t) -> p h t", h=NGH)
        x2 = q_sb[64:128, g * NGH * NT:(g + 1) * NGH * NT].rearrange("p (h t) -> p h t", h=NGH)
        v0 = sc[0][:, :].rearrange("p (h t) -> p h t", h=NGH)
        v1 = sc[1][:, :].rearrange("p (h t) -> p h t", h=NGH)
        v2 = sc[2][:, :].rearrange("p (h t) -> p h t", h=NGH)
        v3 = sc[3][:, :].rearrange("p (h t) -> p h t", h=NGH)
        scr = ("rrs0", "rrs1", "rrs2", "rrs3")
        sd.op("dve", lambda e, x2=x2, v3=v3: e.tensor_copy(out=v3, in_=x2),
              reads=heads, writes=(scr[3],))
        sd.op("dve", lambda e, x1=x1, v0=v0: e.tensor_mul(out=v0, in0=x1, in1=sinv),
              reads=heads + ("c_cos", "c_sin"), writes=(scr[0],))
        sd.op("dve", lambda e, x1=x1, v1=v1: e.tensor_mul(out=v1, in0=x1, in1=cosv),
              reads=heads, writes=(scr[1],))
        sd.op("dve", lambda e, v2=v2, v3=v3: e.tensor_mul(out=v2, in0=v3, in1=sinv),
              reads=(scr[3],), writes=(scr[2],))
        sd.op("dve", lambda e, x1=x1, v1=v1, v2=v2: e.tensor_sub(out=x1, in0=v1, in1=v2),
              reads=(scr[1], scr[2]), writes=heads)
        sd.op("dve", lambda e, v1=v1, v3=v3: e.tensor_mul(out=v1, in0=v3, in1=cosv),
              reads=(scr[3],), writes=(scr[1],))
        sd.op("dve", lambda e, v1=v1, v0=v0, v2=v2: e.tensor_add(out=v2, in0=v1, in1=v0),
              reads=(scr[0], scr[1]), writes=(scr[2],))
        sd.op("dve", lambda e, x2=x2, v2=v2: e.tensor_copy(out=x2, in_=v2),
              reads=(scr[2],), writes=heads)
        sd.fence("dve", heads, ctx)


def emit_es(ctx, ci, s1_sb, es_tm, esname, scratch, psname, ps_half, scname=None):
    nc, sd, t = ctx["nc"], ctx["sd"], ctx["t"]
    scname = scname or (esname + "sc")
    for hf in range(2):
        sl = slice(hf * 512, (hf + 1) * 512)
        sd.op("pe", lambda e, ci=ci, sl=sl: e.matmul(
            ps_half[:, :], s1_sb[:, ci * C:(ci + 1) * C], t["ws2"][:, sl], start=True, stop=True),
            reads=("s1", "c_ws2"), writes=(psname,))
        sd.op("dve", lambda e, sl=sl: e.tensor_add(out=scratch[:, 0:512], in0=ps_half[:, :], in1=t["bs2bc"][:, sl]),
              reads=(psname, "c_bs2bc"), writes=(scname,))
        sd.op("dve", lambda e: e.tensor_scalar(out=scratch[:, 0:512], in0=scratch[:, 0:512],
                                               scalar1=CLAMP, scalar2=-CLAMP, op0=ALU.min, op1=ALU.max),
              reads=(scname,), writes=(scname,))
        sd.op("act", lambda e, sl=sl: e.activation(out=es_tm[:, sl], in_=scratch[:, 0:512], func=AF.Exp),
              reads=(scname,), writes=(esname,))
        sd.fence("act", (esname,), ctx)


def emit_ktm(ctx, ci, k_sb, k_tm, pbk, pbkname):
    nc, sd, t = ctx["nc"], ctx["sd"], ctx["t"]
    for rnd in range(2):
        for hh in range(8):
            h = rnd * 8 + hh
            sd.op("pe", lambda e, h=h, hh=hh, ci=ci: e.transpose(
                pbk[:, hh * 128:(hh + 1) * 128], k_sb[:, h * NT + ci * C: h * NT + (ci + 1) * C], t["ident"][:, :]),
                reads=(f"k{h}", "c_ident"), writes=(pbkname,))
        sd.op("dve", lambda e, rnd=rnd: e.tensor_copy(
            out=k_tm[:, rnd * 1024:(rnd + 1) * 1024], in_=pbk[:, :]),
            reads=(pbkname,), writes=("ktm",))
        sd.fence("dve", ("ktm",), ctx)



def build_launch1():
    nc = bass.Bass("TRN2", target_bir_lowering=False, debug=False)
    I = declare_inputs(nc, launch2=False)
    O = {}
    for n, sh, dt in (("q_sp", (128, H * NT), BF16), ("k_sp", (128, H * NT), BF16),
                      ("v_sp", (128, NCH * D), BF16), ("sg_sp", (128, NCH * D), BF16),
                      ("s1_sp", (16, NT), BF16), ("st_out", (128, 2048), F32),
                      ("c_out", (1, H * M), F32)):
        O[n] = nc.dram_tensor(n, list(sh), dt, kind="ExternalOutput").ap()

    es = ExitStack()
    with es:
        sems = {s: es.enter_context(nc.semaphore(s)) for s in ALL_SEMS}
        sd = Sched()
        ctx = {"nc": nc, "sd": sd, "es": es, "in": I}
        setup_consts(ctx)
        t = ctx["t"]
        Sb = lambda n, sh, dt: es.enter_context(nc.sbuf_tensor(n, list(sh), dt))
        hs = Sb("hs", (128, ND * NT), BF16)
        q_sb = Sb("q_sb", (128, H * NT), BF16)
        k_sb = Sb("k_sb", (128, H * NT), BF16)
        v_tm = Sb("v_tm", (128, NCH * D), BF16)
        sg_tm = Sb("sg_tm", (128, NCH * D), BF16)
        s1_sb = Sb("s1_sb", (16, NT), BF16)
        gnw_sb = Sb("gnw_sb", (128, 1), F32)
        ws1b = Sb("ws1b", (128, ND * 16), BF16)

        sd.op("sp", lambda e: e.dma_start(out=gnw_sb[:], in_=I["gnw"]),
              reads=("c_stg32",), writes=("gnw",), dma=True, lane="d")
        sd.op("sp", lambda e: e.dma_start(
            out=t["stg32"][:, 0:ND * 16].rearrange("p (c s) -> p c s", s=16),
            in_=I["Ws1T"].rearrange("(c p) s -> p c s", p=128)),
            reads=("c_stg32", "gnw"), writes=("c_stg32",), dma=True, lane="d")
        sd.op("dve", lambda e: e.tensor_copy(out=ws1b[:], in_=t["stg32"][:, 0:ND * 16]),
              reads=("c_stg32",), writes=("ws1b",))

        emit_ln(ctx, I["xT"], hs, "ln1")
        if os.environ.get("L1_DBG_HS") == "1":
            hs_dbg = nc.dram_tensor("hs_dbg", [128, ND * NT], BF16, kind="ExternalOutput").ap()
            sd.op("pool", lambda e: e.dma_start(out=hs_dbg, in_=hs[:, :]),
                  reads=tuple(f"hs{fi}" for fi in range(ND)), writes=("hsdbg",), dma=True)

        def q_evac(fo, ppt, ppr):
            sd.op("dve", lambda e, fo=fo: e.tensor_copy(out=q_sb[:, fo * NT:(fo + 1) * NT], in_=ppt[:, :]),
                  reads=(ppr,), writes=(f"q{fo}",))
            sd.fence("dve", (f"q{fo}",), ctx)

        def k_evac(fo, ppt, ppr):
            sd.op("act", lambda e, fo=fo: e.activation(out=k_sb[:, fo * NT:(fo + 1) * NT], in_=ppt[:, :], func=AF.Copy),
                  reads=(ppr,), writes=(f"k{fo}",))
            sd.fence("act", (f"k{fo}",), ctx)

        def v_evac(fo, ppt, ppr):
            sd.op("dve", lambda e, ppt=ppt: e.tensor_copy(out=fmbf[:], in_=ppt[:, :]), reads=(ppr,), writes=("fmbf",))
            sd.fence("dve", ("fmbf",), ctx)
            for ct in range(NCH):
                sd.op("pe", lambda e, ct=ct: e.transpose(
                    vgps[:, ct * 128:(ct + 1) * 128], fmbf[:, ct * C:(ct + 1) * C], t["ident"][:, :]),
                    reads=("fmbf", "c_ident"), writes=("vgps",))
            dst = v_tm[:, :].rearrange("p (ct r) -> p ct r", r=D)[:, :, fo * 128:(fo + 1) * 128]
            sd.op("dve", lambda e, dst=dst: e.tensor_copy(out=dst, in_=vgps[:, :].rearrange("p (ct c) -> p ct c", c=128)),
                  reads=("vgps",), writes=(f"v{fo}",))
            sd.fence("dve", (f"v{fo}",), ctx)

        def g_evac(fo, ppt, ppr):
            sd.op("act", lambda e, ppt=ppt: e.activation(out=fmsc[:], in_=ppt[:, :], func=AF.Silu),
                  reads=(ppr,), writes=("fmsc",))
            sd.op("dve", lambda e: e.tensor_mul(out=fmbf[:], in0=fmsc[:], in1=bcast(gnw_sb[:, 0:1], (128, NT))),
                  reads=("fmsc", "gnw"), writes=("fmbf",))
            sd.fence("dve", ("fmbf",), ctx)
            for ct in range(NCH):
                sd.op("pe", lambda e, ct=ct: e.transpose(
                    vgps[:, ct * 128:(ct + 1) * 128], fmbf[:, ct * C:(ct + 1) * C], t["ident"][:, :]),
                    reads=("fmbf", "c_ident"), writes=("vgps",))
            dst = sg_tm[:, :].rearrange("p (ct r) -> p ct r", r=D)[:, :, fo * 128:(fo + 1) * 128]
            sd.op("dve", lambda e, dst=dst: e.tensor_copy(out=dst, in_=vgps[:, :].rearrange("p (ct c) -> p ct c", c=128)),
                  reads=("vgps",), writes=(f"sg{fo}",))
            sd.fence("dve", (f"sg{fo}",), ctx)

        emit_weight_stream_mm(ctx, "q", I["WqT"], hs, q_evac, H, "pq", grp=4)
        emit_weight_stream_mm(ctx, "k", I["WkT"], hs, k_evac, H, "pk", grp=4)
        with ExitStack() as vgs:
            vgps = vgs.enter_context(nc.psum_tensor("vgps", [128, 1024], BF16))
            fmsc = vgs.enter_context(nc.sbuf_tensor("fmsc", [128, NT], F32))
            fmbf = vgs.enter_context(nc.sbuf_tensor("fmbf", [128, NT], BF16))
            emit_weight_stream_mm(ctx, "v", I["WvT"], hs, v_evac, H, "pv", grp=2, nbuf=1)
            emit_weight_stream_mm(ctx, "g", I["WgT"], hs, g_evac, H, "pg", grp=2, nbuf=1)

        with ExitStack() as pse:
            p16 = [pse.enter_context(nc.psum_tensor(f"s1p{i}", [16, 512], F32)) for i in range(2)]
            for fi in range(ND):
                for hf in range(2):
                    sd.op("pe", lambda e, fi=fi, hf=hf: e.matmul(
                        p16[hf][:, :], ws1b[:, fi * 16:(fi + 1) * 16],
                        hs[:, fi * NT + hf * 512: fi * NT + (hf + 1) * 512],
                        start=(fi == 0), stop=(fi == ND - 1)),
                        reads=("ws1b", f"hs{fi}"), writes=(f"s1p{hf}",))
            for hf in range(2):
                sd.op("dve", lambda e, hf=hf: e.tensor_copy(out=s1_sb[:, hf * 512:(hf + 1) * 512], in_=p16[hf][:, :]),
                      reads=(f"s1p{hf}",), writes=("s1",))
            sd.fence("dve", ("s1",), ctx)
            sd.barrier()

        with ExitStack() as rsc:
            rope_sc = [rsc.enter_context(nc.sbuf_tensor(f"rope_s{i}", [64, 2 * NT], BF16)) for i in range(4)]
            emit_rope(ctx, q_sb, "q", rope_sc)
            emit_rope(ctx, k_sb, "k", rope_sc)
            sd.barrier()

        es_tm = Sb("es_tm", (128, H * M), BF16)
        es_sc = Sb("es_sc", (128, 512), F32)
        k_tm = Sb("k_tm", (128, H * 128), BF16)
        stage = Sb("stage", (128, 2048), F32)
        cstage = Sb("cstage", (1, H * M), F32)
        sd.op("dve", lambda e: e.memset(stage[:], 0.0), writes=("stage",))
        sd.op("dve", lambda e: e.memset(cstage[:], 0.0), writes=("cst",))
        with ExitStack() as pse:
            pS = pse.enter_context(nc.psum_tensor("pS", [128, 1024], F32))
            pZ = pse.enter_context(nc.psum_tensor("pZ", [128, 1024], F32))
            pC = pse.enter_context(nc.psum_tensor("pC", [1, 512], F32))
            pEs = pse.enter_context(nc.psum_tensor("pEs", [128, 512], F32))
            pBk = pse.enter_context(nc.psum_tensor("pBk", [128, 1024], BF16))
            for ci in range(NCH):
                emit_es(ctx, ci, s1_sb, es_tm, "es", es_sc, "pEs", pEs)
                emit_ktm(ctx, ci, k_sb, k_tm, pBk, "pBk")
                for h in range(H):
                    sd.op("pe", lambda e, h=h: e.matmul(
                        pS[:, h * M:(h + 1) * M], k_tm[:, h * 128:(h + 1) * 128],
                        es_tm[:, h * M:(h + 1) * M], start=True, stop=True),
                        reads=("ktm", "es"), writes=("pS",))
                sd.op("dve", lambda e: e.tensor_add(out=stage[:, 0:1024], in0=stage[:, 0:1024], in1=pS[:, :]),
                      reads=("pS", "stage"), writes=("stage",))
                for h in range(H):
                    sd.op("pe", lambda e, h=h, ci=ci: e.matmul(
                        pZ[64 * (h % 2):64 * (h % 2) + 64, (h // 2) * 128:(h // 2) * 128 + 128],
                        es_tm[:, h * M:(h + 1) * M],
                        v_tm[:, ci * D + h * 128: ci * D + (h + 1) * 128], start=True, stop=True),
                        reads=("es", f"v{h}"), writes=("pZ",))
                sd.op("dve", lambda e: e.tensor_add(out=stage[:, 1024:2048], in0=stage[:, 1024:2048], in1=pZ[:, :]),
                      reads=("pZ", "stage"), writes=("stage",))
                sd.fence("dve", ("stage",), ctx)
                for hf in range(2):
                    sd.op("pe", lambda e, hf=hf: e.matmul(
                        pC[:, :], t["ones"][:, 0:1], es_tm[:, hf * 512:(hf + 1) * 512], start=True, stop=True),
                        reads=("es", "c_ones"), writes=("pC",))
                    sd.op("dve", lambda e, hf=hf: e.tensor_add(
                        out=cstage[0:1, hf * 512:(hf + 1) * 512], in0=cstage[0:1, hf * 512:(hf + 1) * 512], in1=pC[:, :]),
                        reads=("pC", "cst"), writes=("cst",))
                    sd.fence("dve", ("cst",), ctx)
            sd.barrier()

        sd.op("pool", lambda e: e.dma_start(out=O["st_out"], in_=stage[:, :]), reads=("stage",), writes=("o1",), dma=True, lane="c")
        sd.op("pool", lambda e: e.dma_start(out=O["c_out"], in_=cstage[:, :]), reads=("cst", "o1"), writes=("o2",), dma=True, lane="c")
        sd.op("pool", lambda e: e.dma_start(out=O["q_sp"], in_=q_sb[:, :]),
              reads=tuple(f"q{h}" for h in range(H)) + ("o2",), writes=("o3",), dma=True, lane="c")
        sd.op("pool", lambda e: e.dma_start(out=O["k_sp"], in_=k_sb[:, :]),
              reads=tuple(f"k{h}" for h in range(H)) + ("o3",), writes=("o4",), dma=True, lane="c")
        sd.op("pool", lambda e: e.dma_start(out=O["v_sp"], in_=v_tm[:, :]),
              reads=tuple(f"v{h}" for h in range(H)) + ("o4",), writes=("o5",), dma=True, lane="c")
        sd.op("pool", lambda e: e.dma_start(out=O["sg_sp"], in_=sg_tm[:, :]),
              reads=tuple(f"sg{h}" for h in range(H)) + ("o5",), writes=("o6",), dma=True, lane="c")
        sd.op("pool", lambda e: e.dma_start(out=O["s1_sp"], in_=s1_sb[:, :]), reads=("s1", "o6"), writes=("o7",), dma=True, lane="c")
        sd.barrier()
        block = es.enter_context(nc.Block())
        sd.emit(block, sems)
    return nc



def build_launch2(nch=NCH, wo=True, ffn=True, stage=9):
    nc = bass.Bass("TRN2", target_bir_lowering=False, debug=False)
    I = declare_inputs(nc, launch2=True)
    y = nc.dram_tensor("y", [D, NT], F32, kind="ExternalOutput").ap()
    if os.environ.get("L2_DBG") == "1":
        ogd = nc.dram_tensor("ogd", [D, NT], BF16, kind="ExternalOutput").ap()
        x2d = nc.dram_tensor("x2d", [D, NT], F32, kind="ExternalOutput").ap()
    else:
        ogd = nc.dram_tensor("ogd", [D, NT], BF16, kind="Internal").ap()
        x2d = nc.dram_tensor("x2d", [D, NT], F32, kind="Internal").ap()

    es = ExitStack()
    with es:
        sems = {s: es.enter_context(nc.semaphore(s)) for s in ALL_SEMS}
        sd = Sched()
        ctx = {"nc": nc, "sd": sd, "es": es, "in": I}
        setup_consts(ctx)
        t = ctx["t"]
        Sb = lambda n, sh, dt: es.enter_context(nc.sbuf_tensor(n, list(sh), dt))

        with ExitStack() as p2s:
            S2 = lambda n, sh, dt: p2s.enter_context(nc.sbuf_tensor(n, list(sh), dt))
            q_sb = S2("q_sb", (128, H * NT), BF16)
            k_sb = S2("k_sb", (128, H * NT), BF16)
            v_tm = S2("v_tm", (128, NCH * D), BF16)
            sg_tm = S2("sg_tm", (128, NCH * D), BF16)
            s1_sb = S2("s1_sb", (16, NT), BF16)
            SZ = S2("SZ", (128, 2048), F32)
            Sbf = S2("Sbf", (128, 1024), BF16)
            Zbf = S2("Zbf", (128, 2 * 1024), BF16)
            csb = S2("csb", (1, H * M), F32)
            cbf = S2("cbf", (1, H * M), BF16)
            stg = [S2("stg0", (128, 2048), F32)]
            cstg = None
            pmb = S2("pmb", (128, R), F32)
            pmr = S2("pmr", (1, R), F32)
            es_tm = S2("es_tm", (128, H * M), BF16)
            es_fm = S2("es_fm", (128, 2 * 1024), BF16)
            k_tm = S2("k_tm", (128, H * 128), BF16)
            ab_sb = S2("ab_sb", (128, H * 128), BF16)
            w_tm = S2("w_tm", (128, H * M), BF16)
            w_fm = S2("w_fm", (128, 2 * 1024), BF16)
            enz = S2("enz", (128, H * M), F32)
            es_sc = enz[:, 0:512]
            smx = S2("smx", (128, 4 * H), F32)
            bigsc = S2("bigsc", (128, D), F32)
            sqv = bigsc
            t1s = bigsc
            okn = bigsc[:, 0:H * M]
            e_sb = bigsc[:, H * M:2 * H * M]
            og_tm = ab_sb

            prevld = ("c_stg32",)
            for n_, src in (("q_sb", "q_in"), ("k_sb", "k_in"), ("v_tm", "v_in"), ("sg_tm", "sg_in")):
                tt = {"q_sb": q_sb, "k_sb": k_sb, "v_tm": v_tm, "sg_tm": sg_tm}[n_]
                wr = {"q_sb": tuple(f"q{h}" for h in range(H)), "k_sb": tuple(f"k{h}" for h in range(H)),
                      "v_tm": tuple(f"v{h}" for h in range(H)), "sg_tm": ("sg",)}[n_]
                sd.op("sp", lambda e, tt=tt, src=src: e.dma_start(out=tt[:], in_=I[src]),
                      reads=prevld, writes=wr, dma=True, lane="d")
                prevld = (wr[0],)
            sd.op("sp", lambda e: e.dma_start(out=s1_sb[:], in_=I["s1_in"]),
                  reads=prevld, writes=("s1",), dma=True, lane="d")
            sd.op("sp", lambda e: e.dma_start(out=pmb[:], in_=I["pmask_bc"]),
                  reads=("s1",), writes=("pmb",), dma=True, lane="d")
            sd.op("sp", lambda e: e.dma_start(out=pmr[:], in_=I["pmask_row"]),
                  reads=("pmb",), writes=("pmr",), dma=True, lane="d")

            sd.op("dve", lambda e: e.memset(SZ[:], 0.0), writes=("SZ",))
            sd.op("dve", lambda e: e.memset(es_fm[:], 0.0), writes=("esf",))
            sd.fence("dve", ("esf",), ctx)
            sd.op("dve", lambda e: e.memset(w_fm[:], 0.0), writes=("wfm",))
            sd.fence("dve", ("wfm",), ctx)
            sd.op("dve", lambda e: e.memset(Zbf[:], 0.0), writes=("Zbf",))
            sd.fence("dve", ("Zbf",), ctx)
            sd.op("dve", lambda e: e.memset(csb[:], 0.0), writes=("csb",))
            for r_ in range(R):
                b = r_ % 2
                sd.op("sp", lambda e, r_=r_: e.dma_start(out=stg[0][:], in_=I["st_all"][r_ * 128:(r_ + 1) * 128, :]),
                      reads=("stg0", "pmr"), writes=("stg0",), dma=True, lane="d")
                sd.op("dve", lambda e, r_=r_: e.scalar_tensor_tensor(
                    out=SZ[:], in0=stg[0][:], scalar=pmb[:, r_:r_ + 1], in1=SZ[:], op0=ALU.mult, op1=ALU.add),
                    reads=("stg0", "pmb", "SZ"), writes=("SZ",))
                sd.op("sp", lambda e, r_=r_: e.dma_start(out=stg[0][0:1, 0:H * M], in_=I["c_all"][0:1, r_ * H * M:(r_ + 1) * H * M]),
                      reads=("stg0", "pmr"), writes=("stg0",), dma=True, lane="d")
                sd.op("dve", lambda e, r_=r_: e.scalar_tensor_tensor(
                    out=csb[:], in0=stg[0][0:1, 0:H * M], scalar=pmr[0:1, r_:r_ + 1], in1=csb[:],
                    op0=ALU.mult, op1=ALU.add),
                    reads=("stg0", "pmr", "csb"), writes=("csb",))
            sd.op("dve", lambda e: e.tensor_scalar(out=Sbf[:], in0=SZ[:, 0:1024], scalar1=SCALE, scalar2=None, op0=ALU.mult),
                  reads=("SZ",), writes=("Sbf",))
            sd.fence("dve", ("Sbf",), ctx)
            sd.op("dve", lambda e: e.tensor_copy(
                out=Zbf[0:64, :].rearrange("p (h c) -> p h c", c=128)[:, 0::2, :],
                in_=SZ[0:64, 1024:2048].rearrange("p (h c) -> p h c", c=128)), reads=("SZ",), writes=("Zbf",))
            sd.fence("dve", ("Zbf",), ctx)
            sd.op("dve", lambda e: e.tensor_copy(
                out=Zbf[0:64, :].rearrange("p (h c) -> p h c", c=128)[:, 1::2, :],
                in_=SZ[64:128, 1024:2048].rearrange("p (h c) -> p h c", c=128)), reads=("SZ",), writes=("Zbf",))
            sd.fence("dve", ("Zbf",), ctx)
            sd.op("dve", lambda e: e.tensor_copy(out=cbf[:], in_=csb[:]), reads=("csb",), writes=("cbf",))
            sd.fence("dve", ("cbf",), ctx)

            with ExitStack() as pse:
                P1 = pse.enter_context(nc.psum_tensor("P1", [128, 2048], F32))
                P2 = pse.enter_context(nc.psum_tensor("P2", [128, 1024], F32))
                PB = pse.enter_context(nc.psum_tensor("PB", [128, 1024], BF16))
                for ci in range(nch):
                    emit_es(ctx, ci, s1_sb, es_tm, "es", es_sc, "P2", P2[:, 0:512], scname="enz")
                    emit_ktm(ctx, ci, k_sb, k_tm, PB, "PB")
                    if stage >= 2:
                        for h in range(H):
                            sd.op("pe", lambda e, h=h, ci=ci: e.matmul(
                                P1[:, h * 128:(h + 1) * 128],
                                k_sb[:, h * NT + ci * C: h * NT + (ci + 1) * C],
                                q_sb[:, h * NT + ci * C: h * NT + (ci + 1) * C], start=True, stop=True),
                                reads=(f"k{h}", f"q{h}"), writes=("P1",))
                        sd.op("dve", lambda e: e.tensor_mul(
                            out=ab_sb[:, :].rearrange("p (h c) -> p h c", c=128),
                            in0=P1[:, :].rearrange("p (h c) -> p h c", c=128),
                            in1=t["maskS"][:, :].unsqueeze(1).broadcast_to([128, H, 128])),
                            reads=("P1", "c_maskS"), writes=("ab",))
                        sd.fence("dve", ("ab",), ctx)
                    if stage >= 3:
                        for h in range(H):
                            sd.op("pe", lambda e, h=h, ci=ci: e.matmul(
                                P1[:, h * M:(h + 1) * M],
                                q_sb[:, h * NT + ci * C: h * NT + (ci + 1) * C],
                                Sbf[:, h * M:(h + 1) * M], start=True, stop=False),
                                reads=(f"q{h}", "Sbf", "ab"), writes=("P1",))
                            sd.op("pe", lambda e, h=h: e.matmul(
                                P1[:, h * M:(h + 1) * M], ab_sb[:, h * 128:(h + 1) * 128],
                                es_tm[:, h * M:(h + 1) * M], start=False, stop=True),
                                reads=("ab", "es"), writes=("P1",))
                        for hf in range(2):
                            sd.op("pe", lambda e, hf=hf: e.matmul(
                                P1[:, 1024 + hf * 512:1024 + (hf + 1) * 512], t["triu"][:, :],
                                es_tm[:, hf * 512:(hf + 1) * 512], start=True, stop=False),
                                reads=("es", "c_triu"), writes=("P1",))
                            sd.op("pe", lambda e, hf=hf: e.matmul(
                                P1[:, 1024 + hf * 512:1024 + (hf + 1) * 512], t["onesK1"][:, :],
                                cbf[0:1, hf * 512:(hf + 1) * 512], start=False, stop=True),
                                reads=("cbf", "c_onesK1"), writes=("P1",))
                    if stage >= 4:
                        sd.op("dve", lambda e: e.reciprocal(out=enz[:], in_=P1[:, 1024:2048]),
                              reads=("P1",), writes=("enz",))
                        sd.op("dve", lambda e: e.tensor_mul(out=okn[:], in0=P1[:, 0:1024], in1=enz[:]),
                              reads=("P1", "enz"), writes=("bigsc",))
                        sd.op("dve", lambda e: e.reduce_max(out=smx[:, 0:H], in_=okn[:].rearrange("p (h m) -> p h m", m=M), axis=AX.X),
                              reads=("bigsc",), writes=("rmax",))
                        sd.op("dve", lambda e: e.tensor_sub(
                            out=okn[:].rearrange("p (h m) -> p h m", m=M),
                            in0=okn[:].rearrange("p (h m) -> p h m", m=M),
                            in1=smx[:, 0:H].unsqueeze(2).broadcast_to([128, H, M])),
                            reads=("bigsc", "rmax"), writes=("bigsc",))
                        sd.op("act", lambda e: e.activation(out=e_sb[:], in_=okn[:], func=AF.Exp),
                              reads=("bigsc",), writes=("bigsc",))
                        sd.op("dve", lambda e: e.reduce_sum(out=smx[:, H:2 * H], in_=e_sb[:].rearrange("p (h m) -> p h m", m=M), axis=AX.X),
                              reads=("bigsc",), writes=("ssum",))
                        sd.op("dve", lambda e: e.reciprocal(out=smx[:, 2 * H:3 * H], in_=smx[:, H:2 * H]),
                              reads=("ssum",), writes=("rs",))
                        sd.op("dve", lambda e: e.tensor_mul(
                            out=okn[:].rearrange("p (h m) -> p h m", m=M),
                            in0=e_sb[:].rearrange("p (h m) -> p h m", m=M),
                            in1=smx[:, 2 * H:3 * H].unsqueeze(2).broadcast_to([128, H, M])),
                            reads=("bigsc", "rs"), writes=("bigsc",))
                        sd.op("dve", lambda e: e.tensor_mul(out=w_tm[:], in0=okn[:], in1=enz[:]),
                              reads=("bigsc", "enz"), writes=("wtm",))
                        sd.fence("dve", ("wtm",), ctx)
                    if stage >= 5:
                        for p_ in range(8):
                            sd.op("pe", lambda e, p_=p_: e.transpose(
                                PB[:, p_ * 128:(p_ + 1) * 128], es_tm[:, p_ * 128:(p_ + 1) * 128], t["ident"][:, :]),
                                reads=("es", "c_ident", "ktm"), writes=("PB",))
                        sd.op("dve", lambda e: e.tensor_copy(
                            out=es_fm[0:64, :].rearrange("p (h c) -> p h c", c=128)[:, 0::2, :],
                            in_=PB[0:64, :].rearrange("p (h c) -> p h c", c=128)), reads=("PB",), writes=("esf",))
                        sd.fence("dve", ("esf",), ctx)
                        sd.op("dve", lambda e: e.tensor_copy(
                            out=es_fm[0:64, :].rearrange("p (h c) -> p h c", c=128)[:, 1::2, :],
                            in_=PB[64:128, :].rearrange("p (h c) -> p h c", c=128)), reads=("PB",), writes=("esf",))
                        sd.fence("dve", ("esf",), ctx)
                        for p_ in range(8):
                            sd.op("pe", lambda e, p_=p_: e.transpose(
                                PB[:, p_ * 128:(p_ + 1) * 128], w_tm[:, p_ * 128:(p_ + 1) * 128], t["ident"][:, :]),
                                reads=("wtm", "c_ident", "esf"), writes=("PB",))
                        sd.op("dve", lambda e: e.tensor_copy(
                            out=w_fm[0:64, :].rearrange("p (h c) -> p h c", c=128)[:, 0::2, :],
                            in_=PB[0:64, :].rearrange("p (h c) -> p h c", c=128)), reads=("PB",), writes=("wfm",))
                        sd.fence("dve", ("wfm",), ctx)
                        sd.op("dve", lambda e: e.tensor_copy(
                            out=w_fm[0:64, :].rearrange("p (h c) -> p h c", c=128)[:, 1::2, :],
                            in_=PB[64:128, :].rearrange("p (h c) -> p h c", c=128)), reads=("PB",), writes=("wfm",))
                        sd.fence("dve", ("wfm",), ctx)
                    if stage >= 6:
                        for h in range(H):
                            sd.op("pe", lambda e, h=h: e.matmul(
                                P1[:, h * 128:(h + 1) * 128],
                                es_fm[:, h * 128:(h + 1) * 128],
                                w_fm[:, h * 128:(h + 1) * 128], start=True, stop=True),
                                reads=("esf", "wfm", "bigsc", "enz"), writes=("P1",))
                        sd.op("dve", lambda e: e.tensor_mul(
                            out=ab_sb[:, :].rearrange("p (h c) -> p h c", c=128),
                            in0=P1[:, :].rearrange("p (h c) -> p h c", c=128),
                            in1=t["mask1"][:, :].unsqueeze(1).broadcast_to([128, H, 128])),
                            reads=("P1", "c_mask1"), writes=("ab",))
                        sd.fence("dve", ("ab",), ctx)
                    if stage >= 7:
                        for h in range(H):
                            sd.op("pe", lambda e, h=h: e.matmul(
                                P1[:, h * 128:(h + 1) * 128],
                                w_fm[:, h * 128:(h + 1) * 128],
                                Zbf[:, h * 128:(h + 1) * 128], start=True, stop=False),
                                reads=("wfm", "Zbf", "ab"), writes=("P1",))
                            sd.op("pe", lambda e, h=h, ci=ci: e.matmul(
                                P1[:, h * 128:(h + 1) * 128], ab_sb[:, h * 128:(h + 1) * 128],
                                v_tm[:, ci * D + h * 128: ci * D + (h + 1) * 128], start=False, stop=True),
                                reads=("ab", f"v{h}"), writes=("P1",))
                    if stage >= 8:
                        sd.op("act", lambda e: e.activation(out=sqv[:], in_=P1[:, :], func=AF.Square),
                              reads=("P1", "bigsc"), writes=("bigsc",))
                        sd.op("dve", lambda e: e.reduce_sum(out=smx[:, 3 * H:4 * H], in_=sqv[:].rearrange("p (h c) -> p h c", c=128), axis=AX.X),
                              reads=("bigsc",), writes=("ms",))
                        sd.op("act", lambda e: e.activation(out=smx[:, 3 * H:4 * H], in_=smx[:, 3 * H:4 * H], func=AF.Ln,
                                                            scale=1.0 / DV, bias=t["eps"][:, 0:1]),
                              reads=("ms", "c_eps"), writes=("ms",))
                        sd.op("act", lambda e: e.activation(out=smx[:, 3 * H:4 * H], in_=smx[:, 3 * H:4 * H], func=AF.Exp, scale=-0.5),
                              reads=("ms",), writes=("ms",))
                        sd.op("dve", lambda e: e.tensor_mul(
                            out=t1s[:].rearrange("p (h c) -> p h c", c=128),
                            in0=P1[:, :].rearrange("p (h c) -> p h c", c=128),
                            in1=smx[:, 3 * H:4 * H].unsqueeze(2).broadcast_to([128, H, DV])),
                            reads=("P1", "ms", "bigsc"), writes=("bigsc",))
                        sd.op("dve", lambda e, ci=ci: e.tensor_mul(out=og_tm[:], in0=t1s[:], in1=sg_tm[:, ci * D:(ci + 1) * D]),
                              reads=("bigsc", "sg"), writes=("ogtm",))
                        sd.fence("dve", ("ogtm",), ctx)
                        for rnd in range(2):
                            for hh in range(8):
                                h = rnd * 8 + hh
                                sd.op("pe", lambda e, h=h, hh=hh: e.transpose(
                                    PB[:, hh * 128:(hh + 1) * 128], og_tm[:, h * 128:(h + 1) * 128], t["ident"][:, :]),
                                    reads=("ogtm", "c_ident", "wfm"), writes=("PB",))
                            sd.op("dve", lambda e, rnd=rnd: e.tensor_copy(out=w_fm[:, rnd * 1024:(rnd + 1) * 1024], in_=PB[:, :]),
                                  reads=("PB",), writes=("wfm",))
                            sd.fence("dve", ("wfm",), ctx)
                            dst = ogd.rearrange("(hh p) (ct tt) -> hh p ct tt", p=128, tt=C)[rnd * 8:(rnd + 1) * 8, :, ci, :].transpose([1, 0, 2])
                            sd.op("pool", lambda e, dst=dst, rnd=rnd: e.dma_start(
                                out=dst, in_=w_fm[:, rnd * 1024:(rnd + 1) * 1024].rearrange("p (hh tt) -> p hh tt", tt=C)),
                                reads=("wfm",), writes=("ogd",), dma=True, lane="c")
                    if stage >= 9:
                        for h in range(H):
                            sd.op("pe", lambda e, h=h: e.matmul(
                                P2[:, h * M:(h + 1) * M], k_tm[:, h * 128:(h + 1) * 128],
                                es_tm[:, h * M:(h + 1) * M], start=True, stop=True),
                                reads=("ktm", "es", "P2"), writes=("P2",))
                        sd.op("dve", lambda e: e.tensor_add(out=SZ[:, 0:1024], in0=SZ[:, 0:1024], in1=P2[:, :]),
                              reads=("P2", "SZ"), writes=("SZ",))
                        sd.op("dve", lambda e: e.tensor_scalar(out=Sbf[:], in0=SZ[:, 0:1024], scalar1=SCALE, scalar2=None, op0=ALU.mult),
                              reads=("SZ", "P1"), writes=("Sbf",))
                        sd.fence("dve", ("Sbf",), ctx)
                        for h in range(H):
                            pb_ = 64 * (h % 2)
                            cb_ = (h // 2) * 128
                            sd.op("pe", lambda e, h=h, ci=ci, pb_=pb_, cb_=cb_: e.matmul(
                                P2[pb_:pb_ + 64, cb_:cb_ + 128], es_tm[:, h * M:(h + 1) * M],
                                v_tm[:, ci * D + h * 128: ci * D + (h + 1) * 128], start=True, stop=True),
                                reads=("es", f"v{h}", "SZ"), writes=("P2",))
                        sd.op("dve", lambda e: e.tensor_add(out=SZ[:, 1024:2048], in0=SZ[:, 1024:2048], in1=P2[:, :]),
                              reads=("P2", "SZ"), writes=("SZ",))
                        sd.op("dve", lambda e: e.tensor_copy(
                            out=Zbf[0:64, :].rearrange("p (h c) -> p h c", c=128)[:, 0::2, :],
                            in_=SZ[0:64, 1024:2048].rearrange("p (h c) -> p h c", c=128)),
                            reads=("SZ", "P1"), writes=("Zbf",))
                        sd.fence("dve", ("Zbf",), ctx)
                        sd.op("dve", lambda e: e.tensor_copy(
                            out=Zbf[0:64, :].rearrange("p (h c) -> p h c", c=128)[:, 1::2, :],
                            in_=SZ[64:128, 1024:2048].rearrange("p (h c) -> p h c", c=128)),
                            reads=("SZ", "P1"), writes=("Zbf",))
                        sd.fence("dve", ("Zbf",), ctx)
                        for hf in range(2):
                            sd.op("pe", lambda e, hf=hf: e.matmul(
                                P2[0:1, hf * 512:(hf + 1) * 512], t["ones"][:, 0:1],
                                es_tm[:, hf * 512:(hf + 1) * 512], start=True, stop=True),
                                reads=("es", "c_ones", "P2"), writes=("P2",))
                        sd.op("dve", lambda e: e.tensor_add(out=csb[:], in0=csb[:], in1=P2[0:1, 0:1024]),
                              reads=("P2", "csb"), writes=("csb",))
                        sd.op("dve", lambda e: e.tensor_copy(out=cbf[:], in_=csb[:]),
                              reads=("csb", "P1"), writes=("cbf",))
                        sd.fence("dve", ("cbf",), ctx)
                sd.barrier()
            sd.barrier()

        if wo:
          with ExitStack() as wos:
            SW = lambda n, sh, dt: wos.enter_context(nc.sbuf_tensor(n, list(sh), dt))
            og_sb = SW("og_sb", (128, ND * NT), BF16)
            xs = [SW(f"xs{i}", (128, NT), F32) for i in range(2)]
            x2s = [SW(f"x2s{i}", (128, NT), F32) for i in range(2)]
            for cchunk in range(ND):
                prevog = (f"og{cchunk - 1}",) if cchunk else ()
                sd.op("sp", lambda e, cchunk=cchunk: e.dma_start(
                    out=og_sb[:, cchunk * NT:(cchunk + 1) * NT], in_=ogd[cchunk * 128:(cchunk + 1) * 128, :]),
                    reads=("ogd",) + prevog, writes=(f"og{cchunk}",), dma=True, lane="c")
            with ExitStack() as pse:
                pp = [pse.enter_context(nc.psum_tensor(f"wop{i}", [128, NT], F32)) for i in range(4)]
                slices = [(fi * 128, fog * 512, 512) for fog in range(4) for fi in range(ND)]

                def consume(j, wb, reads):
                    fi = j % ND
                    for c4 in range(4):
                        for hf in range(2):
                            sd.op("pe", lambda e, fi=fi, wb=wb, c4=c4, hf=hf: e.matmul(
                                pp[c4][:, hf * 512:(hf + 1) * 512],
                                wb[:, c4 * 128:(c4 + 1) * 128],
                                og_sb[:, fi * NT + hf * 512: fi * NT + (hf + 1) * 512],
                                start=(fi == 0), stop=(fi == ND - 1)),
                                reads=reads + (f"og{fi}",), writes=(f"wop{c4}",))

                def post(j):
                    if j % ND != ND - 1:
                        return
                    fog = j // ND
                    for c4 in range(4):
                        fo = fog * 4 + c4
                        b = fo % 2
                        sd.op("pool", lambda e, fo=fo, b=b: e.dma_start(
                            out=xs[b][:], in_=I["xT"][fo * 128:(fo + 1) * 128, :]),
                            reads=(f"xs{b}",), writes=(f"xs{b}",), dma=True, lane="ab"[b])
                        sd.op("dve", lambda e, c4=c4, b=b: e.tensor_add(out=x2s[b][:], in0=pp[c4][:, :], in1=xs[b][:]),
                              reads=(f"wop{c4}", f"xs{b}"), writes=(f"x2s{b}",))
                        sd.fence("dve", (f"x2s{b}",), ctx)
                        sd.op("pool", lambda e, fo=fo, b=b: e.dma_start(
                            out=x2d[fo * 128:(fo + 1) * 128, :], in_=x2s[b][:]),
                            reads=(f"x2s{b}",), writes=("x2d", f"x2s{b}"), dma=True, lane="cd"[b])

                pipe_stream(ctx, "wo", I["WoT"], slices, consume, post)
            sd.barrier()

        with ExitStack() as fs:
            SF = lambda n, sh, dt: fs.enter_context(nc.sbuf_tensor(n, list(sh), dt))
            xln = SF("xln", (128, ND * NT), BF16)
            emit_ln(ctx, x2d, xln, "ln2")
            h_sb = SF("h_sb", (128, NI * NT), BF16)
            usc = SF("usc", (128, NT), F32)
            gscol = SF("gscol", (128, NI), F32)
            uscol = SF("uscol", (128, NI), F32)
            dscol = SF("dscol", (128, ND), F32)
            sd.op("sp", lambda e: e.dma_start(out=gscol[:], in_=I["gs_col"]), writes=("gscol",), dma=True, lane="d")
            sd.op("sp", lambda e: e.dma_start(out=uscol[:], in_=I["us_col"]),
                  reads=("gscol",), writes=("uscol",), dma=True, lane="d")
            sd.op("sp", lambda e: e.dma_start(out=dscol[:], in_=I["ds_col"]),
                  reads=("uscol",), writes=("dscol",), dma=True, lane="d")
            if os.environ.get("L2_DBG") == "1":
                xln_dbg = nc.dram_tensor("xln_dbg", [128, ND * NT], BF16, kind="ExternalOutput").ap()
                sd.op("pool", lambda e: e.dma_start(out=xln_dbg, in_=xln[:, :]),
                      reads=tuple(f"hs{fi}" for fi in range(ND)), writes=("xlndbg",), dma=True)
            sgt = SF("sgt", (128, 4 * NT), F32)
            with ExitStack() as pse:
                pg = [pse.enter_context(nc.psum_tensor(f"pg{i}", [128, NT], F32)) for i in range(4)]
                for grp in range(NI // 4):
                    gslices = [(fi * 128, grp * 512, 512) for fi in range(ND)]

                    def gconsume(j, wb, reads, grp=grp):
                        fi = j
                        for ic in range(4):
                            for hf in range(2):
                                sd.op("pe", lambda e, fi=fi, wb=wb, ic=ic, hf=hf: e.matmul(
                                    pg[ic][:, hf * 512:(hf + 1) * 512],
                                    wb[:, ic * 128:(ic + 1) * 128],
                                    xln[:, fi * NT + hf * 512: fi * NT + (hf + 1) * 512],
                                    start=(fi == 0), stop=(fi == ND - 1)),
                                    reads=reads + (f"hs{fi}",), writes=(f"pg{ic}",))

                    def gpost(j, grp=grp):
                        if j != ND - 1:
                            return
                        for ic in range(4):
                            ich = grp * 4 + ic
                            sd.op("act", lambda e, ic=ic, ich=ich: e.activation(
                                out=sgt[:, ic * NT:(ic + 1) * NT], in_=pg[ic][:, :], func=AF.Silu,
                                scale=gscol[:, ich:ich + 1]),
                                reads=(f"pg{ic}", "gscol"), writes=("sgt",))
                        sd.fence("act", ("sgt",), ctx)

                    pipe_stream(ctx, f"gw{grp}", I["gate_wT"], gslices, gconsume, gpost)

                    def uconsume(j, wb, reads, grp=grp):
                        fi = j
                        for ic in range(4):
                            for hf in range(2):
                                sd.op("pe", lambda e, fi=fi, wb=wb, ic=ic, hf=hf: e.matmul(
                                    pg[ic][:, hf * 512:(hf + 1) * 512],
                                    wb[:, ic * 128:(ic + 1) * 128],
                                    xln[:, fi * NT + hf * 512: fi * NT + (hf + 1) * 512],
                                    start=(fi == 0), stop=(fi == ND - 1)),
                                    reads=reads + (f"hs{fi}", "sgt"), writes=(f"pg{ic}",))

                    def upost(j, grp=grp):
                        if j != ND - 1:
                            return
                        for ic in range(4):
                            ich = grp * 4 + ic
                            sd.op("act", lambda e, ic=ic, ich=ich: e.activation(
                                out=usc[:], in_=pg[ic][:, :], func=AF.Copy, scale=uscol[:, ich:ich + 1]),
                                reads=(f"pg{ic}", "uscol"), writes=("usc",))
                            sd.fence("act", ("usc",), ctx)
                            sd.op("dve", lambda e, ic=ic, ich=ich: e.tensor_mul(
                                out=h_sb[:, ich * NT:(ich + 1) * NT], in0=sgt[:, ic * NT:(ic + 1) * NT], in1=usc[:]),
                                reads=("sgt", "usc"), writes=(f"h{ich}",))
                            sd.fence("dve", (f"h{ich}",), ctx)

                    pipe_stream(ctx, f"uw{grp}", I["up_wT"], gslices, uconsume, upost)
                sd.barrier()
            if os.environ.get("L2_DBG") == "1":
                h_dbg = nc.dram_tensor("h_dbg", [128, NI * NT], BF16, kind="ExternalOutput").ap()
                sd.op("pool", lambda e: e.dma_start(out=h_dbg, in_=h_sb[:, :]),
                      reads=tuple(f"h{i}" for i in range(NI)), writes=("hdbg",), dma=True)
            x2b = [SF(f"x2b{i}", (128, NT), F32) for i in range(2)]
            yst = [SF(f"yst{i}", (128, NT), F32) for i in range(2)]
            with ExitStack() as pse:
                pd = [pse.enter_context(nc.psum_tensor(f"pd{i}", [128, NT], F32)) for i in range(4)]
                for fog in range(4):
                    dslices = [(ich * 128, fog * 512, 512) for ich in range(NI)]

                    def dconsume(j, wb, reads, fog=fog):
                        ich = j
                        for c4 in range(4):
                            for hf in range(2):
                                sd.op("pe", lambda e, ich=ich, wb=wb, c4=c4, hf=hf: e.matmul(
                                    pd[c4][:, hf * 512:(hf + 1) * 512],
                                    wb[:, c4 * 128:(c4 + 1) * 128],
                                    h_sb[:, ich * NT + hf * 512: ich * NT + (hf + 1) * 512],
                                    start=(ich == 0), stop=(ich == NI - 1)),
                                    reads=reads + (f"h{ich}",), writes=(f"pd{c4}",))

                    def dpost(j, fog=fog):
                        if j != NI - 1:
                            return
                        for c4 in range(4):
                            fo = fog * 4 + c4
                            b = fo % 2
                            sd.op("pool", lambda e, fo=fo, b=b: e.dma_start(
                                out=x2b[b][:], in_=x2d[fo * 128:(fo + 1) * 128, :]),
                                reads=(f"x2b{b}",), writes=(f"x2b{b}",), dma=True, lane="ab"[b])
                            sd.op("dve", lambda e, c4=c4, fo=fo, b=b: e.scalar_tensor_tensor(
                                out=yst[b][:], in0=pd[c4][:, :], scalar=dscol[:, fo:fo + 1], in1=x2b[b][:],
                                op0=ALU.mult, op1=ALU.add),
                                reads=(f"pd{c4}", "dscol", f"x2b{b}"), writes=(f"yst{b}",))
                            sd.fence("dve", (f"yst{b}",), ctx)
                            sd.op("pool", lambda e, fo=fo, b=b: e.dma_start(
                                out=y[fo * 128:(fo + 1) * 128, :], in_=yst[b][:]),
                                reads=(f"yst{b}",), writes=(f"yst{b}", "yout"), dma=True, lane="cd"[b])

                    pipe_stream(ctx, f"dw{fog}", I["down_wT"], dslices, dconsume, dpost)
                sd.barrier()
            sd.barrier()
        sd.barrier()
        block = es.enter_context(nc.Block())
        sd.emit(block, sems)
    return nc



_cache = {}


def _programs():
    if "l1" not in _cache:
        _cache["l1"] = build_launch1()
        _cache["l2"] = build_launch2()
    return _cache["l1"], _cache["l2"]


def _common_maps(x2d_np):
    pos = np.arange(T, dtype=np.float32)
    inv = (1.0 / 10000.0 ** (np.arange(0, DK, 2, dtype=np.float32) / DK))
    ang = pos[:, None] * inv[None, :]
    cosT = np.cos(ang).T.astype(NB)
    sinT = np.sin(ang).T.astype(NB)
    triu = np.triu(np.ones((128, 128), np.float32))
    consts = {
        "triu_bf": triu.astype(NB),
        "maskS": (triu * SCALE).astype(np.float32),
        "mask1": triu.astype(np.float32),
        "ident": np.eye(128, dtype=NB),
        "ones_bf": np.ones((128, 1), NB),
        "onesK1": np.ones((1, 128), NB),
    }
    xT = np.ascontiguousarray(x2d_np.T)
    maps = []
    for r_ in range(R):
        m = dict(consts)
        m["xT"] = np.ascontiguousarray(xT[:, r_ * NT:(r_ + 1) * NT])
        m["cosr"] = np.ascontiguousarray(cosT[:, r_ * NT:(r_ + 1) * NT])
        m["sinr"] = np.ascontiguousarray(sinT[:, r_ * NT:(r_ + 1) * NT])
        maps.append(m)
    return maps


def kernel(L0_attn_hidden_states, L0_attn_Wq, L0_attn_Wk, L0_attn_Wv, L0_attn_Wg,
           L0_attn_Wo, L0_attn_Ws1, L0_attn_Ws2, L0_attn_bs2, L0_attn_g_norm_weight,
           L0_ffn_gate_w, L0_ffn_up_w, L0_ffn_down_w, L0_ffn_gate_s, L0_ffn_up_s,
           L0_ffn_down_s):
    x = np.asarray(L0_attn_hidden_states, np.float32).reshape(T, D)
    l1, l2 = _programs()
    common = _common_maps(x)

    tr = lambda w: np.ascontiguousarray(np.asarray(w, np.float32).T)
    WqT, WkT, WvT, WgT, WoT = map(tr, (L0_attn_Wq, L0_attn_Wk, L0_attn_Wv, L0_attn_Wg, L0_attn_Wo))
    Ws1T = tr(L0_attn_Ws1)
    Ws2T = tr(L0_attn_Ws2)
    gateT, upT, downT = map(tr, (L0_ffn_gate_w, L0_ffn_up_w, L0_ffn_down_w))
    bs2r = np.asarray(L0_attn_bs2, np.float32).reshape(1, H * M)
    gnw = np.asarray(L0_attn_g_norm_weight, np.float32).reshape(128, 1)
    gs_col = np.ascontiguousarray(np.asarray(L0_ffn_gate_s, np.float32).reshape(NI, 128).T)
    us_col = np.ascontiguousarray(np.asarray(L0_ffn_up_s, np.float32).reshape(NI, 128).T)
    ds_col = np.ascontiguousarray(np.asarray(L0_ffn_down_s, np.float32).reshape(ND, 128).T)

    maps1 = []
    for r_ in range(R):
        m = dict(common[r_])
        m.update({"WqT": WqT, "WkT": WkT, "WvT": WvT, "WgT": WgT,
                  "Ws1T": Ws1T, "Ws2T": Ws2T, "bs2r": bs2r, "gnw": gnw})
        maps1.append(m)
    res1 = run_bass_kernel_spmd(l1, maps1, core_ids=list(range(R)))

    st_all = np.concatenate([res1.results[r_]["st_out"] for r_ in range(R)], axis=0)
    c_all = np.concatenate([res1.results[r_]["c_out"] for r_ in range(R)], axis=1)
    maps2 = []
    for r_ in range(R):
        m = dict(common[r_])
        pm = np.zeros((R,), np.float32)
        pm[:r_] = 1.0
        m.update({
            "WoT": WoT, "gate_wT": gateT, "up_wT": upT, "down_wT": downT,
            "Ws1T": Ws1T, "Ws2T": Ws2T, "bs2r": bs2r,
            "gs_col": gs_col, "us_col": us_col, "ds_col": ds_col,
            "q_in": res1.results[r_]["q_sp"], "k_in": res1.results[r_]["k_sp"],
            "v_in": res1.results[r_]["v_sp"], "sg_in": res1.results[r_]["sg_sp"],
            "s1_in": res1.results[r_]["s1_sp"],
            "st_all": st_all, "c_all": c_all,
            "pmask_bc": np.ascontiguousarray(np.broadcast_to(pm, (128, R))),
            "pmask_row": pm.reshape(1, R),
        })
        maps2.append(m)
    res2 = run_bass_kernel_spmd(l2, maps2, core_ids=list(range(R)))

    out = np.empty((T, D), np.float32)
    for r_ in range(R):
        out[r_ * NT:(r_ + 1) * NT, :] = res2.results[r_]["y"].T
    return out.reshape(1, T, D)


# version 08
# speedup vs baseline: 1.0408x; 1.0408x over previous
import os
import numpy as np
import ml_dtypes
import concourse.bass as bass
import concourse.mybir as mybir
from concourse.bass_utils import run_bass_kernel_spmd
from contextlib import ExitStack

F32 = mybir.dt.float32
BF16 = mybir.dt.bfloat16
AF = mybir.ActivationFunctionType
ALU = mybir.AluOpType
AX = mybir.AxisListType
NB = ml_dtypes.bfloat16

R = 8
T = 8192
NT = T // R
C = 128
NCH = NT // C
H, DK, DV, M, D = 16, 128, 128, 64, 2048
ND = D // 128
INTER = 5632
NI = INTER // 128
SCALE = DK ** -0.5
EPS = 1e-5
CLAMP = 32.0

ENGINES = ("pe", "act", "dve", "pool", "sp")
DMA_OWNER = {"sp": "dsp", "pool": "dpool", "act": "dact"}
DMA_LANES = ("a", "b", "c", "d")
ALL_SEMS = list(ENGINES) + [q + l for q in DMA_OWNER.values() for l in DMA_LANES]


class Sched:

    def __init__(self):
        self.ops = {e: [] for e in ENGINES}
        self.count = {s: 0 for s in ALL_SEMS}
        self.writer = {}
        self.readers = {}
        self.clock = {e: {} for e in ENGINES}
        self.snap = {}
        self.nwaits = 0

    def op(self, eng, fn, reads=(), writes=(), dma=False, lane="a"):
        sem = (DMA_OWNER[eng] + lane) if dma else eng
        inc = 16 if dma else 1
        need = {}
        for r_ in reads:
            w = self.writer.get(r_)
            if w:
                need[w[0]] = max(need.get(w[0], 0), w[1])
        for w_ in writes:
            lw = self.writer.get(w_)
            if lw:
                need[lw[0]] = max(need.get(lw[0], 0), lw[1])
            for rs, rv in self.readers.get(w_, ()):
                need[rs] = max(need.get(rs, 0), rv)
        clk = self.clock[eng]
        waits = []
        for s, v in sorted(need.items(), key=lambda kv: (kv[0] not in ENGINES, kv[0])):
            if eng == "pe" and s == "pe":
                continue
            if clk.get(s, 0) >= v:
                continue
            waits.append((s, v))
            clk[s] = v
            psnap = self.snap.get((s, v))
            if psnap:
                for s2, v2 in psnap.items():
                    if clk.get(s2, 0) < v2:
                        clk[s2] = v2
        self.nwaits += len(waits)
        self.count[sem] += inc
        cnt = self.count[sem]
        snap_clk = dict(clk)
        snap_clk[sem] = max(snap_clk.get(sem, 0), cnt)
        self.snap[(sem, cnt)] = snap_clk
        self.ops[eng].append((waits, fn, sem, inc))
        key = (sem, cnt)
        for w_ in writes:
            self.writer[w_] = key
            self.readers[w_] = []
        for r_ in reads:
            self.readers.setdefault(r_, []).append(key)
        return key

    def fence(self, eng, names, ctx):
        t = ctx["t"]
        if eng == "dve":
            fn = lambda e: e.tensor_copy(out=t["scr"][0:1, 0:1], in_=t["scr"][0:1, 1:2])
        else:
            fn = lambda e: e.activation(out=t["scr"][0:1, 0:1], in_=t["scr"][0:1, 1:2], func=AF.Copy)
        self.op(eng, fn, reads=(), writes=())
        self.op(eng, fn, reads=(), writes=tuple(names))

    def barrier(self):
        vals = {s: v for s, v in self.count.items() if v > 0}
        for e in ENGINES:
            clk = self.clock[e]
            waits = [(s, v) for s, v in sorted(vals.items()) if clk.get(s, 0) < v]
            for s, v in waits:
                clk[s] = v
            if waits:
                self.ops[e].append((waits, None, None, 0))
            self.nwaits += len(waits)

    def emit(self, block, sems):
        def runner(eng_obj, name):
            for waits, fn, sem, inc in self.ops[name]:
                for s, v in waits:
                    eng_obj.wait_ge(sems[s], v)
                if fn is not None:
                    fn(eng_obj).then_inc(sems[sem], inc)

        @block.tensor
        def _(e):
            runner(e, "pe")

        @block.scalar
        def _(e):
            runner(e, "act")

        @block.vector
        def _(e):
            runner(e, "dve")

        @block.gpsimd
        def _(e):
            runner(e, "pool")

        @block.sync
        def _(e):
            runner(e, "sp")


def bcast(ap, shape):
    return ap.broadcast_to(list(shape))


def mmN(sd, psum_ap, lhsT_ap_fn, rhs_ap, n_total, reads, wr, start, stop):
    o = 0
    while o < n_total:
        n = min(512, n_total - o)
        sd.op("pe",
              (lambda e, o=o, n=n: e.matmul(psum_ap[:, o:o + n], lhsT_ap_fn(), rhs_ap[:, o:o + n],
                                            start=start, stop=stop)),
              reads=reads, writes=(wr,))
        o += n



def declare_inputs(nc, launch2):
    ap = {}

    def di(name, shape, dt=F32):
        ap[name] = nc.dram_tensor(name, list(shape), dt, kind="ExternalInput").ap()

    di("xT", (D, NT))
    di("Ws1T", (D, 16))
    di("Ws2T", (16, H * M))
    di("bs2r", (1, H * M))
    di("cosr", (64, NT), BF16)
    di("sinr", (64, NT), BF16)
    di("triu_bf", (128, 128), BF16)
    di("maskS", (128, 128))
    di("mask1", (128, 128))
    di("ident", (128, 128), BF16)
    di("ones_bf", (128, 1), BF16)
    di("onesK1", (1, 128), BF16)
    if not launch2:
        for w in ("WqT", "WkT", "WvT", "WgT"):
            di(w, (D, D))
        di("gnw", (128, 1))
    else:
        di("WoT", (D, D))
        di("gate_wT", (D, INTER))
        di("up_wT", (D, INTER))
        di("down_wT", (INTER, D))
        di("gs_col", (128, NI))
        di("us_col", (128, NI))
        di("ds_col", (128, ND))
        di("q_in", (128, H * NT), BF16)
        di("k_in", (128, H * NT), BF16)
        di("v_in", (128, NCH * D), BF16)
        di("sg_in", (128, NCH * D), BF16)
        di("s1_in", (16, NT), BF16)
        di("st_all", (R * 128, 2048))
        di("c_all", (1, R * H * M))
        di("pmask_bc", (128, R))
        di("pmask_row", (1, R))
    return ap


def setup_consts(ctx):
    nc, sd, es, I = ctx["nc"], ctx["sd"], ctx["es"], ctx["in"]
    S = lambda n, sh, dt: es.enter_context(nc.sbuf_tensor(n, list(sh), dt))
    t = {}
    t["cos"] = S("c_cos", (64, NT), BF16)
    t["sin"] = S("c_sin", (64, NT), BF16)
    t["triu"] = S("c_triu", (128, 128), BF16)
    t["maskS"] = S("c_maskS", (128, 128), F32)
    t["mask1"] = S("c_mask1", (128, 128), F32)
    t["ident"] = S("c_ident", (128, 128), BF16)
    t["ones"] = S("c_ones", (128, 1), BF16)
    t["onesK1"] = S("c_onesK1", (1, 128), BF16)
    t["eps"] = S("c_eps", (128, 1), F32)
    t["bs2bc"] = S("c_bs2bc", (128, H * M), F32)
    t["ws2"] = S("c_ws2", (16, H * M), BF16)
    t["scr"] = S("c_scr", (1, 4), F32)
    t["stg32"] = S("c_stg32", (128, H * M), F32)
    ctx["t"] = t
    prev = ()
    for n, src in (("cos", "cosr"), ("sin", "sinr"), ("triu", "triu_bf"), ("maskS", "maskS"),
                   ("mask1", "mask1"), ("ident", "ident"), ("ones", "ones_bf"), ("onesK1", "onesK1")):
        sd.op("sp", (lambda e, n=n, src=src: e.dma_start(out=t[n][:], in_=I[src])),
              reads=prev, writes=("c_" + n,), dma=True, lane="d")
        prev = ("c_" + n,)
    sd.op("dve", lambda e: e.memset(t["eps"][:], EPS), writes=("c_eps",))
    sd.op("dve", lambda e: e.memset(t["scr"][:], 0.0), writes=("c_scr",))
    sd.op("sp", lambda e: e.dma_start(out=t["bs2bc"][:], in_=I["bs2r"].broadcast_to([128, H * M])),
          reads=prev, writes=("c_bs2bc",), dma=True, lane="d")
    sd.op("sp", lambda e: e.dma_start(out=t["stg32"][0:16, :], in_=I["Ws2T"]),
          reads=("c_bs2bc",), writes=("c_stg32",), dma=True, lane="d")
    sd.op("dve", lambda e: e.tensor_copy(out=t["ws2"][:], in_=t["stg32"][0:16, :]),
          reads=("c_stg32",), writes=("c_ws2",))


def emit_ln(ctx, x_dram_ap, hs, pref):
    nc, sd, t = ctx["nc"], ctx["sd"], ctx["t"]
    rowd = nc.dram_tensor(pref + "_rowd", [2, NT], F32, kind="Internal").ap()
    with ExitStack() as pse:
        es = pse
        xa = [es.enter_context(nc.sbuf_tensor(f"{pref}_x{i}", [128, NT], F32)) for i in range(2)]
        xb = [es.enter_context(nc.sbuf_tensor(f"{pref}_xb{i}", [128, NT], BF16)) for i in range(2)]
        sq = xb
        stm = es.enter_context(nc.sbuf_tensor(pref + "_stm", [1, NT], F32))
        stv = es.enter_context(nc.sbuf_tensor(pref + "_stv", [1, NT], F32))
        mb = es.enter_context(nc.sbuf_tensor(pref + "_mb", [128, NT], F32))
        m2 = mb
        rb = es.enter_context(nc.sbuf_tensor(pref + "_rb", [128, NT], F32))
        ps = [pse.enter_context(nc.psum_tensor(f"{pref}_p{i}", [1, 512], F32)) for i in range(4)]
        for fi in range(ND):
            b = fi % 2
            xr, xbr, sqr = f"{pref}x{b}", f"{pref}xb{b}", f"{pref}sq{b}"
            sd.op("sp", lambda e, fi=fi, b=b: e.dma_start(out=xa[b][:], in_=x_dram_ap[fi * 128:(fi + 1) * 128, :]),
                  reads=(xr,), writes=(xr,), dma=True)
            sd.op("dve", lambda e, b=b: e.tensor_copy(out=xb[b][:], in_=xa[b][:]),
                  reads=(xr,), writes=(xbr, sqr))
            for hf in range(2):
                sl = slice(hf * 512, (hf + 1) * 512)
                sd.op("pe", lambda e, fi=fi, b=b, hf=hf, sl=sl: e.matmul(
                    ps[hf][:, :], t["ones"][:, 0:1], xb[b][:, sl], start=(fi == 0), stop=(fi == ND - 1)),
                    reads=(xbr, "c_ones"), writes=(f"{pref}pm{hf}",))
            sd.op("act", lambda e, b=b: e.activation(out=sq[b][:], in_=xa[b][:], func=AF.Square),
                  reads=(xr,), writes=(sqr, xbr))
            for hf in range(2):
                sl = slice(hf * 512, (hf + 1) * 512)
                sd.op("pe", lambda e, fi=fi, b=b, hf=hf, sl=sl: e.matmul(
                    ps[2 + hf][:, :], t["ones"][:, 0:1], sq[b][:, sl], start=(fi == 0), stop=(fi == ND - 1)),
                    reads=(sqr, "c_ones"), writes=(f"{pref}pq{hf}",))
        for hf in range(2):
            sl = slice(hf * 512, (hf + 1) * 512)
            sd.op("dve", lambda e, hf=hf, sl=sl: e.tensor_scalar(
                out=stm[0:1, sl], in0=ps[hf][:, :], scalar1=1.0 / D, scalar2=None, op0=ALU.mult),
                reads=(f"{pref}pm{hf}",), writes=(f"{pref}mrow",))
            sd.op("dve", lambda e, hf=hf, sl=sl: e.tensor_scalar(
                out=stv[0:1, sl], in0=ps[2 + hf][:, :], scalar1=1.0 / D, scalar2=None, op0=ALU.mult),
                reads=(f"{pref}pq{hf}",), writes=(f"{pref}vrow",))
        sd.op("dve", lambda e: e.tensor_mul(out=m2[0:1, :], in0=stm[0:1, :], in1=stm[0:1, :]),
              reads=(f"{pref}mrow", f"{pref}mb"), writes=(f"{pref}m2",))
        sd.op("dve", lambda e: e.tensor_sub(out=stv[0:1, :], in0=stv[0:1, :], in1=m2[0:1, :]),
              reads=(f"{pref}vrow", f"{pref}m2"), writes=(f"{pref}vrow",))
        sd.fence("dve", (f"{pref}vrow", f"{pref}mrow"), ctx)
        sd.op("act", lambda e: e.activation(out=m2[0:1, :], in_=stv[0:1, :], func=AF.Ln, bias=t["eps"][0:1, 0:1]),
              reads=(f"{pref}vrow", "c_eps", f"{pref}m2"), writes=(f"{pref}m2",))
        sd.op("act", lambda e: e.activation(out=stv[0:1, :], in_=m2[0:1, :], func=AF.Exp, scale=-0.5),
              reads=(f"{pref}m2",), writes=(f"{pref}vrow",))
        sd.fence("act", (f"{pref}vrow",), ctx)
        sd.op("pool", lambda e: e.dma_start(out=rowd[0:1, :], in_=stm[:, :]),
              reads=(f"{pref}mrow",), writes=(f"{pref}rowd",), dma=True)
        sd.op("pool", lambda e: e.dma_start(out=rowd[1:2, :], in_=stv[:, :]),
              reads=(f"{pref}vrow",), writes=(f"{pref}rowd",), dma=True)
        sd.op("pool", lambda e: e.dma_start(out=mb[:], in_=rowd[0:1, :].broadcast_to([128, NT])),
              reads=(f"{pref}rowd", f"{pref}m2"), writes=(f"{pref}mb",), dma=True)
        sd.op("pool", lambda e: e.dma_start(out=rb[:], in_=rowd[1:2, :].broadcast_to([128, NT])),
              reads=(f"{pref}rowd",), writes=(f"{pref}rb",), dma=True, lane="b")
        if os.environ.get("L1_DBG_HS") == "1":
            st_dbg = nc.dram_tensor(pref + "_stdbg", [4, NT], F32, kind="ExternalOutput").ap()
            sd.op("pool", lambda e: e.dma_start(out=st_dbg[0:1, :], in_=stm[:, :]),
                  reads=(f"{pref}mrow",), writes=("stdbg",), dma=True)
            sd.op("pool", lambda e: e.dma_start(out=st_dbg[1:2, :], in_=stv[:, :]),
                  reads=(f"{pref}vrow",), writes=("stdbg",), dma=True)
            sd.op("pool", lambda e: e.dma_start(out=st_dbg[2:3, :], in_=mb[0:1, :]),
                  reads=(f"{pref}mb",), writes=("stdbg",), dma=True)
            sd.op("pool", lambda e: e.dma_start(out=st_dbg[3:4, :], in_=rb[0:1, :]),
                  reads=(f"{pref}rb",), writes=("stdbg",), dma=True)
        for fi in range(ND):
            b = fi % 2
            xr = f"{pref}x{b}"
            sd.op("sp", lambda e, fi=fi, b=b: e.dma_start(out=xa[b][:], in_=x_dram_ap[fi * 128:(fi + 1) * 128, :]),
                  reads=(xr,), writes=(xr,), dma=True, lane="ab"[b])
            sd.op("dve", lambda e, b=b: e.tensor_sub(out=xa[b][:], in0=xa[b][:], in1=mb[:]),
                  reads=(xr, f"{pref}mb"), writes=(xr,))
            sd.op("dve", lambda e, fi=fi, b=b: e.tensor_mul(out=hs[:, fi * NT:(fi + 1) * NT], in0=xa[b][:], in1=rb[:]),
                  reads=(xr, f"{pref}rb"), writes=(f"hs{fi}",))
            sd.fence("dve", (f"hs{fi}",), ctx)
        sd.barrier()




def pipe_stream(ctx, pref, w_dram, slices, consume, post=None):
    nc, sd = ctx["nc"], ctx["sd"]
    W = slices[0][2]
    with ExitStack() as pse:
        wsa = [pse.enter_context(nc.sbuf_tensor(f"{pref}_wa{i}", [128, W], F32)) for i in range(2)]
        wsb = [pse.enter_context(nc.sbuf_tensor(f"{pref}_wb{i}", [128, W], BF16)) for i in range(2)]
        n = len(slices)

        def _dma(i):
            b = i % 2
            ra = f"{pref}wa{b}"
            r0, c0, _ = slices[i]
            sd.op("sp", lambda e, r0=r0, c0=c0, b=b: e.dma_start(
                out=wsa[b][:], in_=w_dram[r0:r0 + 128, c0:c0 + W]),
                reads=(ra,), writes=(ra,), dma=True, lane="ab"[b])

        def _consume(j, gate):
            b = j % 2
            ra, rb = f"{pref}wa{b}", f"{pref}wb{b}"
            sd.op("dve", lambda e, b=b: e.tensor_copy(out=wsb[b][:], in_=wsa[b][:]),
                  reads=(ra,), writes=(rb,))
            sd.fence("dve", (rb,), ctx)
            consume(j, wsb[b], (rb, gate))
            if post:
                post(j)

        for i in range(n):
            _dma(i)
            if i > 0:
                _consume(i - 1, f"{pref}wa{i % 2}")
        bg = n % 2
        sd.op("sp", lambda e, bg=bg: e.dma_start(out=wsa[bg][0:1, 0:8], in_=w_dram[0:1, 0:8]),
              reads=(f"{pref}wa{bg}",), writes=(f"{pref}wa{bg}",), dma=True, lane="ab"[bg])
        _consume(n - 1, f"{pref}wa{bg}")
        sd.barrier()

def emit_weight_stream_mm(ctx, wname, w_dram, hs, dst_fn, n_fo, pref, grp=4, nbuf=None):
    nc, sd = ctx["nc"], ctx["sd"]
    W = grp * 128
    with ExitStack() as pse:
        pp = [pse.enter_context(nc.psum_tensor(f"{pref}_pp{i}", [128, NT], F32)) for i in range(grp)]
        slices = []
        for fog in range(n_fo // grp):
            for fi in range(ND):
                slices.append((fi * 128, fog * W, W))

        def consume(j, wb, reads):
            fi = j % ND
            for c4 in range(grp):
                for hf in range(2):
                    sd.op("pe", lambda e, fi=fi, wb=wb, c4=c4, hf=hf: e.matmul(
                        pp[c4][:, hf * 512:(hf + 1) * 512],
                        wb[:, c4 * 128:(c4 + 1) * 128],
                        hs[:, fi * NT + hf * 512: fi * NT + (hf + 1) * 512],
                        start=(fi == 0), stop=(fi == ND - 1)),
                        reads=reads + (f"hs{fi}",), writes=(f"{pref}pp{c4}",))

        def post(j):
            if j % ND == ND - 1:
                fog = j // ND
                for c4 in range(grp):
                    dst_fn(fog * grp + c4, pp[c4], f"{pref}pp{c4}")

        pipe_stream(ctx, pref, w_dram, slices, consume, post)


def emit_rope(ctx, q_sb, name, sc):
    nc, sd, t = ctx["nc"], ctx["sd"], ctx["t"]
    NGH = 2
    cosv = t["cos"][:, :].unsqueeze(1).broadcast_to([64, NGH, NT])
    sinv = t["sin"][:, :].unsqueeze(1).broadcast_to([64, NGH, NT])
    for g in range(H // NGH):
        heads = tuple(f"{name}{h}" for h in range(NGH * g, NGH * g + NGH))
        x1 = q_sb[0:64, g * NGH * NT:(g + 1) * NGH * NT].rearrange("p (h t) -> p h t", h=NGH)
        x2 = q_sb[64:128, g * NGH * NT:(g + 1) * NGH * NT].rearrange("p (h t) -> p h t", h=NGH)
        v0 = sc[0][:, :].rearrange("p (h t) -> p h t", h=NGH)
        v1 = sc[1][:, :].rearrange("p (h t) -> p h t", h=NGH)
        v2 = sc[2][:, :].rearrange("p (h t) -> p h t", h=NGH)
        v3 = sc[3][:, :].rearrange("p (h t) -> p h t", h=NGH)
        scr = ("rrs0", "rrs1", "rrs2", "rrs3")
        sd.op("dve", lambda e, x2=x2, v3=v3: e.tensor_copy(out=v3, in_=x2),
              reads=heads, writes=(scr[3],))
        sd.op("dve", lambda e, x1=x1, v0=v0: e.tensor_mul(out=v0, in0=x1, in1=sinv),
              reads=heads + ("c_cos", "c_sin"), writes=(scr[0],))
        sd.op("dve", lambda e, x1=x1, v1=v1: e.tensor_mul(out=v1, in0=x1, in1=cosv),
              reads=heads, writes=(scr[1],))
        sd.op("dve", lambda e, v2=v2, v3=v3: e.tensor_mul(out=v2, in0=v3, in1=sinv),
              reads=(scr[3],), writes=(scr[2],))
        sd.op("dve", lambda e, x1=x1, v1=v1, v2=v2: e.tensor_sub(out=x1, in0=v1, in1=v2),
              reads=(scr[1], scr[2]), writes=heads)
        sd.op("dve", lambda e, v1=v1, v3=v3: e.tensor_mul(out=v1, in0=v3, in1=cosv),
              reads=(scr[3],), writes=(scr[1],))
        sd.op("dve", lambda e, v1=v1, v0=v0, v2=v2: e.tensor_add(out=v2, in0=v1, in1=v0),
              reads=(scr[0], scr[1]), writes=(scr[2],))
        sd.op("dve", lambda e, x2=x2, v2=v2: e.tensor_copy(out=x2, in_=v2),
              reads=(scr[2],), writes=heads)
        sd.fence("dve", heads, ctx)


def emit_es(ctx, ci, s1_sb, es_tm, esname, scratch, psname, ps_half, scname=None):
    nc, sd, t = ctx["nc"], ctx["sd"], ctx["t"]
    scname = scname or (esname + "sc")
    for hf in range(2):
        sl = slice(hf * 512, (hf + 1) * 512)
        sd.op("pe", lambda e, ci=ci, sl=sl: e.matmul(
            ps_half[:, :], s1_sb[:, ci * C:(ci + 1) * C], t["ws2"][:, sl], start=True, stop=True),
            reads=("s1", "c_ws2"), writes=(psname,))
        sd.op("dve", lambda e, sl=sl: e.tensor_add(out=scratch[:, 0:512], in0=ps_half[:, :], in1=t["bs2bc"][:, sl]),
              reads=(psname, "c_bs2bc"), writes=(scname,))
        sd.op("dve", lambda e: e.tensor_scalar(out=scratch[:, 0:512], in0=scratch[:, 0:512],
                                               scalar1=CLAMP, scalar2=-CLAMP, op0=ALU.min, op1=ALU.max),
              reads=(scname,), writes=(scname,))
        sd.op("act", lambda e, sl=sl: e.activation(out=es_tm[:, sl], in_=scratch[:, 0:512], func=AF.Exp),
              reads=(scname,), writes=(esname,))
        sd.fence("act", (esname,), ctx)


def emit_ktm(ctx, ci, k_sb, k_tm, pbk, pbkname):
    nc, sd, t = ctx["nc"], ctx["sd"], ctx["t"]
    for rnd in range(2):
        for hh in range(8):
            h = rnd * 8 + hh
            sd.op("pe", lambda e, h=h, hh=hh, ci=ci: e.transpose(
                pbk[:, hh * 128:(hh + 1) * 128], k_sb[:, h * NT + ci * C: h * NT + (ci + 1) * C], t["ident"][:, :]),
                reads=(f"k{h}", "c_ident"), writes=(pbkname,))
        sd.op("dve", lambda e, rnd=rnd: e.tensor_copy(
            out=k_tm[:, rnd * 1024:(rnd + 1) * 1024], in_=pbk[:, :]),
            reads=(pbkname,), writes=("ktm",))
        sd.fence("dve", ("ktm",), ctx)



def build_launch1():
    nc = bass.Bass("TRN2", target_bir_lowering=False, debug=False)
    I = declare_inputs(nc, launch2=False)
    O = {}
    for n, sh, dt in (("q_sp", (128, H * NT), BF16), ("k_sp", (128, H * NT), BF16),
                      ("v_sp", (128, NCH * D), BF16), ("sg_sp", (128, NCH * D), BF16),
                      ("s1_sp", (16, NT), BF16), ("st_out", (128, 2048), F32),
                      ("c_out", (1, H * M), F32)):
        O[n] = nc.dram_tensor(n, list(sh), dt, kind="ExternalOutput").ap()

    es = ExitStack()
    with es:
        sems = {s: es.enter_context(nc.semaphore(s)) for s in ALL_SEMS}
        sd = Sched()
        ctx = {"nc": nc, "sd": sd, "es": es, "in": I}
        setup_consts(ctx)
        t = ctx["t"]
        Sb = lambda n, sh, dt: es.enter_context(nc.sbuf_tensor(n, list(sh), dt))
        hs = Sb("hs", (128, ND * NT), BF16)
        q_sb = Sb("q_sb", (128, H * NT), BF16)
        k_sb = Sb("k_sb", (128, H * NT), BF16)
        v_tm = Sb("v_tm", (128, NCH * D), BF16)
        sg_tm = Sb("sg_tm", (128, NCH * D), BF16)
        s1_sb = Sb("s1_sb", (16, NT), BF16)
        gnw_sb = Sb("gnw_sb", (128, 1), F32)
        ws1b = Sb("ws1b", (128, ND * 16), BF16)

        sd.op("sp", lambda e: e.dma_start(out=gnw_sb[:], in_=I["gnw"]),
              reads=("c_stg32",), writes=("gnw",), dma=True, lane="d")
        sd.op("sp", lambda e: e.dma_start(
            out=t["stg32"][:, 0:ND * 16].rearrange("p (c s) -> p c s", s=16),
            in_=I["Ws1T"].rearrange("(c p) s -> p c s", p=128)),
            reads=("c_stg32", "gnw"), writes=("c_stg32",), dma=True, lane="d")
        sd.op("dve", lambda e: e.tensor_copy(out=ws1b[:], in_=t["stg32"][:, 0:ND * 16]),
              reads=("c_stg32",), writes=("ws1b",))

        emit_ln(ctx, I["xT"], hs, "ln1")
        if os.environ.get("L1_DBG_HS") == "1":
            hs_dbg = nc.dram_tensor("hs_dbg", [128, ND * NT], BF16, kind="ExternalOutput").ap()
            sd.op("pool", lambda e: e.dma_start(out=hs_dbg, in_=hs[:, :]),
                  reads=tuple(f"hs{fi}" for fi in range(ND)), writes=("hsdbg",), dma=True)

        def q_evac(fo, ppt, ppr):
            sd.op("dve", lambda e, fo=fo: e.tensor_copy(out=q_sb[:, fo * NT:(fo + 1) * NT], in_=ppt[:, :]),
                  reads=(ppr,), writes=(f"q{fo}",))
            sd.fence("dve", (f"q{fo}",), ctx)

        def k_evac(fo, ppt, ppr):
            sd.op("act", lambda e, fo=fo: e.activation(out=k_sb[:, fo * NT:(fo + 1) * NT], in_=ppt[:, :], func=AF.Copy),
                  reads=(ppr,), writes=(f"k{fo}",))
            sd.fence("act", (f"k{fo}",), ctx)

        def v_evac(fo, ppt, ppr):
            sd.op("dve", lambda e, ppt=ppt: e.tensor_copy(out=fmbf[:], in_=ppt[:, :]), reads=(ppr,), writes=("fmbf",))
            sd.fence("dve", ("fmbf",), ctx)
            for ct in range(NCH):
                sd.op("pe", lambda e, ct=ct: e.transpose(
                    vgps[:, ct * 128:(ct + 1) * 128], fmbf[:, ct * C:(ct + 1) * C], t["ident"][:, :]),
                    reads=("fmbf", "c_ident"), writes=("vgps",))
            dst = v_tm[:, :].rearrange("p (ct r) -> p ct r", r=D)[:, :, fo * 128:(fo + 1) * 128]
            sd.op("dve", lambda e, dst=dst: e.tensor_copy(out=dst, in_=vgps[:, :].rearrange("p (ct c) -> p ct c", c=128)),
                  reads=("vgps",), writes=(f"v{fo}",))
            sd.fence("dve", (f"v{fo}",), ctx)

        def g_evac(fo, ppt, ppr):
            sd.op("act", lambda e, ppt=ppt: e.activation(out=fmsc[:], in_=ppt[:, :], func=AF.Silu),
                  reads=(ppr,), writes=("fmsc",))
            sd.op("dve", lambda e: e.tensor_mul(out=fmbf[:], in0=fmsc[:], in1=bcast(gnw_sb[:, 0:1], (128, NT))),
                  reads=("fmsc", "gnw"), writes=("fmbf",))
            sd.fence("dve", ("fmbf",), ctx)
            for ct in range(NCH):
                sd.op("pe", lambda e, ct=ct: e.transpose(
                    vgps[:, ct * 128:(ct + 1) * 128], fmbf[:, ct * C:(ct + 1) * C], t["ident"][:, :]),
                    reads=("fmbf", "c_ident"), writes=("vgps",))
            dst = sg_tm[:, :].rearrange("p (ct r) -> p ct r", r=D)[:, :, fo * 128:(fo + 1) * 128]
            sd.op("dve", lambda e, dst=dst: e.tensor_copy(out=dst, in_=vgps[:, :].rearrange("p (ct c) -> p ct c", c=128)),
                  reads=("vgps",), writes=(f"sg{fo}",))
            sd.fence("dve", (f"sg{fo}",), ctx)

        emit_weight_stream_mm(ctx, "q", I["WqT"], hs, q_evac, H, "pq", grp=4)
        emit_weight_stream_mm(ctx, "k", I["WkT"], hs, k_evac, H, "pk", grp=4)
        with ExitStack() as vgs:
            vgps = vgs.enter_context(nc.psum_tensor("vgps", [128, 1024], BF16))
            fmsc = vgs.enter_context(nc.sbuf_tensor("fmsc", [128, NT], F32))
            fmbf = vgs.enter_context(nc.sbuf_tensor("fmbf", [128, NT], BF16))
            emit_weight_stream_mm(ctx, "v", I["WvT"], hs, v_evac, H, "pv", grp=2, nbuf=1)
            emit_weight_stream_mm(ctx, "g", I["WgT"], hs, g_evac, H, "pg", grp=2, nbuf=1)

        with ExitStack() as pse:
            p16 = [pse.enter_context(nc.psum_tensor(f"s1p{i}", [16, 512], F32)) for i in range(2)]
            for fi in range(ND):
                for hf in range(2):
                    sd.op("pe", lambda e, fi=fi, hf=hf: e.matmul(
                        p16[hf][:, :], ws1b[:, fi * 16:(fi + 1) * 16],
                        hs[:, fi * NT + hf * 512: fi * NT + (hf + 1) * 512],
                        start=(fi == 0), stop=(fi == ND - 1)),
                        reads=("ws1b", f"hs{fi}"), writes=(f"s1p{hf}",))
            for hf in range(2):
                sd.op("dve", lambda e, hf=hf: e.tensor_copy(out=s1_sb[:, hf * 512:(hf + 1) * 512], in_=p16[hf][:, :]),
                      reads=(f"s1p{hf}",), writes=("s1",))
            sd.fence("dve", ("s1",), ctx)
            sd.barrier()

        with ExitStack() as rsc:
            rope_sc = [rsc.enter_context(nc.sbuf_tensor(f"rope_s{i}", [64, 2 * NT], BF16)) for i in range(4)]
            emit_rope(ctx, q_sb, "q", rope_sc)
            emit_rope(ctx, k_sb, "k", rope_sc)
            sd.barrier()

        es_tm = Sb("es_tm", (128, H * M), BF16)
        es_sc = Sb("es_sc", (128, 512), F32)
        k_tm = Sb("k_tm", (128, H * 128), BF16)
        stage = Sb("stage", (128, 2048), F32)
        cstage = Sb("cstage", (1, H * M), F32)
        sd.op("dve", lambda e: e.memset(stage[:], 0.0), writes=("stage",))
        sd.op("dve", lambda e: e.memset(cstage[:], 0.0), writes=("cst",))
        with ExitStack() as pse:
            pS = pse.enter_context(nc.psum_tensor("pS", [128, 1024], F32))
            pZ = pse.enter_context(nc.psum_tensor("pZ", [128, 1024], F32))
            pC = pse.enter_context(nc.psum_tensor("pC", [1, 512], F32))
            pEs = pse.enter_context(nc.psum_tensor("pEs", [128, 512], F32))
            pBk = pse.enter_context(nc.psum_tensor("pBk", [128, 1024], BF16))
            for ci in range(NCH):
                emit_es(ctx, ci, s1_sb, es_tm, "es", es_sc, "pEs", pEs)
                emit_ktm(ctx, ci, k_sb, k_tm, pBk, "pBk")
                for h in range(H):
                    sd.op("pe", lambda e, h=h: e.matmul(
                        pS[:, h * M:(h + 1) * M], k_tm[:, h * 128:(h + 1) * 128],
                        es_tm[:, h * M:(h + 1) * M], start=True, stop=True),
                        reads=("ktm", "es"), writes=("pS",))
                sd.op("dve", lambda e: e.tensor_add(out=stage[:, 0:1024], in0=stage[:, 0:1024], in1=pS[:, :]),
                      reads=("pS", "stage"), writes=("stage",))
                for h in range(H):
                    sd.op("pe", lambda e, h=h, ci=ci: e.matmul(
                        pZ[64 * (h % 2):64 * (h % 2) + 64, (h // 2) * 128:(h // 2) * 128 + 128],
                        es_tm[:, h * M:(h + 1) * M],
                        v_tm[:, ci * D + h * 128: ci * D + (h + 1) * 128], start=True, stop=True),
                        reads=("es", f"v{h}"), writes=("pZ",))
                sd.op("dve", lambda e: e.tensor_add(out=stage[:, 1024:2048], in0=stage[:, 1024:2048], in1=pZ[:, :]),
                      reads=("pZ", "stage"), writes=("stage",))
                sd.fence("dve", ("stage",), ctx)
                for hf in range(2):
                    sd.op("pe", lambda e, hf=hf: e.matmul(
                        pC[:, :], t["ones"][:, 0:1], es_tm[:, hf * 512:(hf + 1) * 512], start=True, stop=True),
                        reads=("es", "c_ones"), writes=("pC",))
                    sd.op("dve", lambda e, hf=hf: e.tensor_add(
                        out=cstage[0:1, hf * 512:(hf + 1) * 512], in0=cstage[0:1, hf * 512:(hf + 1) * 512], in1=pC[:, :]),
                        reads=("pC", "cst"), writes=("cst",))
                    sd.fence("dve", ("cst",), ctx)
            sd.barrier()

        sd.op("pool", lambda e: e.dma_start(out=O["st_out"], in_=stage[:, :]), reads=("stage",), writes=("o1",), dma=True, lane="c")
        sd.op("pool", lambda e: e.dma_start(out=O["c_out"], in_=cstage[:, :]), reads=("cst", "o1"), writes=("o2",), dma=True, lane="c")
        sd.op("pool", lambda e: e.dma_start(out=O["q_sp"], in_=q_sb[:, :]),
              reads=tuple(f"q{h}" for h in range(H)) + ("o2",), writes=("o3",), dma=True, lane="c")
        sd.op("pool", lambda e: e.dma_start(out=O["k_sp"], in_=k_sb[:, :]),
              reads=tuple(f"k{h}" for h in range(H)) + ("o3",), writes=("o4",), dma=True, lane="c")
        sd.op("pool", lambda e: e.dma_start(out=O["v_sp"], in_=v_tm[:, :]),
              reads=tuple(f"v{h}" for h in range(H)) + ("o4",), writes=("o5",), dma=True, lane="c")
        sd.op("pool", lambda e: e.dma_start(out=O["sg_sp"], in_=sg_tm[:, :]),
              reads=tuple(f"sg{h}" for h in range(H)) + ("o5",), writes=("o6",), dma=True, lane="c")
        sd.op("pool", lambda e: e.dma_start(out=O["s1_sp"], in_=s1_sb[:, :]), reads=("s1", "o6"), writes=("o7",), dma=True, lane="c")
        sd.barrier()
        block = es.enter_context(nc.Block())
        sd.emit(block, sems)
    return nc



def build_launch2(nch=NCH, wo=True, ffn=True, stage=9):
    nc = bass.Bass("TRN2", target_bir_lowering=False, debug=False)
    I = declare_inputs(nc, launch2=True)
    y = nc.dram_tensor("y", [D, NT], F32, kind="ExternalOutput").ap()
    if os.environ.get("L2_DBG") == "1":
        ogd = nc.dram_tensor("ogd", [D, NT], BF16, kind="ExternalOutput").ap()
        x2d = nc.dram_tensor("x2d", [D, NT], F32, kind="ExternalOutput").ap()
    else:
        ogd = nc.dram_tensor("ogd", [D, NT], BF16, kind="Internal").ap()
        x2d = nc.dram_tensor("x2d", [D, NT], F32, kind="Internal").ap()

    es = ExitStack()
    with es:
        sems = {s: es.enter_context(nc.semaphore(s)) for s in ALL_SEMS}
        sd = Sched()
        ctx = {"nc": nc, "sd": sd, "es": es, "in": I}
        setup_consts(ctx)
        t = ctx["t"]
        Sb = lambda n, sh, dt: es.enter_context(nc.sbuf_tensor(n, list(sh), dt))

        with ExitStack() as p2s:
            S2 = lambda n, sh, dt: p2s.enter_context(nc.sbuf_tensor(n, list(sh), dt))
            q_sb = S2("q_sb", (128, H * NT), BF16)
            k_sb = S2("k_sb", (128, H * NT), BF16)
            v_tm = S2("v_tm", (128, NCH * D), BF16)
            sg_tm = S2("sg_tm", (128, NCH * D), BF16)
            s1_sb = S2("s1_sb", (16, NT), BF16)
            SZ = S2("SZ", (128, 2048), F32)
            Sbf = S2("Sbf", (128, 1024), BF16)
            Zbf = S2("Zbf", (128, 2 * 1024), BF16)
            csb = S2("csb", (1, H * M), F32)
            cbf = S2("cbf", (1, H * M), BF16)
            stg = [S2("stg0", (128, 2048), F32)]
            cstg = None
            pmb = S2("pmb", (128, R), F32)
            pmr = S2("pmr", (1, R), F32)
            es_tm = S2("es_tm", (128, H * M), BF16)
            es_fm = S2("es_fm", (128, 2 * 1024), BF16)
            k_tm = S2("k_tm", (128, H * 128), BF16)
            ab_sb = S2("ab_sb", (128, H * 128), BF16)
            w_tm = S2("w_tm", (128, H * M), BF16)
            w_fm = S2("w_fm", (128, 2 * 1024), BF16)
            enz = S2("enz", (128, H * M), F32)
            es_sc = enz[:, 0:512]
            smx = S2("smx", (128, 4 * H), F32)
            bigsc = S2("bigsc", (128, D), F32)
            sqv = bigsc
            t1s = bigsc
            okn = bigsc[:, 0:H * M]
            e_sb = bigsc[:, H * M:2 * H * M]
            og_tm = ab_sb

            prevld = ("c_stg32",)
            for n_, src in (("q_sb", "q_in"), ("k_sb", "k_in"), ("v_tm", "v_in"), ("sg_tm", "sg_in")):
                tt = {"q_sb": q_sb, "k_sb": k_sb, "v_tm": v_tm, "sg_tm": sg_tm}[n_]
                wr = {"q_sb": tuple(f"q{h}" for h in range(H)), "k_sb": tuple(f"k{h}" for h in range(H)),
                      "v_tm": tuple(f"v{h}" for h in range(H)), "sg_tm": ("sg",)}[n_]
                sd.op("sp", lambda e, tt=tt, src=src: e.dma_start(out=tt[:], in_=I[src]),
                      reads=prevld, writes=wr, dma=True, lane="d")
                prevld = (wr[0],)
            sd.op("sp", lambda e: e.dma_start(out=s1_sb[:], in_=I["s1_in"]),
                  reads=prevld, writes=("s1",), dma=True, lane="d")
            sd.op("sp", lambda e: e.dma_start(out=pmb[:], in_=I["pmask_bc"]),
                  reads=("s1",), writes=("pmb",), dma=True, lane="d")
            sd.op("sp", lambda e: e.dma_start(out=pmr[:], in_=I["pmask_row"]),
                  reads=("pmb",), writes=("pmr",), dma=True, lane="d")

            sd.op("dve", lambda e: e.memset(SZ[:], 0.0), writes=("SZ",))
            sd.op("dve", lambda e: e.memset(es_fm[:], 0.0), writes=("esf",))
            sd.fence("dve", ("esf",), ctx)
            sd.op("dve", lambda e: e.memset(w_fm[:], 0.0), writes=("wfm",))
            sd.fence("dve", ("wfm",), ctx)
            sd.op("dve", lambda e: e.memset(Zbf[:], 0.0), writes=("Zbf",))
            sd.fence("dve", ("Zbf",), ctx)
            sd.op("dve", lambda e: e.memset(csb[:], 0.0), writes=("csb",))
            for r_ in range(R):
                b = r_ % 2
                sd.op("sp", lambda e, r_=r_: e.dma_start(out=stg[0][:], in_=I["st_all"][r_ * 128:(r_ + 1) * 128, :]),
                      reads=("stg0", "pmr"), writes=("stg0",), dma=True, lane="d")
                sd.op("dve", lambda e, r_=r_: e.scalar_tensor_tensor(
                    out=SZ[:], in0=stg[0][:], scalar=pmb[:, r_:r_ + 1], in1=SZ[:], op0=ALU.mult, op1=ALU.add),
                    reads=("stg0", "pmb", "SZ"), writes=("SZ",))
                sd.op("sp", lambda e, r_=r_: e.dma_start(out=stg[0][0:1, 0:H * M], in_=I["c_all"][0:1, r_ * H * M:(r_ + 1) * H * M]),
                      reads=("stg0", "pmr"), writes=("stg0",), dma=True, lane="d")
                sd.op("dve", lambda e, r_=r_: e.scalar_tensor_tensor(
                    out=csb[:], in0=stg[0][0:1, 0:H * M], scalar=pmr[0:1, r_:r_ + 1], in1=csb[:],
                    op0=ALU.mult, op1=ALU.add),
                    reads=("stg0", "pmr", "csb"), writes=("csb",))
            sd.op("dve", lambda e: e.tensor_scalar(out=Sbf[:], in0=SZ[:, 0:1024], scalar1=SCALE, scalar2=None, op0=ALU.mult),
                  reads=("SZ",), writes=("Sbf",))
            sd.fence("dve", ("Sbf",), ctx)
            sd.op("dve", lambda e: e.tensor_copy(
                out=Zbf[0:64, :].rearrange("p (h c) -> p h c", c=128)[:, 0::2, :],
                in_=SZ[0:64, 1024:2048].rearrange("p (h c) -> p h c", c=128)), reads=("SZ",), writes=("Zbf",))
            sd.fence("dve", ("Zbf",), ctx)
            sd.op("dve", lambda e: e.tensor_copy(
                out=Zbf[0:64, :].rearrange("p (h c) -> p h c", c=128)[:, 1::2, :],
                in_=SZ[64:128, 1024:2048].rearrange("p (h c) -> p h c", c=128)), reads=("SZ",), writes=("Zbf",))
            sd.fence("dve", ("Zbf",), ctx)
            sd.op("dve", lambda e: e.tensor_copy(out=cbf[:], in_=csb[:]), reads=("csb",), writes=("cbf",))
            sd.fence("dve", ("cbf",), ctx)

            with ExitStack() as pse:
                P1 = pse.enter_context(nc.psum_tensor("P1", [128, 2048], F32))
                P2 = pse.enter_context(nc.psum_tensor("P2", [128, 1024], F32))
                PB = pse.enter_context(nc.psum_tensor("PB", [128, 1024], BF16))
                for ci in range(nch):
                    emit_es(ctx, ci, s1_sb, es_tm, "es", es_sc, "P2", P2[:, 0:512], scname="enz")
                    emit_ktm(ctx, ci, k_sb, k_tm, PB, "PB")
                    if stage >= 2:
                        for h in range(H):
                            sd.op("pe", lambda e, h=h, ci=ci: e.matmul(
                                P1[:, h * 128:(h + 1) * 128],
                                k_sb[:, h * NT + ci * C: h * NT + (ci + 1) * C],
                                q_sb[:, h * NT + ci * C: h * NT + (ci + 1) * C], start=True, stop=True),
                                reads=(f"k{h}", f"q{h}"), writes=("P1",))
                        sd.op("dve", lambda e: e.tensor_mul(
                            out=ab_sb[:, :].rearrange("p (h c) -> p h c", c=128),
                            in0=P1[:, :].rearrange("p (h c) -> p h c", c=128),
                            in1=t["maskS"][:, :].unsqueeze(1).broadcast_to([128, H, 128])),
                            reads=("P1", "c_maskS"), writes=("ab",))
                        sd.fence("dve", ("ab",), ctx)
                    if stage >= 3:
                        for h in range(H):
                            sd.op("pe", lambda e, h=h, ci=ci: e.matmul(
                                P1[:, h * M:(h + 1) * M],
                                q_sb[:, h * NT + ci * C: h * NT + (ci + 1) * C],
                                Sbf[:, h * M:(h + 1) * M], start=True, stop=False),
                                reads=(f"q{h}", "Sbf", "ab"), writes=("P1",))
                            sd.op("pe", lambda e, h=h: e.matmul(
                                P1[:, h * M:(h + 1) * M], ab_sb[:, h * 128:(h + 1) * 128],
                                es_tm[:, h * M:(h + 1) * M], start=False, stop=True),
                                reads=("ab", "es"), writes=("P1",))
                        for hf in range(2):
                            sd.op("pe", lambda e, hf=hf: e.matmul(
                                P1[:, 1024 + hf * 512:1024 + (hf + 1) * 512], t["triu"][:, :],
                                es_tm[:, hf * 512:(hf + 1) * 512], start=True, stop=False),
                                reads=("es", "c_triu"), writes=("P1",))
                            sd.op("pe", lambda e, hf=hf: e.matmul(
                                P1[:, 1024 + hf * 512:1024 + (hf + 1) * 512], t["onesK1"][:, :],
                                cbf[0:1, hf * 512:(hf + 1) * 512], start=False, stop=True),
                                reads=("cbf", "c_onesK1"), writes=("P1",))
                    if stage >= 4:
                        sd.op("dve", lambda e: e.reciprocal(out=enz[:], in_=P1[:, 1024:2048]),
                              reads=("P1",), writes=("enz",))
                        sd.op("dve", lambda e: e.tensor_mul(out=okn[:], in0=P1[:, 0:1024], in1=enz[:]),
                              reads=("P1", "enz"), writes=("bigsc",))
                        sd.op("dve", lambda e: e.reduce_max(out=smx[:, 0:H], in_=okn[:].rearrange("p (h m) -> p h m", m=M), axis=AX.X),
                              reads=("bigsc",), writes=("rmax",))
                        sd.op("dve", lambda e: e.tensor_sub(
                            out=okn[:].rearrange("p (h m) -> p h m", m=M),
                            in0=okn[:].rearrange("p (h m) -> p h m", m=M),
                            in1=smx[:, 0:H].unsqueeze(2).broadcast_to([128, H, M])),
                            reads=("bigsc", "rmax"), writes=("bigsc",))
                        sd.op("act", lambda e: e.activation(out=e_sb[:], in_=okn[:], func=AF.Exp),
                              reads=("bigsc",), writes=("bigsc",))
                        sd.op("dve", lambda e: e.reduce_sum(out=smx[:, H:2 * H], in_=e_sb[:].rearrange("p (h m) -> p h m", m=M), axis=AX.X),
                              reads=("bigsc",), writes=("ssum",))
                        sd.op("dve", lambda e: e.reciprocal(out=smx[:, 2 * H:3 * H], in_=smx[:, H:2 * H]),
                              reads=("ssum",), writes=("rs",))
                        sd.op("dve", lambda e: e.tensor_mul(
                            out=okn[:].rearrange("p (h m) -> p h m", m=M),
                            in0=e_sb[:].rearrange("p (h m) -> p h m", m=M),
                            in1=smx[:, 2 * H:3 * H].unsqueeze(2).broadcast_to([128, H, M])),
                            reads=("bigsc", "rs"), writes=("bigsc",))
                        sd.op("dve", lambda e: e.tensor_mul(out=w_tm[:], in0=okn[:], in1=enz[:]),
                              reads=("bigsc", "enz"), writes=("wtm",))
                        sd.fence("dve", ("wtm",), ctx)
                    if stage >= 5:
                        for p_ in range(8):
                            sd.op("pe", lambda e, p_=p_: e.transpose(
                                PB[:, p_ * 128:(p_ + 1) * 128], es_tm[:, p_ * 128:(p_ + 1) * 128], t["ident"][:, :]),
                                reads=("es", "c_ident", "ktm"), writes=("PB",))
                        sd.op("dve", lambda e: e.tensor_copy(
                            out=es_fm[0:64, :].rearrange("p (h c) -> p h c", c=128)[:, 0::2, :],
                            in_=PB[0:64, :].rearrange("p (h c) -> p h c", c=128)), reads=("PB",), writes=("esf",))
                        sd.fence("dve", ("esf",), ctx)
                        sd.op("dve", lambda e: e.tensor_copy(
                            out=es_fm[0:64, :].rearrange("p (h c) -> p h c", c=128)[:, 1::2, :],
                            in_=PB[64:128, :].rearrange("p (h c) -> p h c", c=128)), reads=("PB",), writes=("esf",))
                        sd.fence("dve", ("esf",), ctx)
                        for p_ in range(8):
                            sd.op("pe", lambda e, p_=p_: e.transpose(
                                PB[:, p_ * 128:(p_ + 1) * 128], w_tm[:, p_ * 128:(p_ + 1) * 128], t["ident"][:, :]),
                                reads=("wtm", "c_ident", "esf"), writes=("PB",))
                        sd.op("dve", lambda e: e.tensor_copy(
                            out=w_fm[0:64, :].rearrange("p (h c) -> p h c", c=128)[:, 0::2, :],
                            in_=PB[0:64, :].rearrange("p (h c) -> p h c", c=128)), reads=("PB",), writes=("wfm",))
                        sd.fence("dve", ("wfm",), ctx)
                        sd.op("dve", lambda e: e.tensor_copy(
                            out=w_fm[0:64, :].rearrange("p (h c) -> p h c", c=128)[:, 1::2, :],
                            in_=PB[64:128, :].rearrange("p (h c) -> p h c", c=128)), reads=("PB",), writes=("wfm",))
                        sd.fence("dve", ("wfm",), ctx)
                    if stage >= 6:
                        for h in range(H):
                            sd.op("pe", lambda e, h=h: e.matmul(
                                P1[:, h * 128:(h + 1) * 128],
                                es_fm[:, h * 128:(h + 1) * 128],
                                w_fm[:, h * 128:(h + 1) * 128], start=True, stop=True),
                                reads=("esf", "wfm", "bigsc", "enz"), writes=("P1",))
                        sd.op("dve", lambda e: e.tensor_mul(
                            out=ab_sb[:, :].rearrange("p (h c) -> p h c", c=128),
                            in0=P1[:, :].rearrange("p (h c) -> p h c", c=128),
                            in1=t["mask1"][:, :].unsqueeze(1).broadcast_to([128, H, 128])),
                            reads=("P1", "c_mask1"), writes=("ab",))
                        sd.fence("dve", ("ab",), ctx)
                    if stage >= 7:
                        for h in range(H):
                            sd.op("pe", lambda e, h=h: e.matmul(
                                P1[:, h * 128:(h + 1) * 128],
                                w_fm[:, h * 128:(h + 1) * 128],
                                Zbf[:, h * 128:(h + 1) * 128], start=True, stop=False),
                                reads=("wfm", "Zbf", "ab"), writes=("P1",))
                            sd.op("pe", lambda e, h=h, ci=ci: e.matmul(
                                P1[:, h * 128:(h + 1) * 128], ab_sb[:, h * 128:(h + 1) * 128],
                                v_tm[:, ci * D + h * 128: ci * D + (h + 1) * 128], start=False, stop=True),
                                reads=("ab", f"v{h}"), writes=("P1",))
                    if stage >= 8:
                        sd.op("act", lambda e: e.activation(out=sqv[:], in_=P1[:, :], func=AF.Square),
                              reads=("P1", "bigsc"), writes=("bigsc",))
                        sd.op("dve", lambda e: e.reduce_sum(out=smx[:, 3 * H:4 * H], in_=sqv[:].rearrange("p (h c) -> p h c", c=128), axis=AX.X),
                              reads=("bigsc",), writes=("ms",))
                        sd.op("act", lambda e: e.activation(out=smx[:, 3 * H:4 * H], in_=smx[:, 3 * H:4 * H], func=AF.Ln,
                                                            scale=1.0 / DV, bias=t["eps"][:, 0:1]),
                              reads=("ms", "c_eps"), writes=("ms",))
                        sd.op("act", lambda e: e.activation(out=smx[:, 3 * H:4 * H], in_=smx[:, 3 * H:4 * H], func=AF.Exp, scale=-0.5),
                              reads=("ms",), writes=("ms",))
                        sd.op("dve", lambda e: e.tensor_mul(
                            out=t1s[:].rearrange("p (h c) -> p h c", c=128),
                            in0=P1[:, :].rearrange("p (h c) -> p h c", c=128),
                            in1=smx[:, 3 * H:4 * H].unsqueeze(2).broadcast_to([128, H, DV])),
                            reads=("P1", "ms", "bigsc"), writes=("bigsc",))
                        sd.op("dve", lambda e, ci=ci: e.tensor_mul(out=og_tm[:], in0=t1s[:], in1=sg_tm[:, ci * D:(ci + 1) * D]),
                              reads=("bigsc", "sg"), writes=("ogtm",))
                        sd.fence("dve", ("ogtm",), ctx)
                        for rnd in range(2):
                            for hh in range(8):
                                h = rnd * 8 + hh
                                sd.op("pe", lambda e, h=h, hh=hh: e.transpose(
                                    PB[:, hh * 128:(hh + 1) * 128], og_tm[:, h * 128:(h + 1) * 128], t["ident"][:, :]),
                                    reads=("ogtm", "c_ident", "wfm"), writes=("PB",))
                            sd.op("dve", lambda e, rnd=rnd: e.tensor_copy(out=w_fm[:, rnd * 1024:(rnd + 1) * 1024], in_=PB[:, :]),
                                  reads=("PB",), writes=("wfm",))
                            sd.fence("dve", ("wfm",), ctx)
                            dst = ogd.rearrange("(hh p) (ct tt) -> hh p ct tt", p=128, tt=C)[rnd * 8:(rnd + 1) * 8, :, ci, :].transpose([1, 0, 2])
                            sd.op("pool", lambda e, dst=dst, rnd=rnd: e.dma_start(
                                out=dst, in_=w_fm[:, rnd * 1024:(rnd + 1) * 1024].rearrange("p (hh tt) -> p hh tt", tt=C)),
                                reads=("wfm",), writes=("ogd",), dma=True, lane="c")
                    if stage >= 9:
                        for h in range(H):
                            sd.op("pe", lambda e, h=h: e.matmul(
                                P2[:, h * M:(h + 1) * M], k_tm[:, h * 128:(h + 1) * 128],
                                es_tm[:, h * M:(h + 1) * M], start=True, stop=True),
                                reads=("ktm", "es", "P2"), writes=("P2",))
                        sd.op("dve", lambda e: e.tensor_add(out=SZ[:, 0:1024], in0=SZ[:, 0:1024], in1=P2[:, :]),
                              reads=("P2", "SZ"), writes=("SZ",))
                        sd.op("dve", lambda e: e.tensor_scalar(out=Sbf[:], in0=SZ[:, 0:1024], scalar1=SCALE, scalar2=None, op0=ALU.mult),
                              reads=("SZ", "P1"), writes=("Sbf",))
                        sd.fence("dve", ("Sbf",), ctx)
                        for h in range(H):
                            pb_ = 64 * (h % 2)
                            cb_ = (h // 2) * 128
                            sd.op("pe", lambda e, h=h, ci=ci, pb_=pb_, cb_=cb_: e.matmul(
                                P2[pb_:pb_ + 64, cb_:cb_ + 128], es_tm[:, h * M:(h + 1) * M],
                                v_tm[:, ci * D + h * 128: ci * D + (h + 1) * 128], start=True, stop=True),
                                reads=("es", f"v{h}", "SZ"), writes=("P2",))
                        sd.op("dve", lambda e: e.tensor_add(out=SZ[:, 1024:2048], in0=SZ[:, 1024:2048], in1=P2[:, :]),
                              reads=("P2", "SZ"), writes=("SZ",))
                        sd.op("dve", lambda e: e.tensor_copy(
                            out=Zbf[0:64, :].rearrange("p (h c) -> p h c", c=128)[:, 0::2, :],
                            in_=SZ[0:64, 1024:2048].rearrange("p (h c) -> p h c", c=128)),
                            reads=("SZ", "P1"), writes=("Zbf",))
                        sd.fence("dve", ("Zbf",), ctx)
                        sd.op("dve", lambda e: e.tensor_copy(
                            out=Zbf[0:64, :].rearrange("p (h c) -> p h c", c=128)[:, 1::2, :],
                            in_=SZ[64:128, 1024:2048].rearrange("p (h c) -> p h c", c=128)),
                            reads=("SZ", "P1"), writes=("Zbf",))
                        sd.fence("dve", ("Zbf",), ctx)
                        for hf in range(2):
                            sd.op("pe", lambda e, hf=hf: e.matmul(
                                P2[0:1, hf * 512:(hf + 1) * 512], t["ones"][:, 0:1],
                                es_tm[:, hf * 512:(hf + 1) * 512], start=True, stop=True),
                                reads=("es", "c_ones", "P2"), writes=("P2",))
                        sd.op("dve", lambda e: e.tensor_add(out=csb[:], in0=csb[:], in1=P2[0:1, 0:1024]),
                              reads=("P2", "csb"), writes=("csb",))
                        sd.op("dve", lambda e: e.tensor_copy(out=cbf[:], in_=csb[:]),
                              reads=("csb", "P1"), writes=("cbf",))
                        sd.fence("dve", ("cbf",), ctx)
                sd.barrier()
            sd.barrier()

        if wo:
          with ExitStack() as wos:
            SW = lambda n, sh, dt: wos.enter_context(nc.sbuf_tensor(n, list(sh), dt))
            og_sb = SW("og_sb", (128, ND * NT), BF16)
            xs = [SW(f"xs{i}", (128, NT), F32) for i in range(2)]
            x2s = [SW(f"x2s{i}", (128, NT), F32) for i in range(2)]
            for cchunk in range(ND):
                prevog = (f"og{cchunk - 1}",) if cchunk else ()
                sd.op("sp", lambda e, cchunk=cchunk: e.dma_start(
                    out=og_sb[:, cchunk * NT:(cchunk + 1) * NT], in_=ogd[cchunk * 128:(cchunk + 1) * 128, :]),
                    reads=("ogd",) + prevog, writes=(f"og{cchunk}",), dma=True, lane="c")
            with ExitStack() as pse:
                pp = [pse.enter_context(nc.psum_tensor(f"wop{i}", [128, NT], F32)) for i in range(4)]
                slices = [(fi * 128, fog * 512, 512) for fog in range(4) for fi in range(ND)]

                def consume(j, wb, reads):
                    fi = j % ND
                    for c4 in range(4):
                        for hf in range(2):
                            sd.op("pe", lambda e, fi=fi, wb=wb, c4=c4, hf=hf: e.matmul(
                                pp[c4][:, hf * 512:(hf + 1) * 512],
                                wb[:, c4 * 128:(c4 + 1) * 128],
                                og_sb[:, fi * NT + hf * 512: fi * NT + (hf + 1) * 512],
                                start=(fi == 0), stop=(fi == ND - 1)),
                                reads=reads + (f"og{fi}",), writes=(f"wop{c4}",))

                def post(j):
                    if j % ND != ND - 1:
                        return
                    fog = j // ND
                    for c4 in range(4):
                        fo = fog * 4 + c4
                        b = fo % 2
                        sd.op("pool", lambda e, fo=fo, b=b: e.dma_start(
                            out=xs[b][:], in_=I["xT"][fo * 128:(fo + 1) * 128, :]),
                            reads=(f"xs{b}",), writes=(f"xs{b}",), dma=True, lane="ab"[b])
                        sd.op("dve", lambda e, c4=c4, b=b: e.tensor_add(out=x2s[b][:], in0=pp[c4][:, :], in1=xs[b][:]),
                              reads=(f"wop{c4}", f"xs{b}"), writes=(f"x2s{b}",))
                        sd.fence("dve", (f"x2s{b}",), ctx)
                        sd.op("pool", lambda e, fo=fo, b=b: e.dma_start(
                            out=x2d[fo * 128:(fo + 1) * 128, :], in_=x2s[b][:]),
                            reads=(f"x2s{b}",), writes=("x2d", f"x2s{b}"), dma=True, lane="cd"[b])

                pipe_stream(ctx, "wo", I["WoT"], slices, consume, post)
            sd.barrier()

        with ExitStack() as fs:
            SF = lambda n, sh, dt: fs.enter_context(nc.sbuf_tensor(n, list(sh), dt))
            xln = SF("xln", (128, ND * NT), BF16)
            emit_ln(ctx, x2d, xln, "ln2")
            h_sb = SF("h_sb", (128, NI * NT), BF16)
            usc = SF("usc", (128, NT), F32)
            gscol = SF("gscol", (128, NI), F32)
            uscol = SF("uscol", (128, NI), F32)
            dscol = SF("dscol", (128, ND), F32)
            sd.op("sp", lambda e: e.dma_start(out=gscol[:], in_=I["gs_col"]), writes=("gscol",), dma=True, lane="d")
            sd.op("sp", lambda e: e.dma_start(out=uscol[:], in_=I["us_col"]),
                  reads=("gscol",), writes=("uscol",), dma=True, lane="d")
            sd.op("sp", lambda e: e.dma_start(out=dscol[:], in_=I["ds_col"]),
                  reads=("uscol",), writes=("dscol",), dma=True, lane="d")
            if os.environ.get("L2_DBG") == "1":
                xln_dbg = nc.dram_tensor("xln_dbg", [128, ND * NT], BF16, kind="ExternalOutput").ap()
                sd.op("pool", lambda e: e.dma_start(out=xln_dbg, in_=xln[:, :]),
                      reads=tuple(f"hs{fi}" for fi in range(ND)), writes=("xlndbg",), dma=True)
            sgt = SF("sgt", (128, 4 * NT), F32)
            with ExitStack() as pse:
                pg = [pse.enter_context(nc.psum_tensor(f"pg{i}", [128, NT], F32)) for i in range(4)]
                for grp in range(NI // 4):
                    gslices = [(fi * 128, grp * 512, 512) for fi in range(ND)]

                    def gconsume(j, wb, reads, grp=grp):
                        fi = j
                        for ic in range(4):
                            for hf in range(2):
                                sd.op("pe", lambda e, fi=fi, wb=wb, ic=ic, hf=hf: e.matmul(
                                    pg[ic][:, hf * 512:(hf + 1) * 512],
                                    wb[:, ic * 128:(ic + 1) * 128],
                                    xln[:, fi * NT + hf * 512: fi * NT + (hf + 1) * 512],
                                    start=(fi == 0), stop=(fi == ND - 1)),
                                    reads=reads + (f"hs{fi}",), writes=(f"pg{ic}",))

                    def gpost(j, grp=grp):
                        if j != ND - 1:
                            return
                        for ic in range(4):
                            ich = grp * 4 + ic
                            sd.op("act", lambda e, ic=ic, ich=ich: e.activation(
                                out=sgt[:, ic * NT:(ic + 1) * NT], in_=pg[ic][:, :], func=AF.Silu,
                                scale=gscol[:, ich:ich + 1]),
                                reads=(f"pg{ic}", "gscol"), writes=("sgt",))
                        sd.fence("act", ("sgt",), ctx)

                    pipe_stream(ctx, f"gw{grp}", I["gate_wT"], gslices, gconsume, gpost)

                    def uconsume(j, wb, reads, grp=grp):
                        fi = j
                        for ic in range(4):
                            for hf in range(2):
                                sd.op("pe", lambda e, fi=fi, wb=wb, ic=ic, hf=hf: e.matmul(
                                    pg[ic][:, hf * 512:(hf + 1) * 512],
                                    wb[:, ic * 128:(ic + 1) * 128],
                                    xln[:, fi * NT + hf * 512: fi * NT + (hf + 1) * 512],
                                    start=(fi == 0), stop=(fi == ND - 1)),
                                    reads=reads + (f"hs{fi}", "sgt"), writes=(f"pg{ic}",))

                    def upost(j, grp=grp):
                        if j != ND - 1:
                            return
                        for ic in range(4):
                            ich = grp * 4 + ic
                            sd.op("act", lambda e, ic=ic, ich=ich: e.activation(
                                out=usc[:], in_=pg[ic][:, :], func=AF.Copy, scale=uscol[:, ich:ich + 1]),
                                reads=(f"pg{ic}", "uscol"), writes=("usc",))
                            sd.fence("act", ("usc",), ctx)
                            sd.op("dve", lambda e, ic=ic, ich=ich: e.tensor_mul(
                                out=h_sb[:, ich * NT:(ich + 1) * NT], in0=sgt[:, ic * NT:(ic + 1) * NT], in1=usc[:]),
                                reads=("sgt", "usc"), writes=(f"h{ich}",))
                            sd.fence("dve", (f"h{ich}",), ctx)

                    pipe_stream(ctx, f"uw{grp}", I["up_wT"], gslices, uconsume, upost)
                sd.barrier()
            if os.environ.get("L2_DBG") == "1":
                h_dbg = nc.dram_tensor("h_dbg", [128, NI * NT], BF16, kind="ExternalOutput").ap()
                sd.op("pool", lambda e: e.dma_start(out=h_dbg, in_=h_sb[:, :]),
                      reads=tuple(f"h{i}" for i in range(NI)), writes=("hdbg",), dma=True)
            x2b = [SF(f"x2b{i}", (128, NT), F32) for i in range(2)]
            yst = [SF(f"yst{i}", (128, NT), F32) for i in range(2)]
            with ExitStack() as pse:
                pd = [pse.enter_context(nc.psum_tensor(f"pd{i}", [128, NT], F32)) for i in range(4)]
                for fog in range(4):
                    dslices = [(ich * 128, fog * 512, 512) for ich in range(NI)]

                    def dconsume(j, wb, reads, fog=fog):
                        ich = j
                        for c4 in range(4):
                            for hf in range(2):
                                sd.op("pe", lambda e, ich=ich, wb=wb, c4=c4, hf=hf: e.matmul(
                                    pd[c4][:, hf * 512:(hf + 1) * 512],
                                    wb[:, c4 * 128:(c4 + 1) * 128],
                                    h_sb[:, ich * NT + hf * 512: ich * NT + (hf + 1) * 512],
                                    start=(ich == 0), stop=(ich == NI - 1)),
                                    reads=reads + (f"h{ich}",), writes=(f"pd{c4}",))

                    def dpost(j, fog=fog):
                        if j != NI - 1:
                            return
                        for c4 in range(4):
                            fo = fog * 4 + c4
                            b = fo % 2
                            sd.op("pool", lambda e, fo=fo, b=b: e.dma_start(
                                out=x2b[b][:], in_=x2d[fo * 128:(fo + 1) * 128, :]),
                                reads=(f"x2b{b}",), writes=(f"x2b{b}",), dma=True, lane="ab"[b])
                            sd.op("dve", lambda e, c4=c4, fo=fo, b=b: e.scalar_tensor_tensor(
                                out=yst[b][:], in0=pd[c4][:, :], scalar=dscol[:, fo:fo + 1], in1=x2b[b][:],
                                op0=ALU.mult, op1=ALU.add),
                                reads=(f"pd{c4}", "dscol", f"x2b{b}"), writes=(f"yst{b}",))
                            sd.fence("dve", (f"yst{b}",), ctx)
                            sd.op("pool", lambda e, fo=fo, b=b: e.dma_start(
                                out=y[fo * 128:(fo + 1) * 128, :], in_=yst[b][:]),
                                reads=(f"yst{b}",), writes=(f"yst{b}", "yout"), dma=True, lane="cd"[b])

                    pipe_stream(ctx, f"dw{fog}", I["down_wT"], dslices, dconsume, dpost)
                sd.barrier()
            sd.barrier()
        sd.barrier()
        block = es.enter_context(nc.Block())
        sd.emit(block, sems)
    return nc



_cache = {}


def _programs():
    if "l1" not in _cache:
        _cache["l1"] = build_launch1()
        _cache["l2"] = build_launch2()
    return _cache["l1"], _cache["l2"]


def _common_maps(x2d_np):
    pos = np.arange(T, dtype=np.float32)
    inv = (1.0 / 10000.0 ** (np.arange(0, DK, 2, dtype=np.float32) / DK))
    ang = pos[:, None] * inv[None, :]
    cosT = np.cos(ang).T.astype(NB)
    sinT = np.sin(ang).T.astype(NB)
    triu = np.triu(np.ones((128, 128), np.float32))
    consts = {
        "triu_bf": triu.astype(NB),
        "maskS": (triu * SCALE).astype(np.float32),
        "mask1": triu.astype(np.float32),
        "ident": np.eye(128, dtype=NB),
        "ones_bf": np.ones((128, 1), NB),
        "onesK1": np.ones((1, 128), NB),
    }
    xT = np.ascontiguousarray(x2d_np.T)
    maps = []
    for r_ in range(R):
        m = dict(consts)
        m["xT"] = np.ascontiguousarray(xT[:, r_ * NT:(r_ + 1) * NT])
        m["cosr"] = np.ascontiguousarray(cosT[:, r_ * NT:(r_ + 1) * NT])
        m["sinr"] = np.ascontiguousarray(sinT[:, r_ * NT:(r_ + 1) * NT])
        maps.append(m)
    return maps


def kernel(L0_attn_hidden_states, L0_attn_Wq, L0_attn_Wk, L0_attn_Wv, L0_attn_Wg,
           L0_attn_Wo, L0_attn_Ws1, L0_attn_Ws2, L0_attn_bs2, L0_attn_g_norm_weight,
           L0_ffn_gate_w, L0_ffn_up_w, L0_ffn_down_w, L0_ffn_gate_s, L0_ffn_up_s,
           L0_ffn_down_s):
    x = np.asarray(L0_attn_hidden_states, np.float32).reshape(T, D)
    l1, l2 = _programs()
    common = _common_maps(x)

    tr = lambda w: np.ascontiguousarray(np.asarray(w, np.float32).T)
    WqT, WkT, WvT, WgT, WoT = map(tr, (L0_attn_Wq, L0_attn_Wk, L0_attn_Wv, L0_attn_Wg, L0_attn_Wo))
    Ws1T = tr(L0_attn_Ws1)
    Ws2T = tr(L0_attn_Ws2)
    gateT, upT, downT = map(tr, (L0_ffn_gate_w, L0_ffn_up_w, L0_ffn_down_w))
    bs2r = np.asarray(L0_attn_bs2, np.float32).reshape(1, H * M)
    gnw = np.asarray(L0_attn_g_norm_weight, np.float32).reshape(128, 1)
    gs_col = np.ascontiguousarray(np.asarray(L0_ffn_gate_s, np.float32).reshape(NI, 128).T)
    us_col = np.ascontiguousarray(np.asarray(L0_ffn_up_s, np.float32).reshape(NI, 128).T)
    ds_col = np.ascontiguousarray(np.asarray(L0_ffn_down_s, np.float32).reshape(ND, 128).T)

    maps1 = []
    for r_ in range(R):
        m = dict(common[r_])
        m.update({"WqT": WqT, "WkT": WkT, "WvT": WvT, "WgT": WgT,
                  "Ws1T": Ws1T, "Ws2T": Ws2T, "bs2r": bs2r, "gnw": gnw})
        maps1.append(m)
    res1 = run_bass_kernel_spmd(l1, maps1, core_ids=list(range(R)))

    st_all = np.concatenate([res1.results[r_]["st_out"] for r_ in range(R)], axis=0)
    c_all = np.concatenate([res1.results[r_]["c_out"] for r_ in range(R)], axis=1)
    maps2 = []
    for r_ in range(R):
        m = dict(common[r_])
        pm = np.zeros((R,), np.float32)
        pm[:r_] = 1.0
        m.update({
            "WoT": WoT, "gate_wT": gateT, "up_wT": upT, "down_wT": downT,
            "Ws1T": Ws1T, "Ws2T": Ws2T, "bs2r": bs2r,
            "gs_col": gs_col, "us_col": us_col, "ds_col": ds_col,
            "q_in": res1.results[r_]["q_sp"], "k_in": res1.results[r_]["k_sp"],
            "v_in": res1.results[r_]["v_sp"], "sg_in": res1.results[r_]["sg_sp"],
            "s1_in": res1.results[r_]["s1_sp"],
            "st_all": st_all, "c_all": c_all,
            "pmask_bc": np.ascontiguousarray(np.broadcast_to(pm, (128, R))),
            "pmask_row": pm.reshape(1, R),
        })
        maps2.append(m)
    res2 = run_bass_kernel_spmd(l2, maps2, core_ids=list(range(R)))

    out = np.empty((T, D), np.float32)
    for r_ in range(R):
        out[r_ * NT:(r_ + 1) * NT, :] = res2.results[r_]["y"].T
    return out.reshape(1, T, D)


# version 09
# speedup vs baseline: 1.0411x; 1.0003x over previous
import os
import numpy as np
import ml_dtypes
import concourse.bass as bass
import concourse.mybir as mybir
from concourse.bass_utils import run_bass_kernel_spmd
from contextlib import ExitStack

F32 = mybir.dt.float32
BF16 = mybir.dt.bfloat16
AF = mybir.ActivationFunctionType
ALU = mybir.AluOpType
AX = mybir.AxisListType
NB = ml_dtypes.bfloat16

R = 8
T = 8192
NT = T // R
C = 128
NCH = NT // C
H, DK, DV, M, D = 16, 128, 128, 64, 2048
ND = D // 128
INTER = 5632
NI = INTER // 128
SCALE = DK ** -0.5
EPS = 1e-5
CLAMP = 32.0

ENGINES = ("pe", "act", "dve", "pool", "sp")
DMA_OWNER = {"sp": "dsp", "pool": "dpool", "act": "dact"}
DMA_LANES = ("a", "b", "c", "d")
ALL_SEMS = list(ENGINES) + [q + l for q in DMA_OWNER.values() for l in DMA_LANES]


class Sched:

    def __init__(self):
        self.ops = {e: [] for e in ENGINES}
        self.count = {s: 0 for s in ALL_SEMS}
        self.writer = {}
        self.readers = {}
        self.clock = {e: {} for e in ENGINES}
        self.snap = {}
        self.nwaits = 0

    def op(self, eng, fn, reads=(), writes=(), dma=False, lane="a"):
        sem = (DMA_OWNER[eng] + lane) if dma else eng
        inc = 16 if dma else 1
        if dma:
            reads = tuple(reads) + (f"_lane_{sem}",)
            writes = tuple(writes) + (f"_lane_{sem}",)
        need = {}
        for r_ in reads:
            w = self.writer.get(r_)
            if w:
                need[w[0]] = max(need.get(w[0], 0), w[1])
        for w_ in writes:
            lw = self.writer.get(w_)
            if lw:
                need[lw[0]] = max(need.get(lw[0], 0), lw[1])
            for rs, rv in self.readers.get(w_, ()):
                need[rs] = max(need.get(rs, 0), rv)
        clk = self.clock[eng]
        waits = []
        for s, v in sorted(need.items(), key=lambda kv: (kv[0] not in ENGINES, kv[0])):
            if eng == "pe" and s == "pe":
                continue
            if clk.get(s, 0) >= v:
                continue
            waits.append((s, v))
            clk[s] = v
            psnap = self.snap.get((s, v))
            if psnap:
                for s2, v2 in psnap.items():
                    if clk.get(s2, 0) < v2:
                        clk[s2] = v2
        self.nwaits += len(waits)
        self.count[sem] += inc
        cnt = self.count[sem]
        snap_clk = dict(clk)
        snap_clk[sem] = max(snap_clk.get(sem, 0), cnt)
        self.snap[(sem, cnt)] = snap_clk
        self.ops[eng].append((waits, fn, sem, inc))
        key = (sem, cnt)
        for w_ in writes:
            self.writer[w_] = key
            self.readers[w_] = []
        for r_ in reads:
            self.readers.setdefault(r_, []).append(key)
        return key

    def fence(self, eng, names, ctx):
        t = ctx["t"]
        if eng == "dve":
            fn = lambda e: e.tensor_copy(out=t["scr"][0:1, 0:1], in_=t["scr"][0:1, 1:2])
        else:
            fn = lambda e: e.activation(out=t["scr"][0:1, 0:1], in_=t["scr"][0:1, 1:2], func=AF.Copy)
        self.op(eng, fn, reads=(), writes=())
        self.op(eng, fn, reads=(), writes=tuple(names))

    def barrier(self):
        vals = {s: v for s, v in self.count.items() if v > 0}
        for e in ENGINES:
            clk = self.clock[e]
            waits = [(s, v) for s, v in sorted(vals.items()) if clk.get(s, 0) < v]
            for s, v in waits:
                clk[s] = v
            if waits:
                self.ops[e].append((waits, None, None, 0))
            self.nwaits += len(waits)

    def emit(self, block, sems):
        def runner(eng_obj, name):
            for waits, fn, sem, inc in self.ops[name]:
                for s, v in waits:
                    eng_obj.wait_ge(sems[s], v)
                if fn is not None:
                    fn(eng_obj).then_inc(sems[sem], inc)

        @block.tensor
        def _(e):
            runner(e, "pe")

        @block.scalar
        def _(e):
            runner(e, "act")

        @block.vector
        def _(e):
            runner(e, "dve")

        @block.gpsimd
        def _(e):
            runner(e, "pool")

        @block.sync
        def _(e):
            runner(e, "sp")


def bcast(ap, shape):
    return ap.broadcast_to(list(shape))


def mmN(sd, psum_ap, lhsT_ap_fn, rhs_ap, n_total, reads, wr, start, stop):
    o = 0
    while o < n_total:
        n = min(512, n_total - o)
        sd.op("pe",
              (lambda e, o=o, n=n: e.matmul(psum_ap[:, o:o + n], lhsT_ap_fn(), rhs_ap[:, o:o + n],
                                            start=start, stop=stop)),
              reads=reads, writes=(wr,))
        o += n



def declare_inputs(nc, launch2):
    ap = {}

    def di(name, shape, dt=F32):
        ap[name] = nc.dram_tensor(name, list(shape), dt, kind="ExternalInput").ap()

    di("xT", (D, NT))
    di("Ws1T", (D, 16))
    di("Ws2T", (16, H * M))
    di("bs2r", (1, H * M))
    di("cosr", (64, NT), BF16)
    di("sinr", (64, NT), BF16)
    di("triu_bf", (128, 128), BF16)
    di("maskS", (128, 128))
    di("mask1", (128, 128))
    di("ident", (128, 128), BF16)
    di("ones_bf", (128, 1), BF16)
    di("onesK1", (1, 128), BF16)
    if not launch2:
        for w in ("WqT", "WkT", "WvT", "WgT"):
            di(w, (D, D))
        di("gnw", (128, 1))
    else:
        di("WoT", (D, D))
        di("gate_wT", (D, INTER))
        di("up_wT", (D, INTER))
        di("down_wT", (INTER, D))
        di("gs_col", (128, NI))
        di("us_col", (128, NI))
        di("ds_col", (128, ND))
        di("q_in", (128, H * NT), BF16)
        di("k_in", (128, H * NT), BF16)
        di("v_in", (128, NCH * D), BF16)
        di("sg_in", (128, NCH * D), BF16)
        di("s1_in", (16, NT), BF16)
        di("st_all", (R * 128, 2048))
        di("c_all", (1, R * H * M))
        di("pmask_bc", (128, R))
        di("pmask_row", (1, R))
    return ap


def setup_consts(ctx):
    nc, sd, es, I = ctx["nc"], ctx["sd"], ctx["es"], ctx["in"]
    S = lambda n, sh, dt: es.enter_context(nc.sbuf_tensor(n, list(sh), dt))
    t = {}
    t["cos"] = S("c_cos", (64, NT), BF16)
    t["sin"] = S("c_sin", (64, NT), BF16)
    t["triu"] = S("c_triu", (128, 128), BF16)
    t["maskS"] = S("c_maskS", (128, 128), F32)
    t["mask1"] = S("c_mask1", (128, 128), F32)
    t["ident"] = S("c_ident", (128, 128), BF16)
    t["ones"] = S("c_ones", (128, 1), BF16)
    t["onesK1"] = S("c_onesK1", (1, 128), BF16)
    t["eps"] = S("c_eps", (128, 1), F32)
    t["bs2bc"] = S("c_bs2bc", (128, H * M), F32)
    t["ws2"] = S("c_ws2", (16, H * M), BF16)
    t["scr"] = S("c_scr", (1, 4), F32)
    t["stg32"] = S("c_stg32", (128, H * M), F32)
    ctx["t"] = t
    prev = ()
    for n, src in (("cos", "cosr"), ("sin", "sinr"), ("triu", "triu_bf"), ("maskS", "maskS"),
                   ("mask1", "mask1"), ("ident", "ident"), ("ones", "ones_bf"), ("onesK1", "onesK1")):
        sd.op("sp", (lambda e, n=n, src=src: e.dma_start(out=t[n][:], in_=I[src])),
              reads=prev, writes=("c_" + n,), dma=True, lane="d")
        prev = ("c_" + n,)
    sd.op("dve", lambda e: e.memset(t["eps"][:], EPS), writes=("c_eps",))
    sd.op("dve", lambda e: e.memset(t["scr"][:], 0.0), writes=("c_scr",))
    sd.op("sp", lambda e: e.dma_start(out=t["bs2bc"][:], in_=I["bs2r"].broadcast_to([128, H * M])),
          reads=prev, writes=("c_bs2bc",), dma=True, lane="d")
    sd.op("sp", lambda e: e.dma_start(out=t["stg32"][0:16, :], in_=I["Ws2T"]),
          reads=("c_bs2bc",), writes=("c_stg32",), dma=True, lane="d")
    sd.op("dve", lambda e: e.tensor_copy(out=t["ws2"][:], in_=t["stg32"][0:16, :]),
          reads=("c_stg32",), writes=("c_ws2",))


def emit_ln(ctx, x_dram_ap, hs, pref):
    nc, sd, t = ctx["nc"], ctx["sd"], ctx["t"]
    rowd = nc.dram_tensor(pref + "_rowd", [2, NT], F32, kind="Internal").ap()
    with ExitStack() as pse:
        es = pse
        xa = [es.enter_context(nc.sbuf_tensor(f"{pref}_x{i}", [128, NT], F32)) for i in range(2)]
        xb = [es.enter_context(nc.sbuf_tensor(f"{pref}_xb{i}", [128, NT], BF16)) for i in range(2)]
        sq = xb
        stm = es.enter_context(nc.sbuf_tensor(pref + "_stm", [1, NT], F32))
        stv = es.enter_context(nc.sbuf_tensor(pref + "_stv", [1, NT], F32))
        mb = es.enter_context(nc.sbuf_tensor(pref + "_mb", [128, NT], F32))
        m2 = mb
        rb = es.enter_context(nc.sbuf_tensor(pref + "_rb", [128, NT], F32))
        ps = [pse.enter_context(nc.psum_tensor(f"{pref}_p{i}", [1, 512], F32)) for i in range(4)]
        for fi in range(ND):
            b = fi % 2
            xr, xbr, sqr = f"{pref}x{b}", f"{pref}xb{b}", f"{pref}sq{b}"
            sd.op("sp", lambda e, fi=fi, b=b: e.dma_start(out=xa[b][:], in_=x_dram_ap[fi * 128:(fi + 1) * 128, :]),
                  reads=(xr,), writes=(xr,), dma=True)
            sd.op("dve", lambda e, b=b: e.tensor_copy(out=xb[b][:], in_=xa[b][:]),
                  reads=(xr,), writes=(xbr, sqr))
            for hf in range(2):
                sl = slice(hf * 512, (hf + 1) * 512)
                sd.op("pe", lambda e, fi=fi, b=b, hf=hf, sl=sl: e.matmul(
                    ps[hf][:, :], t["ones"][:, 0:1], xb[b][:, sl], start=(fi == 0), stop=(fi == ND - 1)),
                    reads=(xbr, "c_ones"), writes=(f"{pref}pm{hf}",))
            sd.op("act", lambda e, b=b: e.activation(out=sq[b][:], in_=xa[b][:], func=AF.Square),
                  reads=(xr,), writes=(sqr, xbr))
            for hf in range(2):
                sl = slice(hf * 512, (hf + 1) * 512)
                sd.op("pe", lambda e, fi=fi, b=b, hf=hf, sl=sl: e.matmul(
                    ps[2 + hf][:, :], t["ones"][:, 0:1], sq[b][:, sl], start=(fi == 0), stop=(fi == ND - 1)),
                    reads=(sqr, "c_ones"), writes=(f"{pref}pq{hf}",))
        for hf in range(2):
            sl = slice(hf * 512, (hf + 1) * 512)
            sd.op("dve", lambda e, hf=hf, sl=sl: e.tensor_scalar(
                out=stm[0:1, sl], in0=ps[hf][:, :], scalar1=1.0 / D, scalar2=None, op0=ALU.mult),
                reads=(f"{pref}pm{hf}",), writes=(f"{pref}mrow",))
            sd.op("dve", lambda e, hf=hf, sl=sl: e.tensor_scalar(
                out=stv[0:1, sl], in0=ps[2 + hf][:, :], scalar1=1.0 / D, scalar2=None, op0=ALU.mult),
                reads=(f"{pref}pq{hf}",), writes=(f"{pref}vrow",))
        sd.op("dve", lambda e: e.tensor_mul(out=m2[0:1, :], in0=stm[0:1, :], in1=stm[0:1, :]),
              reads=(f"{pref}mrow", f"{pref}mb"), writes=(f"{pref}m2",))
        sd.op("dve", lambda e: e.tensor_sub(out=stv[0:1, :], in0=stv[0:1, :], in1=m2[0:1, :]),
              reads=(f"{pref}vrow", f"{pref}m2"), writes=(f"{pref}vrow",))
        sd.fence("dve", (f"{pref}vrow", f"{pref}mrow"), ctx)
        sd.op("act", lambda e: e.activation(out=m2[0:1, :], in_=stv[0:1, :], func=AF.Ln, bias=t["eps"][0:1, 0:1]),
              reads=(f"{pref}vrow", "c_eps", f"{pref}m2"), writes=(f"{pref}m2",))
        sd.op("act", lambda e: e.activation(out=stv[0:1, :], in_=m2[0:1, :], func=AF.Exp, scale=-0.5),
              reads=(f"{pref}m2",), writes=(f"{pref}vrow",))
        sd.fence("act", (f"{pref}vrow",), ctx)
        sd.op("pool", lambda e: e.dma_start(out=rowd[0:1, :], in_=stm[:, :]),
              reads=(f"{pref}mrow",), writes=(f"{pref}rowd",), dma=True)
        sd.op("pool", lambda e: e.dma_start(out=rowd[1:2, :], in_=stv[:, :]),
              reads=(f"{pref}vrow",), writes=(f"{pref}rowd",), dma=True)
        sd.op("pool", lambda e: e.dma_start(out=mb[:], in_=rowd[0:1, :].broadcast_to([128, NT])),
              reads=(f"{pref}rowd", f"{pref}m2"), writes=(f"{pref}mb",), dma=True)
        sd.op("pool", lambda e: e.dma_start(out=rb[:], in_=rowd[1:2, :].broadcast_to([128, NT])),
              reads=(f"{pref}rowd",), writes=(f"{pref}rb",), dma=True, lane="b")
        if os.environ.get("L1_DBG_HS") == "1":
            st_dbg = nc.dram_tensor(pref + "_stdbg", [4, NT], F32, kind="ExternalOutput").ap()
            sd.op("pool", lambda e: e.dma_start(out=st_dbg[0:1, :], in_=stm[:, :]),
                  reads=(f"{pref}mrow",), writes=("stdbg",), dma=True)
            sd.op("pool", lambda e: e.dma_start(out=st_dbg[1:2, :], in_=stv[:, :]),
                  reads=(f"{pref}vrow",), writes=("stdbg",), dma=True)
            sd.op("pool", lambda e: e.dma_start(out=st_dbg[2:3, :], in_=mb[0:1, :]),
                  reads=(f"{pref}mb",), writes=("stdbg",), dma=True)
            sd.op("pool", lambda e: e.dma_start(out=st_dbg[3:4, :], in_=rb[0:1, :]),
                  reads=(f"{pref}rb",), writes=("stdbg",), dma=True)
        for fi in range(ND):
            b = fi % 2
            xr = f"{pref}x{b}"
            sd.op("sp", lambda e, fi=fi, b=b: e.dma_start(out=xa[b][:], in_=x_dram_ap[fi * 128:(fi + 1) * 128, :]),
                  reads=(xr,), writes=(xr,), dma=True, lane="ab"[b])
            sd.op("dve", lambda e, b=b: e.tensor_sub(out=xa[b][:], in0=xa[b][:], in1=mb[:]),
                  reads=(xr, f"{pref}mb"), writes=(xr,))
            sd.op("dve", lambda e, fi=fi, b=b: e.tensor_mul(out=hs[:, fi * NT:(fi + 1) * NT], in0=xa[b][:], in1=rb[:]),
                  reads=(xr, f"{pref}rb"), writes=(f"hs{fi}",))
            sd.fence("dve", (f"hs{fi}",), ctx)
        sd.barrier()




def pipe_stream(ctx, pref, w_dram, slices, consume, post=None):
    nc, sd = ctx["nc"], ctx["sd"]
    W = slices[0][2]
    with ExitStack() as pse:
        wsa = [pse.enter_context(nc.sbuf_tensor(f"{pref}_wa{i}", [128, W], F32)) for i in range(2)]
        wsb = [pse.enter_context(nc.sbuf_tensor(f"{pref}_wb{i}", [128, W], BF16)) for i in range(2)]
        n = len(slices)

        def _dma(i):
            b = i % 2
            ra = f"{pref}wa{b}"
            r0, c0, _ = slices[i]
            sd.op("sp", lambda e, r0=r0, c0=c0, b=b: e.dma_start(
                out=wsa[b][:], in_=w_dram[r0:r0 + 128, c0:c0 + W]),
                reads=(ra,), writes=(ra,), dma=True, lane="ab"[b])

        def _consume(j, gate):
            b = j % 2
            ra, rb = f"{pref}wa{b}", f"{pref}wb{b}"
            sd.op("dve", lambda e, b=b: e.tensor_copy(out=wsb[b][:], in_=wsa[b][:]),
                  reads=(ra,), writes=(rb,))
            sd.fence("dve", (rb,), ctx)
            consume(j, wsb[b], (rb, gate))
            if post:
                post(j)

        for i in range(n):
            _dma(i)
            if i > 0:
                _consume(i - 1, f"{pref}wa{i % 2}")
        bg = n % 2
        sd.op("sp", lambda e, bg=bg: e.dma_start(out=wsa[bg][0:1, 0:8], in_=w_dram[0:1, 0:8]),
              reads=(f"{pref}wa{bg}",), writes=(f"{pref}wa{bg}",), dma=True, lane="ab"[bg])
        _consume(n - 1, f"{pref}wa{bg}")
        sd.barrier()

def emit_weight_stream_mm(ctx, wname, w_dram, hs, dst_fn, n_fo, pref, grp=4, nbuf=None):
    nc, sd = ctx["nc"], ctx["sd"]
    W = grp * 128
    with ExitStack() as pse:
        pp = [pse.enter_context(nc.psum_tensor(f"{pref}_pp{i}", [128, NT], F32)) for i in range(grp)]
        slices = []
        for fog in range(n_fo // grp):
            for fi in range(ND):
                slices.append((fi * 128, fog * W, W))

        def consume(j, wb, reads):
            fi = j % ND
            for c4 in range(grp):
                for hf in range(2):
                    sd.op("pe", lambda e, fi=fi, wb=wb, c4=c4, hf=hf: e.matmul(
                        pp[c4][:, hf * 512:(hf + 1) * 512],
                        wb[:, c4 * 128:(c4 + 1) * 128],
                        hs[:, fi * NT + hf * 512: fi * NT + (hf + 1) * 512],
                        start=(fi == 0), stop=(fi == ND - 1)),
                        reads=reads + (f"hs{fi}",), writes=(f"{pref}pp{c4}",))

        def post(j):
            if j % ND == ND - 1:
                fog = j // ND
                for c4 in range(grp):
                    dst_fn(fog * grp + c4, pp[c4], f"{pref}pp{c4}")

        pipe_stream(ctx, pref, w_dram, slices, consume, post)


def emit_rope(ctx, q_sb, name, sc):
    nc, sd, t = ctx["nc"], ctx["sd"], ctx["t"]
    NGH = 2
    cosv = t["cos"][:, :].unsqueeze(1).broadcast_to([64, NGH, NT])
    sinv = t["sin"][:, :].unsqueeze(1).broadcast_to([64, NGH, NT])
    for g in range(H // NGH):
        heads = tuple(f"{name}{h}" for h in range(NGH * g, NGH * g + NGH))
        x1 = q_sb[0:64, g * NGH * NT:(g + 1) * NGH * NT].rearrange("p (h t) -> p h t", h=NGH)
        x2 = q_sb[64:128, g * NGH * NT:(g + 1) * NGH * NT].rearrange("p (h t) -> p h t", h=NGH)
        v0 = sc[0][:, :].rearrange("p (h t) -> p h t", h=NGH)
        v1 = sc[1][:, :].rearrange("p (h t) -> p h t", h=NGH)
        v2 = sc[2][:, :].rearrange("p (h t) -> p h t", h=NGH)
        v3 = sc[3][:, :].rearrange("p (h t) -> p h t", h=NGH)
        scr = ("rrs0", "rrs1", "rrs2", "rrs3")
        sd.op("dve", lambda e, x2=x2, v3=v3: e.tensor_copy(out=v3, in_=x2),
              reads=heads, writes=(scr[3],))
        sd.op("dve", lambda e, x1=x1, v0=v0: e.tensor_mul(out=v0, in0=x1, in1=sinv),
              reads=heads + ("c_cos", "c_sin"), writes=(scr[0],))
        sd.op("dve", lambda e, x1=x1, v1=v1: e.tensor_mul(out=v1, in0=x1, in1=cosv),
              reads=heads, writes=(scr[1],))
        sd.op("dve", lambda e, v2=v2, v3=v3: e.tensor_mul(out=v2, in0=v3, in1=sinv),
              reads=(scr[3],), writes=(scr[2],))
        sd.op("dve", lambda e, x1=x1, v1=v1, v2=v2: e.tensor_sub(out=x1, in0=v1, in1=v2),
              reads=(scr[1], scr[2]), writes=heads)
        sd.op("dve", lambda e, v1=v1, v3=v3: e.tensor_mul(out=v1, in0=v3, in1=cosv),
              reads=(scr[3],), writes=(scr[1],))
        sd.op("dve", lambda e, v1=v1, v0=v0, v2=v2: e.tensor_add(out=v2, in0=v1, in1=v0),
              reads=(scr[0], scr[1]), writes=(scr[2],))
        sd.op("dve", lambda e, x2=x2, v2=v2: e.tensor_copy(out=x2, in_=v2),
              reads=(scr[2],), writes=heads)
        sd.fence("dve", heads, ctx)


def emit_es(ctx, ci, s1_sb, es_tm, esname, scratch, psname, ps_half, scname=None):
    nc, sd, t = ctx["nc"], ctx["sd"], ctx["t"]
    scname = scname or (esname + "sc")
    for hf in range(2):
        sl = slice(hf * 512, (hf + 1) * 512)
        sd.op("pe", lambda e, ci=ci, sl=sl: e.matmul(
            ps_half[:, :], s1_sb[:, ci * C:(ci + 1) * C], t["ws2"][:, sl], start=True, stop=True),
            reads=("s1", "c_ws2"), writes=(psname,))
        sd.op("dve", lambda e, sl=sl: e.tensor_add(out=scratch[:, 0:512], in0=ps_half[:, :], in1=t["bs2bc"][:, sl]),
              reads=(psname, "c_bs2bc"), writes=(scname,))
        sd.op("dve", lambda e: e.tensor_scalar(out=scratch[:, 0:512], in0=scratch[:, 0:512],
                                               scalar1=CLAMP, scalar2=-CLAMP, op0=ALU.min, op1=ALU.max),
              reads=(scname,), writes=(scname,))
        sd.op("act", lambda e, sl=sl: e.activation(out=es_tm[:, sl], in_=scratch[:, 0:512], func=AF.Exp),
              reads=(scname,), writes=(esname,))
        sd.fence("act", (esname,), ctx)


def emit_ktm(ctx, ci, k_sb, k_tm, pbk, pbkname):
    nc, sd, t = ctx["nc"], ctx["sd"], ctx["t"]
    for rnd in range(2):
        for hh in range(8):
            h = rnd * 8 + hh
            sd.op("pe", lambda e, h=h, hh=hh, ci=ci: e.transpose(
                pbk[:, hh * 128:(hh + 1) * 128], k_sb[:, h * NT + ci * C: h * NT + (ci + 1) * C], t["ident"][:, :]),
                reads=(f"k{h}", "c_ident"), writes=(pbkname,))
        sd.op("dve", lambda e, rnd=rnd: e.tensor_copy(
            out=k_tm[:, rnd * 1024:(rnd + 1) * 1024], in_=pbk[:, :]),
            reads=(pbkname,), writes=("ktm",))
        sd.fence("dve", ("ktm",), ctx)



def build_launch1():
    nc = bass.Bass("TRN2", target_bir_lowering=False, debug=False)
    I = declare_inputs(nc, launch2=False)
    O = {}
    for n, sh, dt in (("q_sp", (128, H * NT), BF16), ("k_sp", (128, H * NT), BF16),
                      ("v_sp", (128, NCH * D), BF16), ("sg_sp", (128, NCH * D), BF16),
                      ("s1_sp", (16, NT), BF16), ("st_out", (128, 2048), F32),
                      ("c_out", (1, H * M), F32)):
        O[n] = nc.dram_tensor(n, list(sh), dt, kind="ExternalOutput").ap()

    es = ExitStack()
    with es:
        sems = {s: es.enter_context(nc.semaphore(s)) for s in ALL_SEMS}
        sd = Sched()
        ctx = {"nc": nc, "sd": sd, "es": es, "in": I}
        setup_consts(ctx)
        t = ctx["t"]
        Sb = lambda n, sh, dt: es.enter_context(nc.sbuf_tensor(n, list(sh), dt))
        hs = Sb("hs", (128, ND * NT), BF16)
        q_sb = Sb("q_sb", (128, H * NT), BF16)
        k_sb = Sb("k_sb", (128, H * NT), BF16)
        v_tm = Sb("v_tm", (128, NCH * D), BF16)
        sg_tm = Sb("sg_tm", (128, NCH * D), BF16)
        s1_sb = Sb("s1_sb", (16, NT), BF16)
        gnw_sb = Sb("gnw_sb", (128, 1), F32)
        ws1b = Sb("ws1b", (128, ND * 16), BF16)

        sd.op("sp", lambda e: e.dma_start(out=gnw_sb[:], in_=I["gnw"]),
              reads=("c_stg32",), writes=("gnw",), dma=True, lane="d")
        sd.op("sp", lambda e: e.dma_start(
            out=t["stg32"][:, 0:ND * 16].rearrange("p (c s) -> p c s", s=16),
            in_=I["Ws1T"].rearrange("(c p) s -> p c s", p=128)),
            reads=("c_stg32", "gnw"), writes=("c_stg32",), dma=True, lane="d")
        sd.op("dve", lambda e: e.tensor_copy(out=ws1b[:], in_=t["stg32"][:, 0:ND * 16]),
              reads=("c_stg32",), writes=("ws1b",))

        emit_ln(ctx, I["xT"], hs, "ln1")
        if os.environ.get("L1_DBG_HS") == "1":
            hs_dbg = nc.dram_tensor("hs_dbg", [128, ND * NT], BF16, kind="ExternalOutput").ap()
            sd.op("pool", lambda e: e.dma_start(out=hs_dbg, in_=hs[:, :]),
                  reads=tuple(f"hs{fi}" for fi in range(ND)), writes=("hsdbg",), dma=True)

        def q_evac(fo, ppt, ppr):
            sd.op("dve", lambda e, fo=fo: e.tensor_copy(out=q_sb[:, fo * NT:(fo + 1) * NT], in_=ppt[:, :]),
                  reads=(ppr,), writes=(f"q{fo}",))
            sd.fence("dve", (f"q{fo}",), ctx)

        def k_evac(fo, ppt, ppr):
            sd.op("act", lambda e, fo=fo: e.activation(out=k_sb[:, fo * NT:(fo + 1) * NT], in_=ppt[:, :], func=AF.Copy),
                  reads=(ppr,), writes=(f"k{fo}",))
            sd.fence("act", (f"k{fo}",), ctx)

        def v_evac(fo, ppt, ppr):
            sd.op("dve", lambda e, ppt=ppt: e.tensor_copy(out=fmbf[:], in_=ppt[:, :]), reads=(ppr,), writes=("fmbf",))
            sd.fence("dve", ("fmbf",), ctx)
            for ct in range(NCH):
                sd.op("pe", lambda e, ct=ct: e.transpose(
                    vgps[:, ct * 128:(ct + 1) * 128], fmbf[:, ct * C:(ct + 1) * C], t["ident"][:, :]),
                    reads=("fmbf", "c_ident"), writes=("vgps",))
            dst = v_tm[:, :].rearrange("p (ct r) -> p ct r", r=D)[:, :, fo * 128:(fo + 1) * 128]
            sd.op("dve", lambda e, dst=dst: e.tensor_copy(out=dst, in_=vgps[:, :].rearrange("p (ct c) -> p ct c", c=128)),
                  reads=("vgps",), writes=(f"v{fo}",))
            sd.fence("dve", (f"v{fo}",), ctx)

        def g_evac(fo, ppt, ppr):
            sd.op("act", lambda e, ppt=ppt: e.activation(out=fmsc[:], in_=ppt[:, :], func=AF.Silu),
                  reads=(ppr,), writes=("fmsc",))
            sd.op("dve", lambda e: e.tensor_mul(out=fmbf[:], in0=fmsc[:], in1=bcast(gnw_sb[:, 0:1], (128, NT))),
                  reads=("fmsc", "gnw"), writes=("fmbf",))
            sd.fence("dve", ("fmbf",), ctx)
            for ct in range(NCH):
                sd.op("pe", lambda e, ct=ct: e.transpose(
                    vgps[:, ct * 128:(ct + 1) * 128], fmbf[:, ct * C:(ct + 1) * C], t["ident"][:, :]),
                    reads=("fmbf", "c_ident"), writes=("vgps",))
            dst = sg_tm[:, :].rearrange("p (ct r) -> p ct r", r=D)[:, :, fo * 128:(fo + 1) * 128]
            sd.op("dve", lambda e, dst=dst: e.tensor_copy(out=dst, in_=vgps[:, :].rearrange("p (ct c) -> p ct c", c=128)),
                  reads=("vgps",), writes=(f"sg{fo}",))
            sd.fence("dve", (f"sg{fo}",), ctx)

        emit_weight_stream_mm(ctx, "q", I["WqT"], hs, q_evac, H, "pq", grp=4)
        emit_weight_stream_mm(ctx, "k", I["WkT"], hs, k_evac, H, "pk", grp=4)
        with ExitStack() as vgs:
            vgps = vgs.enter_context(nc.psum_tensor("vgps", [128, 1024], BF16))
            fmsc = vgs.enter_context(nc.sbuf_tensor("fmsc", [128, NT], F32))
            fmbf = vgs.enter_context(nc.sbuf_tensor("fmbf", [128, NT], BF16))
            emit_weight_stream_mm(ctx, "v", I["WvT"], hs, v_evac, H, "pv", grp=2, nbuf=1)
            emit_weight_stream_mm(ctx, "g", I["WgT"], hs, g_evac, H, "pg", grp=2, nbuf=1)

        with ExitStack() as pse:
            p16 = [pse.enter_context(nc.psum_tensor(f"s1p{i}", [16, 512], F32)) for i in range(2)]
            for fi in range(ND):
                for hf in range(2):
                    sd.op("pe", lambda e, fi=fi, hf=hf: e.matmul(
                        p16[hf][:, :], ws1b[:, fi * 16:(fi + 1) * 16],
                        hs[:, fi * NT + hf * 512: fi * NT + (hf + 1) * 512],
                        start=(fi == 0), stop=(fi == ND - 1)),
                        reads=("ws1b", f"hs{fi}"), writes=(f"s1p{hf}",))
            for hf in range(2):
                sd.op("dve", lambda e, hf=hf: e.tensor_copy(out=s1_sb[:, hf * 512:(hf + 1) * 512], in_=p16[hf][:, :]),
                      reads=(f"s1p{hf}",), writes=("s1",))
            sd.fence("dve", ("s1",), ctx)
            sd.barrier()

        with ExitStack() as rsc:
            rope_sc = [rsc.enter_context(nc.sbuf_tensor(f"rope_s{i}", [64, 2 * NT], BF16)) for i in range(4)]
            emit_rope(ctx, q_sb, "q", rope_sc)
            emit_rope(ctx, k_sb, "k", rope_sc)
            sd.barrier()

        es_tm = Sb("es_tm", (128, H * M), BF16)
        es_sc = Sb("es_sc", (128, 512), F32)
        k_tm = Sb("k_tm", (128, H * 128), BF16)
        stage = Sb("stage", (128, 2048), F32)
        cstage = Sb("cstage", (1, H * M), F32)
        sd.op("dve", lambda e: e.memset(stage[:], 0.0), writes=("stage",))
        sd.op("dve", lambda e: e.memset(cstage[:], 0.0), writes=("cst",))
        with ExitStack() as pse:
            pS = pse.enter_context(nc.psum_tensor("pS", [128, 1024], F32))
            pZ = pse.enter_context(nc.psum_tensor("pZ", [128, 1024], F32))
            pC = pse.enter_context(nc.psum_tensor("pC", [1, 512], F32))
            pEs = pse.enter_context(nc.psum_tensor("pEs", [128, 512], F32))
            pBk = pse.enter_context(nc.psum_tensor("pBk", [128, 1024], BF16))
            for ci in range(NCH):
                emit_es(ctx, ci, s1_sb, es_tm, "es", es_sc, "pEs", pEs)
                emit_ktm(ctx, ci, k_sb, k_tm, pBk, "pBk")
                for h in range(H):
                    sd.op("pe", lambda e, h=h: e.matmul(
                        pS[:, h * M:(h + 1) * M], k_tm[:, h * 128:(h + 1) * 128],
                        es_tm[:, h * M:(h + 1) * M], start=True, stop=True),
                        reads=("ktm", "es"), writes=("pS",))
                sd.op("dve", lambda e: e.tensor_add(out=stage[:, 0:1024], in0=stage[:, 0:1024], in1=pS[:, :]),
                      reads=("pS", "stage"), writes=("stage",))
                for h in range(H):
                    sd.op("pe", lambda e, h=h, ci=ci: e.matmul(
                        pZ[64 * (h % 2):64 * (h % 2) + 64, (h // 2) * 128:(h // 2) * 128 + 128],
                        es_tm[:, h * M:(h + 1) * M],
                        v_tm[:, ci * D + h * 128: ci * D + (h + 1) * 128], start=True, stop=True),
                        reads=("es", f"v{h}"), writes=("pZ",))
                sd.op("dve", lambda e: e.tensor_add(out=stage[:, 1024:2048], in0=stage[:, 1024:2048], in1=pZ[:, :]),
                      reads=("pZ", "stage"), writes=("stage",))
                sd.fence("dve", ("stage",), ctx)
                for hf in range(2):
                    sd.op("pe", lambda e, hf=hf: e.matmul(
                        pC[:, :], t["ones"][:, 0:1], es_tm[:, hf * 512:(hf + 1) * 512], start=True, stop=True),
                        reads=("es", "c_ones"), writes=("pC",))
                    sd.op("dve", lambda e, hf=hf: e.tensor_add(
                        out=cstage[0:1, hf * 512:(hf + 1) * 512], in0=cstage[0:1, hf * 512:(hf + 1) * 512], in1=pC[:, :]),
                        reads=("pC", "cst"), writes=("cst",))
                    sd.fence("dve", ("cst",), ctx)
            sd.barrier()

        sd.op("pool", lambda e: e.dma_start(out=O["st_out"], in_=stage[:, :]), reads=("stage",), writes=("o1",), dma=True, lane="c")
        sd.op("pool", lambda e: e.dma_start(out=O["c_out"], in_=cstage[:, :]), reads=("cst", "o1"), writes=("o2",), dma=True, lane="c")
        sd.op("pool", lambda e: e.dma_start(out=O["q_sp"], in_=q_sb[:, :]),
              reads=tuple(f"q{h}" for h in range(H)) + ("o2",), writes=("o3",), dma=True, lane="c")
        sd.op("pool", lambda e: e.dma_start(out=O["k_sp"], in_=k_sb[:, :]),
              reads=tuple(f"k{h}" for h in range(H)) + ("o3",), writes=("o4",), dma=True, lane="c")
        sd.op("pool", lambda e: e.dma_start(out=O["v_sp"], in_=v_tm[:, :]),
              reads=tuple(f"v{h}" for h in range(H)) + ("o4",), writes=("o5",), dma=True, lane="c")
        sd.op("pool", lambda e: e.dma_start(out=O["sg_sp"], in_=sg_tm[:, :]),
              reads=tuple(f"sg{h}" for h in range(H)) + ("o5",), writes=("o6",), dma=True, lane="c")
        sd.op("pool", lambda e: e.dma_start(out=O["s1_sp"], in_=s1_sb[:, :]), reads=("s1", "o6"), writes=("o7",), dma=True, lane="c")
        sd.barrier()
        block = es.enter_context(nc.Block())
        sd.emit(block, sems)
    return nc



def build_launch2(nch=NCH, wo=True, ffn=True, stage=9):
    nc = bass.Bass("TRN2", target_bir_lowering=False, debug=False)
    I = declare_inputs(nc, launch2=True)
    y = nc.dram_tensor("y", [D, NT], F32, kind="ExternalOutput").ap()
    if os.environ.get("L2_DBG") == "1":
        ogd = nc.dram_tensor("ogd", [D, NT], BF16, kind="ExternalOutput").ap()
        x2d = nc.dram_tensor("x2d", [D, NT], F32, kind="ExternalOutput").ap()
    else:
        ogd = nc.dram_tensor("ogd", [D, NT], BF16, kind="Internal").ap()
        x2d = nc.dram_tensor("x2d", [D, NT], F32, kind="Internal").ap()

    es = ExitStack()
    with es:
        sems = {s: es.enter_context(nc.semaphore(s)) for s in ALL_SEMS}
        sd = Sched()
        ctx = {"nc": nc, "sd": sd, "es": es, "in": I}
        setup_consts(ctx)
        t = ctx["t"]
        Sb = lambda n, sh, dt: es.enter_context(nc.sbuf_tensor(n, list(sh), dt))

        with ExitStack() as p2s:
            S2 = lambda n, sh, dt: p2s.enter_context(nc.sbuf_tensor(n, list(sh), dt))
            q_sb = S2("q_sb", (128, H * NT), BF16)
            k_sb = S2("k_sb", (128, H * NT), BF16)
            v_tm = S2("v_tm", (128, NCH * D), BF16)
            sg_tm = S2("sg_tm", (128, NCH * D), BF16)
            s1_sb = S2("s1_sb", (16, NT), BF16)
            SZ = S2("SZ", (128, 2048), F32)
            Sbf = S2("Sbf", (128, 1024), BF16)
            Zbf = S2("Zbf", (128, 2 * 1024), BF16)
            csb = S2("csb", (1, H * M), F32)
            cbf = S2("cbf", (1, H * M), BF16)
            stg = [S2("stg0", (128, 2048), F32)]
            cstg = None
            pmb = S2("pmb", (128, R), F32)
            pmr = S2("pmr", (1, R), F32)
            es_tm = S2("es_tm", (128, H * M), BF16)
            es_fm = S2("es_fm", (128, 2 * 1024), BF16)
            k_tm = S2("k_tm", (128, H * 128), BF16)
            ab_sb = S2("ab_sb", (128, H * 128), BF16)
            w_tm = S2("w_tm", (128, H * M), BF16)
            w_fm = S2("w_fm", (128, 2 * 1024), BF16)
            enz = S2("enz", (128, H * M), F32)
            es_sc = enz[:, 0:512]
            smx = S2("smx", (128, 4 * H), F32)
            bigsc = S2("bigsc", (128, D), F32)
            sqv = bigsc
            t1s = bigsc
            okn = bigsc[:, 0:H * M]
            e_sb = bigsc[:, H * M:2 * H * M]
            og_tm = ab_sb

            prevld = ("c_stg32",)
            for n_, src in (("q_sb", "q_in"), ("k_sb", "k_in"), ("v_tm", "v_in"), ("sg_tm", "sg_in")):
                tt = {"q_sb": q_sb, "k_sb": k_sb, "v_tm": v_tm, "sg_tm": sg_tm}[n_]
                wr = {"q_sb": tuple(f"q{h}" for h in range(H)), "k_sb": tuple(f"k{h}" for h in range(H)),
                      "v_tm": tuple(f"v{h}" for h in range(H)), "sg_tm": ("sg",)}[n_]
                sd.op("sp", lambda e, tt=tt, src=src: e.dma_start(out=tt[:], in_=I[src]),
                      reads=prevld, writes=wr, dma=True, lane="d")
                prevld = (wr[0],)
            sd.op("sp", lambda e: e.dma_start(out=s1_sb[:], in_=I["s1_in"]),
                  reads=prevld, writes=("s1",), dma=True, lane="d")
            sd.op("sp", lambda e: e.dma_start(out=pmb[:], in_=I["pmask_bc"]),
                  reads=("s1",), writes=("pmb",), dma=True, lane="d")
            sd.op("sp", lambda e: e.dma_start(out=pmr[:], in_=I["pmask_row"]),
                  reads=("pmb",), writes=("pmr",), dma=True, lane="d")

            sd.op("dve", lambda e: e.memset(SZ[:], 0.0), writes=("SZ",))
            sd.op("dve", lambda e: e.memset(es_fm[:], 0.0), writes=("esf",))
            sd.fence("dve", ("esf",), ctx)
            sd.op("dve", lambda e: e.memset(w_fm[:], 0.0), writes=("wfm",))
            sd.fence("dve", ("wfm",), ctx)
            sd.op("dve", lambda e: e.memset(Zbf[:], 0.0), writes=("Zbf",))
            sd.fence("dve", ("Zbf",), ctx)
            sd.op("dve", lambda e: e.memset(csb[:], 0.0), writes=("csb",))
            for r_ in range(R):
                b = r_ % 2
                sd.op("sp", lambda e, r_=r_: e.dma_start(out=stg[0][:], in_=I["st_all"][r_ * 128:(r_ + 1) * 128, :]),
                      reads=("stg0", "pmr"), writes=("stg0",), dma=True, lane="d")
                sd.op("dve", lambda e, r_=r_: e.scalar_tensor_tensor(
                    out=SZ[:], in0=stg[0][:], scalar=pmb[:, r_:r_ + 1], in1=SZ[:], op0=ALU.mult, op1=ALU.add),
                    reads=("stg0", "pmb", "SZ"), writes=("SZ",))
                sd.op("sp", lambda e, r_=r_: e.dma_start(out=stg[0][0:1, 0:H * M], in_=I["c_all"][0:1, r_ * H * M:(r_ + 1) * H * M]),
                      reads=("stg0", "pmr"), writes=("stg0",), dma=True, lane="d")
                sd.op("dve", lambda e, r_=r_: e.scalar_tensor_tensor(
                    out=csb[:], in0=stg[0][0:1, 0:H * M], scalar=pmr[0:1, r_:r_ + 1], in1=csb[:],
                    op0=ALU.mult, op1=ALU.add),
                    reads=("stg0", "pmr", "csb"), writes=("csb",))
            sd.op("dve", lambda e: e.tensor_scalar(out=Sbf[:], in0=SZ[:, 0:1024], scalar1=SCALE, scalar2=None, op0=ALU.mult),
                  reads=("SZ",), writes=("Sbf",))
            sd.fence("dve", ("Sbf",), ctx)
            sd.op("dve", lambda e: e.tensor_copy(
                out=Zbf[0:64, :].rearrange("p (h c) -> p h c", c=128)[:, 0::2, :],
                in_=SZ[0:64, 1024:2048].rearrange("p (h c) -> p h c", c=128)), reads=("SZ",), writes=("Zbf",))
            sd.fence("dve", ("Zbf",), ctx)
            sd.op("dve", lambda e: e.tensor_copy(
                out=Zbf[0:64, :].rearrange("p (h c) -> p h c", c=128)[:, 1::2, :],
                in_=SZ[64:128, 1024:2048].rearrange("p (h c) -> p h c", c=128)), reads=("SZ",), writes=("Zbf",))
            sd.fence("dve", ("Zbf",), ctx)
            sd.op("dve", lambda e: e.tensor_copy(out=cbf[:], in_=csb[:]), reads=("csb",), writes=("cbf",))
            sd.fence("dve", ("cbf",), ctx)

            with ExitStack() as pse:
                P1 = pse.enter_context(nc.psum_tensor("P1", [128, 2048], F32))
                P2 = pse.enter_context(nc.psum_tensor("P2", [128, 1024], F32))
                PB = pse.enter_context(nc.psum_tensor("PB", [128, 1024], BF16))
                for ci in range(nch):
                    emit_es(ctx, ci, s1_sb, es_tm, "es", es_sc, "P2", P2[:, 0:512], scname="enz")
                    emit_ktm(ctx, ci, k_sb, k_tm, PB, "PB")
                    if stage >= 2:
                        for h in range(H):
                            sd.op("pe", lambda e, h=h, ci=ci: e.matmul(
                                P1[:, h * 128:(h + 1) * 128],
                                k_sb[:, h * NT + ci * C: h * NT + (ci + 1) * C],
                                q_sb[:, h * NT + ci * C: h * NT + (ci + 1) * C], start=True, stop=True),
                                reads=(f"k{h}", f"q{h}"), writes=("P1",))
                        sd.op("dve", lambda e: e.tensor_mul(
                            out=ab_sb[:, :].rearrange("p (h c) -> p h c", c=128),
                            in0=P1[:, :].rearrange("p (h c) -> p h c", c=128),
                            in1=t["maskS"][:, :].unsqueeze(1).broadcast_to([128, H, 128])),
                            reads=("P1", "c_maskS"), writes=("ab",))
                        sd.fence("dve", ("ab",), ctx)
                    if stage >= 3:
                        for h in range(H):
                            sd.op("pe", lambda e, h=h, ci=ci: e.matmul(
                                P1[:, h * M:(h + 1) * M],
                                q_sb[:, h * NT + ci * C: h * NT + (ci + 1) * C],
                                Sbf[:, h * M:(h + 1) * M], start=True, stop=False),
                                reads=(f"q{h}", "Sbf", "ab"), writes=("P1",))
                            sd.op("pe", lambda e, h=h: e.matmul(
                                P1[:, h * M:(h + 1) * M], ab_sb[:, h * 128:(h + 1) * 128],
                                es_tm[:, h * M:(h + 1) * M], start=False, stop=True),
                                reads=("ab", "es"), writes=("P1",))
                        for hf in range(2):
                            sd.op("pe", lambda e, hf=hf: e.matmul(
                                P1[:, 1024 + hf * 512:1024 + (hf + 1) * 512], t["triu"][:, :],
                                es_tm[:, hf * 512:(hf + 1) * 512], start=True, stop=False),
                                reads=("es", "c_triu"), writes=("P1",))
                            sd.op("pe", lambda e, hf=hf: e.matmul(
                                P1[:, 1024 + hf * 512:1024 + (hf + 1) * 512], t["onesK1"][:, :],
                                cbf[0:1, hf * 512:(hf + 1) * 512], start=False, stop=True),
                                reads=("cbf", "c_onesK1"), writes=("P1",))
                    if stage >= 4:
                        sd.op("dve", lambda e: e.reciprocal(out=enz[:], in_=P1[:, 1024:2048]),
                              reads=("P1",), writes=("enz",))
                        sd.op("dve", lambda e: e.tensor_mul(out=okn[:], in0=P1[:, 0:1024], in1=enz[:]),
                              reads=("P1", "enz"), writes=("bigsc",))
                        sd.op("dve", lambda e: e.reduce_max(out=smx[:, 0:H], in_=okn[:].rearrange("p (h m) -> p h m", m=M), axis=AX.X),
                              reads=("bigsc",), writes=("rmax",))
                        sd.op("dve", lambda e: e.tensor_sub(
                            out=okn[:].rearrange("p (h m) -> p h m", m=M),
                            in0=okn[:].rearrange("p (h m) -> p h m", m=M),
                            in1=smx[:, 0:H].unsqueeze(2).broadcast_to([128, H, M])),
                            reads=("bigsc", "rmax"), writes=("bigsc",))
                        sd.op("act", lambda e: e.activation(out=e_sb[:], in_=okn[:], func=AF.Exp),
                              reads=("bigsc",), writes=("bigsc",))
                        sd.op("dve", lambda e: e.reduce_sum(out=smx[:, H:2 * H], in_=e_sb[:].rearrange("p (h m) -> p h m", m=M), axis=AX.X),
                              reads=("bigsc",), writes=("ssum",))
                        sd.op("dve", lambda e: e.reciprocal(out=smx[:, 2 * H:3 * H], in_=smx[:, H:2 * H]),
                              reads=("ssum",), writes=("rs",))
                        sd.op("dve", lambda e: e.tensor_mul(
                            out=okn[:].rearrange("p (h m) -> p h m", m=M),
                            in0=e_sb[:].rearrange("p (h m) -> p h m", m=M),
                            in1=smx[:, 2 * H:3 * H].unsqueeze(2).broadcast_to([128, H, M])),
                            reads=("bigsc", "rs"), writes=("bigsc",))
                        sd.op("dve", lambda e: e.tensor_mul(out=w_tm[:], in0=okn[:], in1=enz[:]),
                              reads=("bigsc", "enz"), writes=("wtm",))
                        sd.fence("dve", ("wtm",), ctx)
                    if stage >= 5:
                        for p_ in range(8):
                            sd.op("pe", lambda e, p_=p_: e.transpose(
                                PB[:, p_ * 128:(p_ + 1) * 128], es_tm[:, p_ * 128:(p_ + 1) * 128], t["ident"][:, :]),
                                reads=("es", "c_ident", "ktm"), writes=("PB",))
                        sd.op("dve", lambda e: e.tensor_copy(
                            out=es_fm[0:64, :].rearrange("p (h c) -> p h c", c=128)[:, 0::2, :],
                            in_=PB[0:64, :].rearrange("p (h c) -> p h c", c=128)), reads=("PB",), writes=("esf",))
                        sd.fence("dve", ("esf",), ctx)
                        sd.op("dve", lambda e: e.tensor_copy(
                            out=es_fm[0:64, :].rearrange("p (h c) -> p h c", c=128)[:, 1::2, :],
                            in_=PB[64:128, :].rearrange("p (h c) -> p h c", c=128)), reads=("PB",), writes=("esf",))
                        sd.fence("dve", ("esf",), ctx)
                        for p_ in range(8):
                            sd.op("pe", lambda e, p_=p_: e.transpose(
                                PB[:, p_ * 128:(p_ + 1) * 128], w_tm[:, p_ * 128:(p_ + 1) * 128], t["ident"][:, :]),
                                reads=("wtm", "c_ident", "esf"), writes=("PB",))
                        sd.op("dve", lambda e: e.tensor_copy(
                            out=w_fm[0:64, :].rearrange("p (h c) -> p h c", c=128)[:, 0::2, :],
                            in_=PB[0:64, :].rearrange("p (h c) -> p h c", c=128)), reads=("PB",), writes=("wfm",))
                        sd.fence("dve", ("wfm",), ctx)
                        sd.op("dve", lambda e: e.tensor_copy(
                            out=w_fm[0:64, :].rearrange("p (h c) -> p h c", c=128)[:, 1::2, :],
                            in_=PB[64:128, :].rearrange("p (h c) -> p h c", c=128)), reads=("PB",), writes=("wfm",))
                        sd.fence("dve", ("wfm",), ctx)
                    if stage >= 6:
                        for h in range(H):
                            sd.op("pe", lambda e, h=h: e.matmul(
                                P1[:, h * 128:(h + 1) * 128],
                                es_fm[:, h * 128:(h + 1) * 128],
                                w_fm[:, h * 128:(h + 1) * 128], start=True, stop=True),
                                reads=("esf", "wfm", "bigsc", "enz"), writes=("P1",))
                        sd.op("dve", lambda e: e.tensor_mul(
                            out=ab_sb[:, :].rearrange("p (h c) -> p h c", c=128),
                            in0=P1[:, :].rearrange("p (h c) -> p h c", c=128),
                            in1=t["mask1"][:, :].unsqueeze(1).broadcast_to([128, H, 128])),
                            reads=("P1", "c_mask1"), writes=("ab",))
                        sd.fence("dve", ("ab",), ctx)
                    if stage >= 7:
                        for h in range(H):
                            sd.op("pe", lambda e, h=h: e.matmul(
                                P1[:, h * 128:(h + 1) * 128],
                                w_fm[:, h * 128:(h + 1) * 128],
                                Zbf[:, h * 128:(h + 1) * 128], start=True, stop=False),
                                reads=("wfm", "Zbf", "ab"), writes=("P1",))
                            sd.op("pe", lambda e, h=h, ci=ci: e.matmul(
                                P1[:, h * 128:(h + 1) * 128], ab_sb[:, h * 128:(h + 1) * 128],
                                v_tm[:, ci * D + h * 128: ci * D + (h + 1) * 128], start=False, stop=True),
                                reads=("ab", f"v{h}"), writes=("P1",))
                    if stage >= 8:
                        sd.op("act", lambda e: e.activation(out=sqv[:], in_=P1[:, :], func=AF.Square),
                              reads=("P1", "bigsc"), writes=("bigsc",))
                        sd.op("dve", lambda e: e.reduce_sum(out=smx[:, 3 * H:4 * H], in_=sqv[:].rearrange("p (h c) -> p h c", c=128), axis=AX.X),
                              reads=("bigsc",), writes=("ms",))
                        sd.op("act", lambda e: e.activation(out=smx[:, 3 * H:4 * H], in_=smx[:, 3 * H:4 * H], func=AF.Ln,
                                                            scale=1.0 / DV, bias=t["eps"][:, 0:1]),
                              reads=("ms", "c_eps"), writes=("ms",))
                        sd.op("act", lambda e: e.activation(out=smx[:, 3 * H:4 * H], in_=smx[:, 3 * H:4 * H], func=AF.Exp, scale=-0.5),
                              reads=("ms",), writes=("ms",))
                        sd.op("dve", lambda e: e.tensor_mul(
                            out=t1s[:].rearrange("p (h c) -> p h c", c=128),
                            in0=P1[:, :].rearrange("p (h c) -> p h c", c=128),
                            in1=smx[:, 3 * H:4 * H].unsqueeze(2).broadcast_to([128, H, DV])),
                            reads=("P1", "ms", "bigsc"), writes=("bigsc",))
                        sd.op("dve", lambda e, ci=ci: e.tensor_mul(out=og_tm[:], in0=t1s[:], in1=sg_tm[:, ci * D:(ci + 1) * D]),
                              reads=("bigsc", "sg"), writes=("ogtm",))
                        sd.fence("dve", ("ogtm",), ctx)
                        for rnd in range(2):
                            for hh in range(8):
                                h = rnd * 8 + hh
                                sd.op("pe", lambda e, h=h, hh=hh: e.transpose(
                                    PB[:, hh * 128:(hh + 1) * 128], og_tm[:, h * 128:(h + 1) * 128], t["ident"][:, :]),
                                    reads=("ogtm", "c_ident", "wfm"), writes=("PB",))
                            sd.op("dve", lambda e, rnd=rnd: e.tensor_copy(out=w_fm[:, rnd * 1024:(rnd + 1) * 1024], in_=PB[:, :]),
                                  reads=("PB",), writes=("wfm",))
                            sd.fence("dve", ("wfm",), ctx)
                            dst = ogd.rearrange("(hh p) (ct tt) -> hh p ct tt", p=128, tt=C)[rnd * 8:(rnd + 1) * 8, :, ci, :].transpose([1, 0, 2])
                            sd.op("pool", lambda e, dst=dst, rnd=rnd: e.dma_start(
                                out=dst, in_=w_fm[:, rnd * 1024:(rnd + 1) * 1024].rearrange("p (hh tt) -> p hh tt", tt=C)),
                                reads=("wfm",), writes=("ogd",), dma=True, lane="c")
                    if stage >= 9:
                        for h in range(H):
                            sd.op("pe", lambda e, h=h: e.matmul(
                                P2[:, h * M:(h + 1) * M], k_tm[:, h * 128:(h + 1) * 128],
                                es_tm[:, h * M:(h + 1) * M], start=True, stop=True),
                                reads=("ktm", "es", "P2"), writes=("P2",))
                        sd.op("dve", lambda e: e.tensor_add(out=SZ[:, 0:1024], in0=SZ[:, 0:1024], in1=P2[:, :]),
                              reads=("P2", "SZ"), writes=("SZ",))
                        sd.op("dve", lambda e: e.tensor_scalar(out=Sbf[:], in0=SZ[:, 0:1024], scalar1=SCALE, scalar2=None, op0=ALU.mult),
                              reads=("SZ", "P1"), writes=("Sbf",))
                        sd.fence("dve", ("Sbf",), ctx)
                        for h in range(H):
                            pb_ = 64 * (h % 2)
                            cb_ = (h // 2) * 128
                            sd.op("pe", lambda e, h=h, ci=ci, pb_=pb_, cb_=cb_: e.matmul(
                                P2[pb_:pb_ + 64, cb_:cb_ + 128], es_tm[:, h * M:(h + 1) * M],
                                v_tm[:, ci * D + h * 128: ci * D + (h + 1) * 128], start=True, stop=True),
                                reads=("es", f"v{h}", "SZ"), writes=("P2",))
                        sd.op("dve", lambda e: e.tensor_add(out=SZ[:, 1024:2048], in0=SZ[:, 1024:2048], in1=P2[:, :]),
                              reads=("P2", "SZ"), writes=("SZ",))
                        sd.op("dve", lambda e: e.tensor_copy(
                            out=Zbf[0:64, :].rearrange("p (h c) -> p h c", c=128)[:, 0::2, :],
                            in_=SZ[0:64, 1024:2048].rearrange("p (h c) -> p h c", c=128)),
                            reads=("SZ", "P1"), writes=("Zbf",))
                        sd.fence("dve", ("Zbf",), ctx)
                        sd.op("dve", lambda e: e.tensor_copy(
                            out=Zbf[0:64, :].rearrange("p (h c) -> p h c", c=128)[:, 1::2, :],
                            in_=SZ[64:128, 1024:2048].rearrange("p (h c) -> p h c", c=128)),
                            reads=("SZ", "P1"), writes=("Zbf",))
                        sd.fence("dve", ("Zbf",), ctx)
                        for hf in range(2):
                            sd.op("pe", lambda e, hf=hf: e.matmul(
                                P2[0:1, hf * 512:(hf + 1) * 512], t["ones"][:, 0:1],
                                es_tm[:, hf * 512:(hf + 1) * 512], start=True, stop=True),
                                reads=("es", "c_ones", "P2"), writes=("P2",))
                        sd.op("dve", lambda e: e.tensor_add(out=csb[:], in0=csb[:], in1=P2[0:1, 0:1024]),
                              reads=("P2", "csb"), writes=("csb",))
                        sd.op("dve", lambda e: e.tensor_copy(out=cbf[:], in_=csb[:]),
                              reads=("csb", "P1"), writes=("cbf",))
                        sd.fence("dve", ("cbf",), ctx)
                sd.barrier()
            sd.barrier()

        if wo:
          with ExitStack() as wos:
            SW = lambda n, sh, dt: wos.enter_context(nc.sbuf_tensor(n, list(sh), dt))
            og_sb = SW("og_sb", (128, ND * NT), BF16)
            xs = [SW(f"xs{i}", (128, NT), F32) for i in range(2)]
            x2s = [SW(f"x2s{i}", (128, NT), F32) for i in range(2)]
            for cchunk in range(ND):
                prevog = (f"og{cchunk - 1}",) if cchunk else ()
                sd.op("sp", lambda e, cchunk=cchunk: e.dma_start(
                    out=og_sb[:, cchunk * NT:(cchunk + 1) * NT], in_=ogd[cchunk * 128:(cchunk + 1) * 128, :]),
                    reads=("ogd",) + prevog, writes=(f"og{cchunk}",), dma=True, lane="c")
            with ExitStack() as pse:
                pp = [pse.enter_context(nc.psum_tensor(f"wop{i}", [128, NT], F32)) for i in range(4)]
                slices = [(fi * 128, fog * 512, 512) for fog in range(4) for fi in range(ND)]

                def consume(j, wb, reads):
                    fi = j % ND
                    for c4 in range(4):
                        for hf in range(2):
                            sd.op("pe", lambda e, fi=fi, wb=wb, c4=c4, hf=hf: e.matmul(
                                pp[c4][:, hf * 512:(hf + 1) * 512],
                                wb[:, c4 * 128:(c4 + 1) * 128],
                                og_sb[:, fi * NT + hf * 512: fi * NT + (hf + 1) * 512],
                                start=(fi == 0), stop=(fi == ND - 1)),
                                reads=reads + (f"og{fi}",), writes=(f"wop{c4}",))

                def post(j):
                    if j % ND != ND - 1:
                        return
                    fog = j // ND
                    for c4 in range(4):
                        fo = fog * 4 + c4
                        b = fo % 2
                        sd.op("pool", lambda e, fo=fo, b=b: e.dma_start(
                            out=xs[b][:], in_=I["xT"][fo * 128:(fo + 1) * 128, :]),
                            reads=(f"xs{b}",), writes=(f"xs{b}",), dma=True, lane="ab"[b])
                        sd.op("dve", lambda e, c4=c4, b=b: e.tensor_add(out=x2s[b][:], in0=pp[c4][:, :], in1=xs[b][:]),
                              reads=(f"wop{c4}", f"xs{b}"), writes=(f"x2s{b}",))
                        sd.fence("dve", (f"x2s{b}",), ctx)
                        sd.op("pool", lambda e, fo=fo, b=b: e.dma_start(
                            out=x2d[fo * 128:(fo + 1) * 128, :], in_=x2s[b][:]),
                            reads=(f"x2s{b}",), writes=("x2d", f"x2s{b}"), dma=True, lane="cd"[b])

                pipe_stream(ctx, "wo", I["WoT"], slices, consume, post)
            sd.barrier()

        with ExitStack() as fs:
            SF = lambda n, sh, dt: fs.enter_context(nc.sbuf_tensor(n, list(sh), dt))
            xln = SF("xln", (128, ND * NT), BF16)
            emit_ln(ctx, x2d, xln, "ln2")
            h_sb = SF("h_sb", (128, NI * NT), BF16)
            usc = SF("usc", (128, NT), F32)
            gscol = SF("gscol", (128, NI), F32)
            uscol = SF("uscol", (128, NI), F32)
            dscol = SF("dscol", (128, ND), F32)
            sd.op("sp", lambda e: e.dma_start(out=gscol[:], in_=I["gs_col"]), writes=("gscol",), dma=True, lane="d")
            sd.op("sp", lambda e: e.dma_start(out=uscol[:], in_=I["us_col"]),
                  reads=("gscol",), writes=("uscol",), dma=True, lane="d")
            sd.op("sp", lambda e: e.dma_start(out=dscol[:], in_=I["ds_col"]),
                  reads=("uscol",), writes=("dscol",), dma=True, lane="d")
            if os.environ.get("L2_DBG") == "1":
                xln_dbg = nc.dram_tensor("xln_dbg", [128, ND * NT], BF16, kind="ExternalOutput").ap()
                sd.op("pool", lambda e: e.dma_start(out=xln_dbg, in_=xln[:, :]),
                      reads=tuple(f"hs{fi}" for fi in range(ND)), writes=("xlndbg",), dma=True)
            sgt = SF("sgt", (128, 4 * NT), F32)
            with ExitStack() as pse:
                pg = [pse.enter_context(nc.psum_tensor(f"pg{i}", [128, NT], F32)) for i in range(4)]
                for grp in range(NI // 4):
                    gslices = [(fi * 128, grp * 512, 512) for fi in range(ND)]

                    def gconsume(j, wb, reads, grp=grp):
                        fi = j
                        for ic in range(4):
                            for hf in range(2):
                                sd.op("pe", lambda e, fi=fi, wb=wb, ic=ic, hf=hf: e.matmul(
                                    pg[ic][:, hf * 512:(hf + 1) * 512],
                                    wb[:, ic * 128:(ic + 1) * 128],
                                    xln[:, fi * NT + hf * 512: fi * NT + (hf + 1) * 512],
                                    start=(fi == 0), stop=(fi == ND - 1)),
                                    reads=reads + (f"hs{fi}",), writes=(f"pg{ic}",))

                    def gpost(j, grp=grp):
                        if j != ND - 1:
                            return
                        for ic in range(4):
                            ich = grp * 4 + ic
                            sd.op("act", lambda e, ic=ic, ich=ich: e.activation(
                                out=sgt[:, ic * NT:(ic + 1) * NT], in_=pg[ic][:, :], func=AF.Silu,
                                scale=gscol[:, ich:ich + 1]),
                                reads=(f"pg{ic}", "gscol"), writes=("sgt",))
                        sd.fence("act", ("sgt",), ctx)

                    pipe_stream(ctx, f"gw{grp}", I["gate_wT"], gslices, gconsume, gpost)

                    def uconsume(j, wb, reads, grp=grp):
                        fi = j
                        for ic in range(4):
                            for hf in range(2):
                                sd.op("pe", lambda e, fi=fi, wb=wb, ic=ic, hf=hf: e.matmul(
                                    pg[ic][:, hf * 512:(hf + 1) * 512],
                                    wb[:, ic * 128:(ic + 1) * 128],
                                    xln[:, fi * NT + hf * 512: fi * NT + (hf + 1) * 512],
                                    start=(fi == 0), stop=(fi == ND - 1)),
                                    reads=reads + (f"hs{fi}", "sgt"), writes=(f"pg{ic}",))

                    def upost(j, grp=grp):
                        if j != ND - 1:
                            return
                        for ic in range(4):
                            ich = grp * 4 + ic
                            sd.op("act", lambda e, ic=ic, ich=ich: e.activation(
                                out=usc[:], in_=pg[ic][:, :], func=AF.Copy, scale=uscol[:, ich:ich + 1]),
                                reads=(f"pg{ic}", "uscol"), writes=("usc",))
                            sd.fence("act", ("usc",), ctx)
                            sd.op("dve", lambda e, ic=ic, ich=ich: e.tensor_mul(
                                out=h_sb[:, ich * NT:(ich + 1) * NT], in0=sgt[:, ic * NT:(ic + 1) * NT], in1=usc[:]),
                                reads=("sgt", "usc"), writes=(f"h{ich}",))
                            sd.fence("dve", (f"h{ich}",), ctx)

                    pipe_stream(ctx, f"uw{grp}", I["up_wT"], gslices, uconsume, upost)
                sd.barrier()
            if os.environ.get("L2_DBG") == "1":
                h_dbg = nc.dram_tensor("h_dbg", [128, NI * NT], BF16, kind="ExternalOutput").ap()
                sd.op("pool", lambda e: e.dma_start(out=h_dbg, in_=h_sb[:, :]),
                      reads=tuple(f"h{i}" for i in range(NI)), writes=("hdbg",), dma=True)
            x2b = [SF(f"x2b{i}", (128, NT), F32) for i in range(2)]
            yst = [SF(f"yst{i}", (128, NT), F32) for i in range(2)]
            with ExitStack() as pse:
                pd = [pse.enter_context(nc.psum_tensor(f"pd{i}", [128, NT], F32)) for i in range(4)]
                for fog in range(4):
                    dslices = [(ich * 128, fog * 512, 512) for ich in range(NI)]

                    def dconsume(j, wb, reads, fog=fog):
                        ich = j
                        for c4 in range(4):
                            for hf in range(2):
                                sd.op("pe", lambda e, ich=ich, wb=wb, c4=c4, hf=hf: e.matmul(
                                    pd[c4][:, hf * 512:(hf + 1) * 512],
                                    wb[:, c4 * 128:(c4 + 1) * 128],
                                    h_sb[:, ich * NT + hf * 512: ich * NT + (hf + 1) * 512],
                                    start=(ich == 0), stop=(ich == NI - 1)),
                                    reads=reads + (f"h{ich}",), writes=(f"pd{c4}",))

                    def dpost(j, fog=fog):
                        if j != NI - 1:
                            return
                        for c4 in range(4):
                            fo = fog * 4 + c4
                            b = fo % 2
                            sd.op("pool", lambda e, fo=fo, b=b: e.dma_start(
                                out=x2b[b][:], in_=x2d[fo * 128:(fo + 1) * 128, :]),
                                reads=(f"x2b{b}",), writes=(f"x2b{b}",), dma=True, lane="ab"[b])
                            sd.op("dve", lambda e, c4=c4, fo=fo, b=b: e.scalar_tensor_tensor(
                                out=yst[b][:], in0=pd[c4][:, :], scalar=dscol[:, fo:fo + 1], in1=x2b[b][:],
                                op0=ALU.mult, op1=ALU.add),
                                reads=(f"pd{c4}", "dscol", f"x2b{b}"), writes=(f"yst{b}",))
                            sd.fence("dve", (f"yst{b}",), ctx)
                            sd.op("pool", lambda e, fo=fo, b=b: e.dma_start(
                                out=y[fo * 128:(fo + 1) * 128, :], in_=yst[b][:]),
                                reads=(f"yst{b}",), writes=(f"yst{b}", "yout"), dma=True, lane="cd"[b])

                    pipe_stream(ctx, f"dw{fog}", I["down_wT"], dslices, dconsume, dpost)
                sd.barrier()
            sd.barrier()
        sd.barrier()
        block = es.enter_context(nc.Block())
        sd.emit(block, sems)
    return nc



_cache = {}


def _programs():
    if "l1" not in _cache:
        _cache["l1"] = build_launch1()
        _cache["l2"] = build_launch2()
    return _cache["l1"], _cache["l2"]


def _common_maps(x2d_np):
    pos = np.arange(T, dtype=np.float32)
    inv = (1.0 / 10000.0 ** (np.arange(0, DK, 2, dtype=np.float32) / DK))
    ang = pos[:, None] * inv[None, :]
    cosT = np.cos(ang).T.astype(NB)
    sinT = np.sin(ang).T.astype(NB)
    triu = np.triu(np.ones((128, 128), np.float32))
    consts = {
        "triu_bf": triu.astype(NB),
        "maskS": (triu * SCALE).astype(np.float32),
        "mask1": triu.astype(np.float32),
        "ident": np.eye(128, dtype=NB),
        "ones_bf": np.ones((128, 1), NB),
        "onesK1": np.ones((1, 128), NB),
    }
    xT = np.ascontiguousarray(x2d_np.T)
    maps = []
    for r_ in range(R):
        m = dict(consts)
        m["xT"] = np.ascontiguousarray(xT[:, r_ * NT:(r_ + 1) * NT])
        m["cosr"] = np.ascontiguousarray(cosT[:, r_ * NT:(r_ + 1) * NT])
        m["sinr"] = np.ascontiguousarray(sinT[:, r_ * NT:(r_ + 1) * NT])
        maps.append(m)
    return maps


def kernel(L0_attn_hidden_states, L0_attn_Wq, L0_attn_Wk, L0_attn_Wv, L0_attn_Wg,
           L0_attn_Wo, L0_attn_Ws1, L0_attn_Ws2, L0_attn_bs2, L0_attn_g_norm_weight,
           L0_ffn_gate_w, L0_ffn_up_w, L0_ffn_down_w, L0_ffn_gate_s, L0_ffn_up_s,
           L0_ffn_down_s):
    x = np.asarray(L0_attn_hidden_states, np.float32).reshape(T, D)
    l1, l2 = _programs()
    common = _common_maps(x)

    tr = lambda w: np.ascontiguousarray(np.asarray(w, np.float32).T)
    WqT, WkT, WvT, WgT, WoT = map(tr, (L0_attn_Wq, L0_attn_Wk, L0_attn_Wv, L0_attn_Wg, L0_attn_Wo))
    Ws1T = tr(L0_attn_Ws1)
    Ws2T = tr(L0_attn_Ws2)
    gateT, upT, downT = map(tr, (L0_ffn_gate_w, L0_ffn_up_w, L0_ffn_down_w))
    bs2r = np.asarray(L0_attn_bs2, np.float32).reshape(1, H * M)
    gnw = np.asarray(L0_attn_g_norm_weight, np.float32).reshape(128, 1)
    gs_col = np.ascontiguousarray(np.asarray(L0_ffn_gate_s, np.float32).reshape(NI, 128).T)
    us_col = np.ascontiguousarray(np.asarray(L0_ffn_up_s, np.float32).reshape(NI, 128).T)
    ds_col = np.ascontiguousarray(np.asarray(L0_ffn_down_s, np.float32).reshape(ND, 128).T)

    maps1 = []
    for r_ in range(R):
        m = dict(common[r_])
        m.update({"WqT": WqT, "WkT": WkT, "WvT": WvT, "WgT": WgT,
                  "Ws1T": Ws1T, "Ws2T": Ws2T, "bs2r": bs2r, "gnw": gnw})
        maps1.append(m)
    res1 = run_bass_kernel_spmd(l1, maps1, core_ids=list(range(R)))

    st_all = np.concatenate([res1.results[r_]["st_out"] for r_ in range(R)], axis=0)
    c_all = np.concatenate([res1.results[r_]["c_out"] for r_ in range(R)], axis=1)
    maps2 = []
    for r_ in range(R):
        m = dict(common[r_])
        pm = np.zeros((R,), np.float32)
        pm[:r_] = 1.0
        m.update({
            "WoT": WoT, "gate_wT": gateT, "up_wT": upT, "down_wT": downT,
            "Ws1T": Ws1T, "Ws2T": Ws2T, "bs2r": bs2r,
            "gs_col": gs_col, "us_col": us_col, "ds_col": ds_col,
            "q_in": res1.results[r_]["q_sp"], "k_in": res1.results[r_]["k_sp"],
            "v_in": res1.results[r_]["v_sp"], "sg_in": res1.results[r_]["sg_sp"],
            "s1_in": res1.results[r_]["s1_sp"],
            "st_all": st_all, "c_all": c_all,
            "pmask_bc": np.ascontiguousarray(np.broadcast_to(pm, (128, R))),
            "pmask_row": pm.reshape(1, R),
        })
        maps2.append(m)
    res2 = run_bass_kernel_spmd(l2, maps2, core_ids=list(range(R)))

    out = np.empty((T, D), np.float32)
    for r_ in range(R):
        out[r_ * NT:(r_ + 1) * NT, :] = res2.results[r_]["y"].T
    return out.reshape(1, T, D)
